# version 12
# baseline (speedup 1.0000x reference)
import numpy as np
from contextlib import ExitStack
import concourse.bass as bass
import concourse.mybir as mybir

F32 = mybir.dt.float32
BF16 = mybir.dt.bfloat16
I8 = mybir.dt.int8
AX = mybir.AxisListType.X
ALU = mybir.AluOpType
ACTF = mybir.ActivationFunctionType

E = 2048
D = 128
SCALE = float(1.0 / np.sqrt(np.float32(D)))


def _base_ap(t):
    if isinstance(t, bass.AP):
        return t
    return t[:]


def APv(t, off, dims):
    a = _base_ap(t)
    return bass.AP(a.tensor, a.offset + off, [list(a.ap[0])] + dims)


def APd(t, off, dims):
    a = _base_ap(t)
    return bass.AP(a.tensor, a.offset + off, dims)


class Cfg:
    def __init__(self, nt=8, mt=16, fb=4, oc=64, ffn_tc=256, debug=False):
        self.debug = debug
        self.NT = nt
        self.MT = mt
        self.FB = fb
        self.OC = oc
        self.TC = ffn_tc
        self.EC = 16
        self.heads = 4 * fb
        self.NTOK = nt * 128
        self.MTOK = mt * 128
        self.INTER = oc * 128

    def Jtile(self, qt):
        return min(4 * (qt // 2) + (qt % 2) + 2, self.MT - 1)


class Sem:
    __slots__ = ("h", "n", "name")
    def __init__(self, h, name):
        self.h = h; self.n = 0; self.name = name
    @property
    def tok(self):
        return (self, self.n)


def flat(*toks):
    out = []
    for t in toks:
        if t is None:
            continue
        if isinstance(t, list):
            out.extend(x for x in flat(*t))
        else:
            out.append(t)
    return out


class Plan:
    def __init__(self):
        self.ops = []

    def op(self, engine, fn, waits=(), inc=None, tag="", is_dma=False):
        ws = flat(*waits)
        best = {}
        for (s, v) in ws:
            if v > 0 and best.get(s, 0) < v:
                best[s] = v
        ws = list(best.items())
        rec_inc = None
        if inc is not None:
            inc[0].n += inc[1]
            rec_inc = (inc[0], inc[1], inc[0].n)
        self.ops.append((engine, fn, ws, rec_inc, tag, is_dma))
        return None if inc is None else (inc[0], inc[0].n)

    def dma(self, engine, dst, src, waits=(), sem=None, tag=""):
        return self.op(engine, lambda e, d=dst, s=src: e.dma_start(d, s),
                       waits=waits, inc=(sem, 16), tag=tag, is_dma=True)

    def barrier(self, engine, waits, tag="bar"):
        ws = flat(*waits)
        best = {}
        for (s, v) in ws:
            if v > 0 and best.get(s, 0) < v:
                best[s] = v
        self.ops.append((engine, None, list(best.items()), None, tag, False))

    def emit(self, block, chain_sems=None):
        per = {}
        for rec in self.ops:
            per.setdefault(rec[0], []).append(rec)
        for name in ("sync", "tensor", "vector", "scalar", "gpsimd"):
            if name not in per:
                continue
            records = per[name]
            chain = chain_sems.get(name) if chain_sems else None
            def body(eng, records=records, chain=chain):
                seen = {}
                cnt = 0
                last_inc = None
                for (_, fn, waits, inc, tag, is_dma) in records:
                    for (s, v) in waits:
                        if seen.get(s, 0) < v:
                            eng.wait_ge(s.h, v)
                            seen[s] = v
                    if fn is None:
                        continue
                    if chain is not None:
                        if cnt > 0:
                            eng.wait_ge(chain, cnt)
                        if last_inc is not None:
                            eng.wait_ge(*last_inc)
                    ins = fn(eng)
                    if ins is not None:
                        if inc is not None:
                            ins.then_inc(inc[0].h, inc[1])
                            if chain is not None and not is_dma:
                                last_inc = (inc[0].h, inc[2])
                        elif chain is not None and not is_dma:
                            ins.then_inc(chain, 1)
                            cnt += 1
                    else:
                        assert inc is None, f"op {tag} must return an instruction"
            getattr(block, name)(body)


class Builder:
    def __init__(self, cfg: Cfg, simsafe=False):
        self.cfg = cfg
        self.simsafe = simsafe
        self.nc = bass.Bass("TRN2", target_bir_lowering=False,
                            detect_race_conditions=simsafe)
        self.p = Plan()
        self.stack = ExitStack()
        self._semid = 0

    def sem(self, name):
        if not hasattr(self, "_sems"):
            self._sems = {}
        if name not in self._sems:
            self._semid += 1
            h = self.stack.enter_context(self.nc.semaphore(f"s{self._semid}_{name}"))
            self._sems[name] = Sem(h, name)
        return self._sems[name]

    def build(self):
        cfg, nc, p = self.cfg, self.nc, self.p
        NTOK, MTOK, INTER = cfg.NTOK, cfg.MTOK, cfg.INTER
        HD, EC, OC = cfg.heads, cfg.EC, cfg.OC

        d = {}
        d["x_seq"] = nc.dram_tensor("x_seq", [MTOK, E], F32, kind="ExternalInput").ap()
        d["x_own"] = nc.dram_tensor("x_own", [NTOK, E], F32, kind="ExternalInput").ap()
        for w in ("wq", "wk", "wv", "wo"):
            d[w] = nc.dram_tensor(w, [E, E], F32, kind="ExternalInput").ap()
        d["wg8"] = nc.dram_tensor("wg8", [E, INTER], I8, kind="ExternalInput").ap()
        d["wu8"] = nc.dram_tensor("wu8", [E, INTER], I8, kind="ExternalInput").ap()
        d["wd8"] = nc.dram_tensor("wd8", [INTER, E], I8, kind="ExternalInput").ap()
        d["sg"] = nc.dram_tensor("sg", [128, OC], F32, kind="ExternalInput").ap()
        d["su"] = nc.dram_tensor("su", [128, OC], F32, kind="ExternalInput").ap()
        d["sd"] = nc.dram_tensor("sd", [128, EC], F32, kind="ExternalInput").ap()
        d["ident"] = nc.dram_tensor("ident", [128, 128], BF16, kind="ExternalInput").ap()
        d["identf"] = nc.dram_tensor("identf", [128, 128], F32, kind="ExternalInput").ap()
        d["epsb"] = nc.dram_tensor("epsb", [128, 1], F32, kind="ExternalInput").ap()
        d["masks"] = nc.dram_tensor("masks", [3, 128, 128], BF16, kind="ExternalInput").ap()
        d["cos_own"] = nc.dram_tensor("cos_own", [NTOK, 512], F32, kind="ExternalInput").ap()
        d["sin_own"] = nc.dram_tensor("sin_own", [NTOK, 512], F32, kind="ExternalInput").ap()
        d["cos_all"] = nc.dram_tensor("cos_all", [MTOK, 512], F32, kind="ExternalInput").ap()
        d["sin_all"] = nc.dram_tensor("sin_all", [MTOK, 512], F32, kind="ExternalInput").ap()
        d["out"] = nc.dram_tensor("out", [NTOK, E], F32, kind="ExternalOutput").ap()

        kd = "ExternalOutput" if cfg.debug else "Internal"
        q_hi = nc.dram_tensor("q_scr_hi", [HD, 128, NTOK], BF16, kind=kd).ap()
        q_lo = nc.dram_tensor("q_scr_lo", [HD, 128, NTOK], BF16, kind=kd).ap()
        ao_scr = nc.dram_tensor("ao_scr", [HD, 128, NTOK], F32, kind=kd).ap()
        o2_scr = nc.dram_tensor("o2_scr", [NTOK, E], F32, kind=kd).ap()
        x1_scr = nc.dram_tensor("x1_scr", [NTOK, E], F32, kind=kd).ap()
        lnf_hi = nc.dram_tensor("lnf_hi", [EC, 128, NTOK], BF16, kind=kd).ap()
        lnf_lo = nc.dram_tensor("lnf_lo", [EC, 128, NTOK], BF16, kind=kd).ap()
        wg_scr = nc.dram_tensor("wg_scr", [OC, EC, 128, 128], BF16, kind=kd).ap()
        wu_scr = nc.dram_tensor("wu_scr", [OC, EC, 128, 128], BF16, kind=kd).ap()
        wd_scr = nc.dram_tensor("wd_scr", [EC, OC, 128, 128], BF16, kind=kd).ap()
        self.d = d

        sb = self.stack.enter_context
        self.ident = sb(nc.sbuf_tensor("c_ident", [128, 128], BF16))
        self.identf = sb(nc.sbuf_tensor("c_identf", [128, 128], F32))
        self.epsb = sb(nc.sbuf_tensor("c_eps", [128, 1], F32))
        self.masks = sb(nc.sbuf_tensor("c_masks", [128, 3 * 128], BF16))
        self.sg_sb = sb(nc.sbuf_tensor("c_sg", [128, OC], F32))
        self.su_sb = sb(nc.sbuf_tensor("c_su", [128, OC], F32))
        self.sd_sb = sb(nc.sbuf_tensor("c_sd", [128, EC], F32))
        ld0 = self.sem("ld0")
        p.dma("sync", self.ident[:], d["ident"][:], sem=ld0)
        p.dma("sync", self.identf[:], d["identf"][:], sem=ld0)
        p.dma("sync", self.epsb[:], d["epsb"][:], sem=ld0)
        p.dma("sync", self.masks[:],
              APd(d["masks"], 0, [[128, 128], [128 * 128, 3], [1, 128]]), sem=ld0)
        p.dma("sync", self.sg_sb[:], d["sg"][:], sem=ld0)
        p.dma("sync", self.su_sb[:], d["su"][:], sem=ld0)
        p.dma("sync", self.sd_sb[:], d["sd"][:], sem=ld0)
        self.consts_ready = ld0.tok
        self.out_sem = self.sem("outd")

        KSEG = 4 * 128 * NTOK
        VSEG = 4 * (cfg.NT) * 128 * 256
        FBSEG = 2 * KSEG + VSEG
        kv_in = [nc.dram_tensor(f"kvin{f}", [FBSEG], BF16).ap()
                 for f in range(cfg.FB)]
        kv_out = [nc.dram_tensor(f"kvout{f}", [2, FBSEG], BF16).ap()
                  for f in range(cfg.FB)]
        self.kv_geom = (KSEG, VSEG, FBSEG)
        lno_hi = nc.dram_tensor("lno_hi", [EC, 128, NTOK], BF16).ap()
        lno_lo = nc.dram_tensor("lno_lo", [EC, 128, NTOK], BF16).ap()
        with ExitStack() as ph:
            ln = self.ln_phase(ph, d["x_own"], cfg.NT, "own", self.consts_ready)
            s_lno = self.sem("lno")
            for half, (slab, dst) in enumerate([(ln[0], lno_hi), (ln[1], lno_lo)]):
                for ec in range(EC):
                    p.dma("gpsimd", APd(dst, ec * 128 * NTOK, [[NTOK, 128], [1, NTOK]]),
                          slab[:, ec * NTOK:(ec + 1) * NTOK],
                          waits=[ln[3]], sem=s_lno, tag=f"lno{half}_{ec}")
            lno_done = s_lno.tok
            self.q_done = self.qkv_phase(ph, ln, d["wq"], cfg.NT, "own",
                                         (q_hi, q_lo), "q")
        fc_stack = ExitStack()
        fc_chunks = self.ffn_weight_cache_mk(fc_stack, d, wg_scr, wu_scr, wd_scr,
                                             flat(self.q_done, lno_done))
        self.attn_merged(d, (lno_hi, lno_lo, NTOK, flat(lno_done, self.q_done)),
                         kv_in, kv_out, q_hi, q_lo, ao_scr, fc_chunks)
        for fn in fc_chunks:
            fn()
        fc_chunks.clear()
        self.fc_finalize()
        fc_stack.close()
        self.oproj_phase(d, ao_scr, o2_scr, x1_scr, lnf_hi, lnf_lo)
        self.ffn_phase(d, lnf_hi, lnf_lo, wg_scr, wu_scr, wd_scr, x1_scr)

        p.barrier("sync", [self.out_sem.tok], tag="final")
        chain_sems = None
        if self.simsafe:
            chain_sems = {}
            for name in ("sync", "tensor", "vector", "scalar", "gpsimd"):
                chain_sems[name] = self.stack.enter_context(
                    nc.semaphore(f"chain_{name}"))
        with nc.Block() as block:
            p.emit(block, chain_sems)
        self.stack.close()
        return nc


    def ln_stats(self, bns, bno, src_fn, width, waits, s_v, tag):
        p = self.p
        nch = (width + 511) // 512
        st = None
        for c in range(nch):
            st = p.op("vector", lambda e, c=c: e.bn_stats(bns[:, c * 6:(c + 1) * 6], src_fn(c)),
                      waits=waits if c == 0 else (),
                      inc=(s_v, 1) if c == nch - 1 else None, tag=f"{tag}bs{c}")
        return p.op("vector", lambda e, n=nch: e.bn_aggr(bno[:], bns[:, :6 * n]),
                    waits=[st], inc=(s_v, 1), tag=f"{tag}ba")

    def ln_phase(self, ph, x_dram, ntiles, name, start_tok, to_hbm=None,
                 as_chunks=False):
        nc, p, cfg = self.nc, self.p, self.cfg
        ntok = ntiles * 128
        EC = cfg.EC
        if to_hbm is None:
            slab_hi = ph.enter_context(nc.sbuf_tensor(f"ln_{name}_hi", [128, EC * ntok], BF16))
            slab_lo = ph.enter_context(nc.sbuf_tensor(f"ln_{name}_lo", [128, EC * ntok], BF16))
        sc = ExitStack() if not as_chunks else None
        with (sc if sc is not None else ExitStack()) as _sc:
            s = (_sc.enter_context if not as_chunks else ph.enter_context)
            if to_hbm is not None:
                stg = [s(nc.sbuf_tensor(f"lnstg{name}{i}", [128, 512], BF16))
                       for i in range(2)]
                s_sd = [self.sem(f"ln_sd{i}") for i in range(2)]
                stg_free = [start_tok, start_tok]
            xt = [s(nc.sbuf_tensor(f"lnx{name}{i}", [128, E], F32)) for i in range(2)]
            ln32 = s(nc.sbuf_tensor(f"ln32{name}", [128, E], F32))
            lhi = s(nc.sbuf_tensor(f"lnhi{name}", [128, E], BF16))
            llo = s(nc.sbuf_tensor(f"lnlo{name}", [128, E], BF16))
            bns = s(nc.sbuf_tensor(f"bns{name}", [128, 24], F32))
            bno = s(nc.sbuf_tensor(f"bno{name}", [128, 2], F32))
            std = s(nc.sbuf_tensor(f"std{name}", [128, 1], F32))
            rstd = s(nc.sbuf_tensor(f"rstd{name}", [128, 1], F32))
            trp = [s(nc.psum_tensor(f"lntr{name}{i}", [128, 1024], BF16)) for i in range(2)]
            s_ld = [self.sem(f"ln_ld{i}") for i in range(2)]
            s_v = self.sem("ln_v")
            s_s = self.sem("ln_s")
            s_t = self.sem("ln_t")
            s_e = self.sem("ln_e")
            xt_free = [start_tok, start_tok]
            bank_free = [start_tok, start_tok]

            def emit_tile(i):
                ldt = p.dma("sync", xt[i % 2][:], x_dram[i * 128:(i + 1) * 128, :],
                            waits=[xt_free[i % 2]], sem=s_ld[i % 2], tag=f"ln{name}x{i}")
                stats = self.ln_stats(bns, bno,
                                      lambda c, b=xt[i % 2]: b[:, c * 512:(c + 1) * 512],
                                      E, [ldt, start_tok], s_v, f"ln{name}{i}")
                sq = p.op("scalar", lambda e: e.activation(std[:], bno[:, 1:2], ACTF.Sqrt,
                                                           bias=self.epsb[:]),
                          waits=[stats, start_tok], inc=(s_s, 1), tag=f"ln{name}sq{i}")
                rct = p.op("vector", lambda e: e.reciprocal(rstd[:], std[:]), waits=[sq],
                           inc=(s_v, 1), tag=f"ln{name}rc{i}")
                apl = p.op("vector", lambda e, b=xt[i % 2]: e.tensor_scalar(
                        ln32[:], b[:], bno[:, 0:1], rstd[:], ALU.subtract, ALU.mult),
                     waits=[rct], inc=(s_v, 1), tag=f"ln{name}ap{i}")
                xt_free[i % 2] = apl
                p.op("vector", lambda e: e.tensor_copy(lhi[:], ln32[:]), tag=f"ln{name}h{i}")
                spl = p.op("vector", lambda e: e.tensor_sub(llo[:], ln32[:], lhi[:]),
                           inc=(s_v, 1), tag=f"ln{name}l{i}")
                gidx = 0
                if to_hbm is None:
                    dests = [(lhi, slab_hi), (llo, slab_lo)]
                else:
                    dests = [(lhi, to_hbm[0]), (llo, to_hbm[1])]
                for half, (src, dslab) in enumerate(dests):
                    for g in range(EC // 4):
                        bi = (i * 2 * (EC // 4) + gidx) % 2
                        bank = trp[bi]
                        bf = bank_free[bi]
                        gidx += 1
                        tt = None
                        for j in range(4):
                            ec = g * 4 + j
                            tt = p.op("tensor", lambda e, s2=src, b=bank, ec=ec, j=j:
                                      e.transpose(b[:, j * 128:(j + 1) * 128],
                                                  s2[:, ec * 128:(ec + 1) * 128],
                                                  self.ident[:]),
                                      waits=[spl, bf] if j == 0 else (),
                                      inc=(s_t, 1) if j == 3 else None,
                                      tag=f"ln{name}t{i}_{half}_{g}_{j}")
                        if to_hbm is None:
                            base = (g * 4) * ntok + i * 128
                            ev = p.op("scalar",
                                      lambda e, b=bank, ds=dslab, base=base, ntok=ntok:
                                      e.copy(APv(ds, base, [[ntok, 4], [1, 128]]),
                                             b[:, :512]),
                                      waits=[tt], inc=(s_e, 1),
                                      tag=f"ln{name}e{i}_{half}_{g}")
                            bank_free[bi] = ev
                        else:
                            sb2 = stg[bi]
                            ev = p.op("scalar",
                                      lambda e, b=bank, sb2=sb2:
                                      e.copy(sb2[:], b[:, :512]),
                                      waits=[tt, stg_free[bi]], inc=(s_e, 1),
                                      tag=f"ln{name}e{i}_{half}_{g}")
                            dmat = p.dma("gpsimd",
                                         APd(dslab, (g * 4) * 128 * ntok + i * 128,
                                             [[ntok, 128], [ntok * 128, 4], [1, 128]]),
                                         APv(sb2, 0, [[128, 4], [1, 128]]),
                                         waits=[ev], sem=s_sd[bi],
                                         tag=f"ln{name}sd{i}_{half}_{g}")
                            stg_free[bi] = dmat
                            bank_free[bi] = ev
            if as_chunks:
                chunks = [lambda i=i: emit_tile(i) for i in range(ntiles)]

                def done_fn():
                    return [(s_sd[0], s_sd[0].n), (s_sd[1], s_sd[1].n)]
                return (chunks, done_fn)
            for i in range(ntiles):
                emit_tile(i)
            if to_hbm is not None:
                ln_done = [(s_sd[0], s_sd[0].n), (s_sd[1], s_sd[1].n)]
                return (to_hbm[0], to_hbm[1], ntok, ln_done)
            ln_done = (s_e, s_e.n)
        return (slab_hi, slab_lo, ntok, ln_done)

    def wT_prep(self, o, w_dram, fb, tag):
        nc, p, cfg = self.nc, self.p, self.cfg
        EC = cfg.EC
        ev = None
        for sub in range(4):
            f0 = fb * 512 + sub * 128
            buf = o["wrow"][sub % 2]
            w = [o["wrow_free"][sub % 2]]
            if sub == 0:
                w.append(o.get("slab_free"))
            ldt = p.dma("sync", buf[:], w_dram[f0:f0 + 128, :], waits=w,
                        sem=o["s_wld"][sub % 2], tag=f"{tag}wld{fb}_{sub}")
            p.op("vector", lambda e, b=buf: e.tensor_copy(o["wrhi"][:], b[:]),
                 waits=[ldt, o.get("vec_start"), o.get("wsplit_free")],
                 tag=f"{tag}wh{fb}{sub}")
            spl = p.op("vector", lambda e, b=buf: e.tensor_sub(o["wrlo"][:], b[:], o["wrhi"][:]),
                       inc=(o["s_wv"], 1), tag=f"{tag}wl{fb}{sub}")
            o["wrow_free"][sub % 2] = spl
            if o.get("wrow_single"):
                o["wrow_free"][(sub + 1) % 2] = spl
            o["vec_start"] = None
            gidx = 0
            last_tt = [None, None]
            for half, (src, dslab) in enumerate([(o["wrhi"], o["wT"][0]),
                                                 (o["wrlo"], o["wT"][1])]):
                for g in range(EC // 4):
                    bi = gidx % 2
                    bank = o["trp"][bi]
                    bf = o["bank_free"][bi]
                    gidx += 1
                    tt = None
                    for j in range(4):
                        ec = g * 4 + j
                        tt = p.op("tensor", lambda e, s2=src, b=bank, ec=ec, j=j:
                                  e.transpose(b[:, j * 128:(j + 1) * 128],
                                              s2[:, ec * 128:(ec + 1) * 128],
                                              self.ident[:]),
                                  waits=[spl, bf] if j == 0 else (),
                                  inc=(o["s_wt"], 1) if (j == 3 or g * 4 + j == EC - 1)
                                      and j == 3 else None,
                                  tag=f"{tag}wt{fb}{sub}{half}{g}{j}")
                    base = (g * 4) * 512 + sub * 128
                    ev = p.op("scalar", lambda e, b=bank, ds=dslab, base=base:
                              e.copy(APv(ds, base, [[512, 4], [1, 128]]), b[:, :512]),
                              waits=[tt], inc=(o["s_we"], 1),
                              tag=f"{tag}we{fb}{sub}{half}{g}")
                    o["bank_free"][bi] = ev
                    last_tt[half] = tt
            o["wsplit_free"] = [last_tt[0], last_tt[1]]
        return ev

    def qkv_phase(self, ph, ln, w_dram, ntiles, rope, out, kind, prev=None,
                  interleave=None):
        nc, p, cfg = self.nc, self.p, self.cfg
        slab_hi, slab_lo, ntok, ln_done = ln
        ln_done = flat(ln_done, prev)
        EC, FB, MT = cfg.EC, cfg.FB, cfg.MT
        with ExitStack() as sc:
            s = sc.enter_context
            o = {}
            o["wrow"] = [s(nc.sbuf_tensor(f"{kind}wr{i}", [128, E], F32)) for i in range(2)]
            o["wrhi"] = s(nc.sbuf_tensor(f"{kind}wrh", [128, E], BF16))
            o["wrlo"] = s(nc.sbuf_tensor(f"{kind}wrl", [128, E], BF16))
            o["wT"] = [s(nc.sbuf_tensor(f"{kind}wT{h}", [128, EC * 512], BF16))
                       for h in range(2)]
            o["trp"] = [s(nc.psum_tensor(f"{kind}tp{i}", [128, 1024], BF16)) for i in range(2)]
            mm = [s(nc.psum_tensor(f"{kind}mm{i}", [128, 512], F32)) for i in range(2)]
            q32 = s(nc.sbuf_tensor(f"{kind}q32", [128, 512], F32))
            qc = s(nc.sbuf_tensor(f"{kind}qc", [128, 512], F32))
            qhib2 = [s(nc.sbuf_tensor(f"{kind}hib{i}", [128, 512], BF16)) for i in range(2)]
            qlob2 = [s(nc.sbuf_tensor(f"{kind}lob{i}", [128, 512], BF16)) for i in range(2)]
            ebuf = [s(nc.sbuf_tensor(f"{kind}eb{i}", [128, 512], BF16)) for i in range(2)]
            if rope:
                costl = [s(nc.sbuf_tensor(f"{kind}cs{i}", [128, 512], F32)) for i in range(2)]
                sintl = [s(nc.sbuf_tensor(f"{kind}sn{i}", [128, 512], F32)) for i in range(2)]
            else:
                vint = s(nc.sbuf_tensor("vint", [128, 1024], BF16))
            o["s_wld"] = [self.sem(f"pj_wld{i}") for i in range(2)]
            o["s_wv"] = self.sem("pj_wv")
            o["s_wt"] = self.sem("pj_wt")
            o["s_we"] = self.sem("pj_we")
            s_mm = self.sem("pj_mm")
            s_rv = self.sem("pj_rv")
            s_tt = self.sem("pj_tt")
            s_ee = self.sem("pj_ee")
            s_do = self.sem("pj_do")
            s_dos = [self.sem(f"pj_dos{i}") for i in range(2)]
            s_cs = [self.sem(f"pj_cs{i}") for i in range(2)]
            o["wrow_free"] = [ln_done, ln_done]
            o["bank_free"] = [ln_done, ln_done]
            o["vec_start"] = ln_done
            o["slab_free"] = None

            if rope:
                cos_d = self.d["cos_own" if rope == "own" else "cos_all"]
                sin_d = self.d["sin_own" if rope == "own" else "sin_all"]

            mm_free = [ln_done, ln_done]
            cs_free = [ln_done, ln_done]
            ebuf_free = [ln_done, ln_done]
            vint_free = ln_done
            pending_qkT = None
            qsplit_free = [None, None]
            for fb in range(FB):
                wready = self.wT_prep(o, w_dram, fb, kind)
                last_mm_of_fb = None
                for i in range(ntiles):
                    bank = mm[i % 2]
                    waits = [wready, ln_done, mm_free[i % 2]]
                    for ec in range(EC):
                        c0 = ec * ntok + i * 128
                        p.op("tensor", lambda e, b=bank, c0=c0, ec=ec, wt=o["wT"][0]:
                             e.matmul(b[:], slab_hi[:, c0:c0 + 128],
                                      wt[:, ec * 512:(ec + 1) * 512],
                                      start=(ec == 0), stop=False),
                             waits=waits if ec == 0 else (), tag=f"{kind}m{fb}_{i}_{ec}a")
                        p.op("tensor", lambda e, b=bank, c0=c0, ec=ec, wt=o["wT"][1]:
                             e.matmul(b[:], slab_hi[:, c0:c0 + 128],
                                      wt[:, ec * 512:(ec + 1) * 512],
                                      start=False, stop=False), tag=f"{kind}m{fb}_{i}_{ec}b")
                        mmt = p.op("tensor", lambda e, b=bank, c0=c0, ec=ec, wt=o["wT"][0]:
                             e.matmul(b[:], slab_lo[:, c0:c0 + 128],
                                      wt[:, ec * 512:(ec + 1) * 512],
                                      start=False, stop=(ec == EC - 1)),
                             inc=(s_mm, 1) if ec == EC - 1 else None,
                             tag=f"{kind}m{fb}_{i}_{ec}c")
                    last_mm_of_fb = mmt
                    if rope:
                        cb, sb_ = costl[i % 2], sintl[i % 2]
                        ct = p.dma("sync", cb[:], cos_d[i * 128:(i + 1) * 128, :],
                                   waits=[cs_free[i % 2]], sem=s_cs[i % 2], tag=f"{kind}c{fb}{i}")
                        st_ = p.dma("sync", sb_[:], sin_d[i * 128:(i + 1) * 128, :],
                                    sem=s_cs[i % 2], tag=f"{kind}s{fb}{i}")
                        p.op("vector", lambda e, b=bank, cb=cb: e.tensor_mul(qc[:], b[:], cb[:]),
                             waits=[mmt, ct, st_], tag=f"{kind}r1{fb}_{i}")
                        p.op("vector", lambda e, b=bank, sb2=sb_:
                             e.tensor_mul(q32[:],
                                          APd(b, 64, [b[:].ap[0], [128, 4], [-64, 2], [1, 64]]),
                                          sb2[:]),
                             tag=f"{kind}r2{fb}_{i}")
                        ropet = p.op("vector", lambda e: e.tensor_add(q32[:], q32[:], qc[:]),
                                     inc=(s_rv, 1), tag=f"{kind}r3{fb}_{i}")
                        mm_free[i % 2] = ropet
                        cs_free[i % 2] = ropet
                        qp = i % 2
                        qhib, qlob = qhib2[qp], qlob2[qp]
                        p.op("vector", lambda e, qhib=qhib: e.tensor_copy(qhib[:], q32[:]),
                             waits=[qsplit_free[qp]], tag=f"{kind}p1{fb}_{i}")
                        spl = p.op("vector",
                                   lambda e, qhib=qhib, qlob=qlob:
                                   e.tensor_sub(qlob[:], q32[:], qhib[:]),
                                   inc=(s_rv, 1), tag=f"{kind}p2{fb}_{i}")

                        def emit_qkT(fb=fb, i=i, spl=spl, qhib=qhib, qlob=qlob, qp=qp):
                            last_tt2 = None
                            for half, src in enumerate([qhib, qlob]):
                                bank_t = o["trp"][half]
                                bf = o["bank_free"][half]
                                tt = None
                                for hh in range(4):
                                    tt = p.op("tensor", lambda e, s2=src, b=bank_t, hh=hh:
                                              e.transpose(b[:, hh * 128:(hh + 1) * 128],
                                                          s2[:, hh * 128:(hh + 1) * 128],
                                                          self.ident[:]),
                                              waits=[spl, bf] if hh == 0 else (),
                                              inc=(s_tt, 1) if hh == 3 else None,
                                              tag=f"{kind}tt{fb}{i}{half}{hh}")
                                eb = ebuf[half]
                                ev = p.op("scalar", lambda e, b=bank_t, eb=eb:
                                          e.copy(eb[:], b[:, :512]),
                                          waits=[tt, ebuf_free[half]], inc=(s_ee, 1),
                                          tag=f"{kind}te{fb}{i}{half}")
                                o["bank_free"][half] = ev
                                dst = out[half]
                                dmat = p.dma("gpsimd",
                                             APd(dst, (fb * 4) * 128 * ntok + i * 128,
                                                 [[ntok, 128], [128 * ntok, 4], [1, 128]]),
                                             APv(eb, 0, [[128, 4], [1, 128]]),
                                             waits=[ev], sem=s_dos[half],
                                             tag=f"{kind}d{fb}_{i}_{half}")
                                ebuf_free[half] = dmat
                                last_tt2 = tt
                            qsplit_free[qp] = last_tt2

                        if pending_qkT is not None:
                            pending_qkT()
                        pending_qkT = emit_qkT
                        if interleave:
                            interleave.pop(0)()
                    else:
                        vw = [mmt, vint_free]
                        p.op("vector", lambda e, b=bank:
                             e.tensor_copy(APv(vint, 0, [[256, 4], [1, 128]]), b[:]),
                             waits=vw, tag=f"vp1{fb}_{i}")
                        spl = p.op("vector", lambda e, b=bank:
                                   e.tensor_sub(APv(vint, 128, [[256, 4], [1, 128]]), b[:],
                                                APv(vint, 0, [[256, 4], [1, 128]])),
                                   inc=(s_rv, 1), tag=f"vp2{fb}_{i}")
                        mm_free[i % 2] = spl
                        vint_free = p.dma("gpsimd",
                              APd(out[0], ((fb * 4) * MT + i) * 128 * 256,
                                  [[256, 128], [MT * 128 * 256, 4], [1, 256]]),
                              vint[:], waits=[spl], sem=s_do, tag=f"vd{fb}_{i}")
                if pending_qkT is not None:
                    pending_qkT()
                    pending_qkT = None
                o["slab_free"] = last_mm_of_fb
            while interleave:
                interleave.pop(0)()
            return [(s_do, s_do.n), (s_dos[0], s_dos[0].n), (s_dos[1], s_dos[1].n)]

    def kv_setup(self, sc, ln, d, kv_in, kv_out, prev):
        nc, p, cfg = self.nc, self.p, self.cfg
        EC = cfg.EC
        s = sc.enter_context
        slab_hi, slab_lo, ntok, ln_done = ln
        st = {}
        o = {}
        wr0 = s(nc.sbuf_tensor("kvwr0", [128, E], F32))
        o["wrow"] = [wr0, wr0]
        o["wrow_single"] = True
        o["wrhi"] = s(nc.sbuf_tensor("kvwrh", [128, E], BF16))
        o["wrlo"] = s(nc.sbuf_tensor("kvwrl", [128, E], BF16))
        o["wT"] = [s(nc.sbuf_tensor(f"kvwT{h}", [128, EC * 512], BF16)) for h in range(2)]
        o["trp"] = [s(nc.psum_tensor(f"kvtp{i}", [128, 1024], BF16)) for i in range(2)]
        o["s_wld"] = [self.sem(f"pj_wld{i}") for i in range(2)]
        o["s_wv"] = self.sem("pj_wv")
        o["s_wt"] = self.sem("pj_wt")
        o["s_we"] = self.sem("pj_we")
        start = flat(prev, ln_done)
        o["wrow_free"] = [start, start]
        o["bank_free"] = [start, start]
        o["vec_start"] = start
        o["slab_free"] = None
        st["o"] = o
        st["ln"] = (slab_hi, slab_lo, ntok)
        st["lnb_hi"] = [s(nc.sbuf_tensor(f"kvlh{i}", [128, EC * 128], BF16))
                        for i in range(2)]
        st["lnb_lo"] = [s(nc.sbuf_tensor(f"kvll{i}", [128, EC * 128], BF16))
                        for i in range(2)]
        st["s_lnb"] = [self.sem(f"pj_lnb{i}") for i in range(2)]
        st["lnb_free"] = [start, start]
        st["mm"] = [s(nc.psum_tensor(f"kvmm{i}", [128, 512], F32)) for i in range(2)]
        st["q32"] = s(nc.sbuf_tensor("kvq32", [128, 512], F32))
        st["qc"] = s(nc.sbuf_tensor("kvqc", [128, 512], F32))
        st["qhib"] = [s(nc.sbuf_tensor(f"kvhib{i}", [128, 512], BF16)) for i in range(2)]
        st["qlob"] = [s(nc.sbuf_tensor(f"kvlob{i}", [128, 512], BF16)) for i in range(2)]
        cs0 = s(nc.sbuf_tensor("kvcs0", [128, 512], F32))
        sn0 = s(nc.sbuf_tensor("kvsn0", [128, 512], F32))
        st["cs"] = [cs0, cs0]
        st["sn"] = [sn0, sn0]
        st["ebuf"] = [s(nc.sbuf_tensor(f"kveb{i}", [128, 512], BF16)) for i in range(2)]
        st["vint"] = s(nc.sbuf_tensor("kvvint", [128, 1024], BF16))
        st["s_mm"] = self.sem("pj_mm")
        st["s_rv"] = self.sem("pj_rv")
        st["s_tt"] = self.sem("pj_tt")
        st["s_ee"] = self.sem("pj_ee")
        st["s_do"] = self.sem("pj_do")
        st["s_dos"] = [self.sem(f"pj_dos{i}") for i in range(2)]
        st["s_cc"] = self.sem("pj_cc")
        st["mm_free"] = [start, start]
        st["cs_free"] = [start, start]
        st["ebuf_free"] = [start, start]
        st["vint_free"] = start
        st["pending"] = None
        st["qsplit_free"] = [start, start]
        st["d"] = d
        st["kv_in"] = kv_in
        st["kv_out"] = kv_out
        st["fbdone"] = {}
        return st

    def kv_chunks(self, st, fb):
        nc, p, cfg = self.nc, self.p, self.cfg
        EC, NT = cfg.EC, cfg.NT
        o = st["o"]
        slab_hi, slab_lo, ntok = st["ln"]
        KSEG, VSEG, FBSEG = self.kv_geom
        kin = st["kv_in"][fb]
        chunks = []
        store_toks = []

        def tile(kind, fb, i):
            wready = st["holder"]["wready"]
            bank = st["mm"][i % 2]
            sl = i % 2
            lnbh, lnbl = st["lnb_hi"][sl], st["lnb_lo"][sl]
            lnt = None
            for half, (scr, buf) in enumerate([(slab_hi, lnbh), (slab_lo, lnbl)]):
                lnt = p.dma("sync", buf[:],
                            APd(scr, i * 128, [[ntok, 128], [ntok * 128, EC], [1, 128]]),
                            waits=[st["lnb_free"][sl]] if half == 0 else (),
                            sem=st["s_lnb"][sl], tag=f"{kind}ln{fb}_{i}_{half}")
            waits = [wready, st["mm_free"][i % 2], lnt]
            mmt = None
            for ec in range(EC):
                c0 = ec * 128
                p.op("tensor", lambda e, b=bank, c0=c0, ec=ec, x=lnbh, wt=o["wT"][0]:
                     e.matmul(b[:], x[:, c0:c0 + 128],
                              wt[:, ec * 512:(ec + 1) * 512],
                              start=(ec == 0), stop=False),
                     waits=waits if ec == 0 else (), tag=f"{kind}m{fb}_{i}_{ec}a")
                p.op("tensor", lambda e, b=bank, c0=c0, ec=ec, x=lnbh, wt=o["wT"][1]:
                     e.matmul(b[:], x[:, c0:c0 + 128],
                              wt[:, ec * 512:(ec + 1) * 512],
                              start=False, stop=False), tag=f"{kind}m{fb}_{i}_{ec}b")
                mmt = p.op("tensor", lambda e, b=bank, c0=c0, ec=ec, x=lnbl, wt=o["wT"][0]:
                     e.matmul(b[:], x[:, c0:c0 + 128],
                              wt[:, ec * 512:(ec + 1) * 512],
                              start=False, stop=(ec == EC - 1)),
                     inc=(st["s_mm"], 1) if ec == EC - 1 else None,
                     tag=f"{kind}m{fb}_{i}_{ec}c")
            st["lnb_free"][sl] = mmt
            if kind == "k":
                cb, sb_ = st["cs"][i % 2], st["sn"][i % 2]
                ct = p.dma("sync", cb[:], self.d["cos_own"][i * 128:(i + 1) * 128, :],
                           waits=[st["cs_free"][0], st["cs_free"][1]],
                           sem=self.sem(f"pj_cs{i % 2}"), tag=f"kc{fb}{i}")
                st_2 = p.dma("sync", sb_[:], self.d["sin_own"][i * 128:(i + 1) * 128, :],
                             sem=self.sem(f"pj_cs{i % 2}"), tag=f"ks{fb}{i}")
                p.op("vector", lambda e, b=bank, cb=cb:
                     e.tensor_mul(st["qc"][:], b[:], cb[:]),
                     waits=[mmt, ct, st_2], tag=f"kr1{fb}_{i}")
                p.op("vector", lambda e, b=bank, sb2=sb_:
                     e.tensor_mul(st["q32"][:],
                                  APd(b, 64, [b[:].ap[0], [128, 4], [-64, 2], [1, 64]]),
                                  sb2[:]),
                     tag=f"kr2{fb}_{i}")
                ropet = p.op("vector", lambda e: e.tensor_add(st["q32"][:], st["q32"][:],
                                                              st["qc"][:]),
                             inc=(st["s_rv"], 1), tag=f"kr3{fb}_{i}")
                st["mm_free"][i % 2] = ropet
                st["cs_free"][i % 2] = ropet
                qp = i % 2
                qhib, qlob = st["qhib"][qp], st["qlob"][qp]
                p.op("vector", lambda e, qhib=qhib: e.tensor_copy(qhib[:], st["q32"][:]),
                     waits=[st["qsplit_free"][qp]], tag=f"kp1{fb}_{i}")
                spl = p.op("vector", lambda e, qhib=qhib, qlob=qlob:
                           e.tensor_sub(qlob[:], st["q32"][:], qhib[:]),
                           inc=(st["s_rv"], 1), tag=f"kp2{fb}_{i}")

                def emit_T(fb=fb, i=i, spl=spl, qhib=qhib, qlob=qlob, qp=qp):
                    last_tt2 = None
                    for half, src in enumerate([qhib, qlob]):
                        bank_t = o["trp"][half]
                        bf = o["bank_free"][half]
                        tt = None
                        for hh in range(4):
                            tt = p.op("tensor", lambda e, s2=src, b=bank_t, hh=hh:
                                      e.transpose(b[:, hh * 128:(hh + 1) * 128],
                                                  s2[:, hh * 128:(hh + 1) * 128],
                                                  self.ident[:]),
                                      waits=[spl, bf] if hh == 0 else (),
                                      inc=(st["s_tt"], 1) if hh == 3 else None,
                                      tag=f"ktt{fb}{i}{half}{hh}")
                        eb = st["ebuf"][half]
                        ev = p.op("scalar", lambda e, b=bank_t, eb=eb:
                                  e.copy(eb[:], b[:, :512]),
                                  waits=[tt, st["ebuf_free"][half]], inc=(st["s_ee"], 1),
                                  tag=f"kte{fb}{i}{half}")
                        o["bank_free"][half] = ev
                        dmat = p.dma("gpsimd",
                                     APd(kin, half * KSEG + i * 128,
                                         [[ntok, 128], [128 * ntok, 4], [1, 128]]),
                                     APv(eb, 0, [[128, 4], [1, 128]]),
                                     waits=[ev], sem=st["s_dos"][half],
                                     tag=f"kd{fb}_{i}_{half}")
                        st["ebuf_free"][half] = dmat
                        store_toks.append(dmat)
                        last_tt2 = tt
                    st["qsplit_free"][qp] = last_tt2

                if st["pending"] is not None:
                    st["pending"]()
                st["pending"] = emit_T
            else:
                vint = st["vint"]
                vw = [mmt, st["vint_free"]]
                p.op("vector", lambda e, b=bank:
                     e.tensor_copy(APv(vint, 0, [[256, 4], [1, 128]]), b[:]),
                     waits=vw, tag=f"vp1{fb}_{i}")
                spl = p.op("vector", lambda e, b=bank:
                           e.tensor_sub(APv(vint, 128, [[256, 4], [1, 128]]), b[:],
                                        APv(vint, 0, [[256, 4], [1, 128]])),
                           inc=(st["s_rv"], 1), tag=f"vp2{fb}_{i}")
                st["mm_free"][i % 2] = spl
                t = p.dma("gpsimd",
                          APd(kin, 2 * KSEG + i * 128 * 256,
                              [[256, 128], [NT * 128 * 256, 4], [1, 256]]),
                          vint[:], waits=[spl], sem=st["s_do"], tag=f"vd{fb}_{i}")
                st["vint_free"] = t
                store_toks.append(t)

        st.setdefault("holder", {})

        def prep(which, fb=fb):
            w_dram = st["d"]["wk" if which == "k" else "wv"]
            st["holder"]["wready"] = self.wT_prep(o, w_dram, fb, "kv" + which)

        chunks.append(lambda: prep("k"))
        for i in range(NT):
            chunks.append(lambda i=i: tile("k", fb, i))

        def flush_and_prep_v():
            if st["pending"] is not None:
                st["pending"]()
                st["pending"] = None
            o["slab_free"] = st["mm_free"][0]
            prep("v")
        chunks.append(flush_and_prep_v)
        for i in range(NT):
            chunks.append(lambda i=i: tile("v", fb, i))

        def fin():
            o["slab_free"] = st["mm_free"][0]
            cct = p.op("gpsimd",
                       lambda e, kin=kin, ko=st["kv_out"][fb]:
                       e.collective_compute("AllGather", mybir.AluOpType.bypass,
                                            [[0, 1], [2, 3], [4, 5], [6, 7]],
                                            [kin[:]], [ko[:]]),
                       waits=list(store_toks), inc=(st["s_cc"], 1), tag=f"kvag{fb}")
            st["fbdone"][fb] = cct
        chunks.append(fin)
        return chunks

    def attn_merged(self, d, ln, kv_in, kv_out, q_hi, q_lo, ao_scr,
                    fc_chunks=None):
        nc, p, cfg = self.nc, self.p, self.cfg
        NT, MT, HD = cfg.NT, cfg.MT, cfg.heads
        NTOK, MTOK = cfg.NTOK, cfg.MTOK
        KSEG, VSEG, FBSEG = self.kv_geom
        with ExitStack() as sc:
            s = sc.enter_context
            kvst = self.kv_setup(sc, ln, d, kv_in, kv_out, self.q_done)
            qtb = [[s(nc.sbuf_tensor(f"atq{b}{h}", [128, 128], BF16)) for h in range(2)]
                   for b in range(2)]
            kb = [[s(nc.sbuf_tensor(f"atk{b}{h}", [128, MTOK], BF16)) for h in range(2)]
                  for b in range(2)]
            vb0 = s(nc.sbuf_tensor("atv", [128, MT * 256], BF16))
            vb = [vb0, vb0]
            ehi0 = s(nc.sbuf_tensor("atehi", [128, MTOK], BF16))
            elo0 = s(nc.sbuf_tensor("atelo", [128, MTOK], BF16))
            ehi2 = [ehi0, ehi0]
            elo2 = [elo0, elo0]
            eThi = s(nc.sbuf_tensor("ateThi", [128, MT * 512], BF16))
            eTlo = s(nc.sbuf_tensor("ateTlo", [128, MT * 512], BF16))
            ao = s(nc.sbuf_tensor("atao", [128, NTOK], F32))
            mx = s(nc.sbuf_tensor("atmx", [128, 1], F32))
            nbias = s(nc.sbuf_tensor("atnb", [128, 1], F32))
            rsp = s(nc.sbuf_tensor("atrsp", [128, 4], F32))
            rs = s(nc.sbuf_tensor("atrs", [128, 1], F32))
            rr = s(nc.sbuf_tensor("atrr", [128, 1], F32))
            if cfg.debug:
                self.dbg_scb = s(nc.sbuf_tensor("atdsc", [128, 4], F32))
            s320 = s(nc.sbuf_tensor("atS32", [128, 2048], F32))
            s32 = [s320, s320]
            psS = [s(nc.psum_tensor("atS0", [128, 512], F32))]
            psS.append(psS[0])
            psT = [s(nc.psum_tensor(f"atT{i}", [128, 1024], BF16)) for i in range(2)]
            psPV = s(nc.psum_tensor("atPV", [128, 512], F32))
            s_ld = [self.sem(f"at_ld{i}") for i in range(2)]
            s_S = self.sem("at_S")
            s_v = self.sem("at_v")
            s_s = self.sem("at_s")
            s_t = self.sem("at_t")
            s_e = self.sem("at_e")
            s_pv = self.sem("at_pv")
            s_ao = self.sem("at_ao")
            s_do = self.sem("at_do")

            start = flat(self.q_done, ln[3])
            start1 = start[0]
            inbuf_free = [None, None]
            NTAU = MTOK // 512

            def load_head(h, vwait=None):
                buf = h % 2
                fbh, hl = h // 4, h % 4
                ko = kv_out[fbh]
                w = flat(self.q_done, kvst["fbdone"].get(fbh), inbuf_free[buf])
                first = True
                for half in range(2):
                    for tau in range(NTAU):
                        p.dma("sync",
                              APd(kb[buf][half], tau * 512,
                                  [[MTOK, 128], [256, 2], [1, 256]]),
                              APd(ko, half * KSEG + hl * 128 * NTOK + tau * 256,
                                  [[NTOK, 128], [FBSEG, 2], [1, 256]]),
                              waits=w if first else (),
                              sem=s_ld[buf], tag=f"ak{half}_{h}_{tau}")
                        first = False
                t = None
                first_v = True
                for sg in range(2):
                    for tau in range(NTAU):
                        t = p.dma("sync",
                              APd(vb[buf], sg * 256 + tau * 1024,
                                  [[MT * 256, 128], [512, 2], [1, 256]]),
                              APd(ko, 2 * KSEG + hl * NT * 128 * 256 + sg * 32768
                                  + tau * 65536,
                                  [[256, 128], [FBSEG, 2], [1, 256]]),
                              waits=[vwait] if first_v else (),
                              sem=s_ld[buf], tag=f"av{sg}_{h}_{tau}")
                        first_v = False
                return t

            FBn = cfg.FB
            for fn in self.kv_chunks(kvst, 0):
                fn()
            pend_q = []
            s_qld = [self.sem(f"at_qld{i}") for i in range(2)]
            qtb_free = [start1, start1]
            ldt = load_head(0)
            if cfg.debug:
                kb_dbg = nc.dram_tensor("kb_dbg", [2, 128, MTOK], BF16,
                                        kind="ExternalOutput").ap()
                vb_dbg = nc.dram_tensor("vb_dbg", [128, MT * 256], BF16,
                                        kind="ExternalOutput").ap()
                ko_dbg = nc.dram_tensor("kvout0d", [2, FBSEG], BF16,
                                        kind="ExternalOutput").ap()
                ki_dbg = nc.dram_tensor("kvin0d", [FBSEG], BF16,
                                        kind="ExternalOutput").ap()
                self.dbg_s = nc.dram_tensor("dbg_s", [2, 128, MTOK], F32,
                                            kind="ExternalOutput").ap()
                self.dbg_e = nc.dram_tensor("dbg_e", [2, 128, MTOK], F32,
                                            kind="ExternalOutput").ap()
                self.dbg_ehi = nc.dram_tensor("dbg_ehi", [2, 128, MTOK], BF16,
                                              kind="ExternalOutput").ap()
                self.dbg_elo = nc.dram_tensor("dbg_elo", [2, 128, MTOK], BF16,
                                              kind="ExternalOutput").ap()
                self.dbg_sc = nc.dram_tensor("dbg_sc", [2, 128, 4], F32,
                                             kind="ExternalOutput").ap()
                self.dbg_eT = nc.dram_tensor("dbg_eT", [2, 128, MT * 512], BF16,
                                             kind="ExternalOutput").ap()
                s_dbg = self.sem("at_dbg")
                dw = [(s_ld[0], s_ld[0].n)]
                for hf in range(2):
                    p.dma("gpsimd", APd(kb_dbg, hf * 128 * MTOK,
                                        [[MTOK, 128], [1, MTOK]]),
                          kb[0][hf][:], waits=dw, sem=s_dbg, tag=f"dbgk{hf}")
                p.dma("gpsimd", APd(vb_dbg, 0, [[MT * 256, 128], [1, MT * 256]]),
                      vb[0][:], waits=dw, sem=s_dbg, tag="dbgv")
                p.dma("gpsimd", APd(ko_dbg, 0, [[FBSEG, 2], [1, FBSEG]]),
                      APd(kv_out[0], 0, [[FBSEG, 2], [1, FBSEG]]),
                      waits=dw, sem=s_dbg, tag="dbgko")
                p.dma("gpsimd", APd(ki_dbg, 0, [[1, 1], [1, FBSEG]]),
                      APd(kv_in[0], 0, [[1, 1], [1, FBSEG]]),
                      waits=dw, sem=s_dbg, tag="dbgki")
            pending_T = None
            esplit_free = [None, None]
            psS_free = [start1]
            psS_free.append(psS_free[0])
            s32_free = [start1, start1]
            eT_bank_free = [start1, start1]
            eT_slab_free = None
            ao_free = None
            tidx = 0
            for h in range(HD):
                if h % 4 == 0 and h // 4 + 1 < FBn:
                    pend_q = self.kv_chunks(kvst, h // 4 + 1)
                buf = h % 2
                kh, vh = kb[buf], vb[buf]
                ngroups = (NT + 3) // 4
                pv_last = None
                for qg in range(ngroups):
                    ntile_g = min(4, NT - qg * 4)
                    for qi in range(ntile_g):
                        qt = qg * 4 + qi
                        J = cfg.Jtile(qt)
                        keys = (J + 1) * 128
                        nch = (keys + 511) // 512
                        sp_ = tidx % 2
                        s32b = s32[sp_]
                        tidx += 1
                        qsl = qt % 2
                        qldt = None
                        for half, qsc in enumerate([q_hi, q_lo]):
                            qldt = p.dma("sync", qtb[qsl][half][:],
                                         APd(qsc, h * 128 * NTOK + qt * 128,
                                             [[NTOK, 128], [1, 128]]),
                                         waits=[qtb_free[qsl]] if half == 0 else (),
                                         sem=s_qld[qsl], tag=f"aq{h}_{qt}_{half}")
                        ev_last = None
                        for c in range(nch):
                            k0, k1 = c * 512, min((c + 1) * 512, keys)
                            bank = psS[c % 2]
                            sS_tok = None
                            for pi, (xs, ws) in enumerate([(0, 0), (1, 0), (0, 1)]):
                                w = []
                                if pi == 0:
                                    w = [psS_free[c % 2]]
                                    if c == 0:
                                        w.append(ldt)
                                        w.append(qldt)
                                sS_tok = p.op("tensor",
                                     lambda e, b=bank, q2=qtb[qsl][xs], k2=kh[ws],
                                            k0=k0, k1=k1, st=(pi == 0), sp2=(pi == 2):
                                     e.matmul(b[:, :k1 - k0], q2[:],
                                              k2[:, k0:k1], start=st, stop=sp2),
                                     waits=w,
                                     inc=(s_S, 1) if pi == 2 else None,
                                     tag=f"S{h}_{qt}_{c}_{pi}")
                                qtb_free[qsl] = sS_tok
                            wjl = (J - 2) * 128
                            if k0 >= wjl or k1 > wjl:
                                ov0 = max(k0, wjl)
                                ev1 = None
                                if ov0 > k0:
                                    ev1 = p.op("scalar",
                                         lambda e, b=bank, s3=s32b, k0=k0, ov0=ov0:
                                         e.copy(s3[:, k0:ov0], b[:, :ov0 - k0]),
                                         waits=[sS_tok, s32_free[sp_]],
                                         inc=(s_s, 1), tag=f"Se{h}_{qt}_{c}")
                                ev_last = p.op("vector",
                                     lambda e, b=bank, s3=s32b, k0=k0, ov0=ov0, k1=k1,
                                            wjl=wjl:
                                     e.tensor_add(s3[:, ov0:k1],
                                                  b[:, ov0 - k0:k1 - k0],
                                                  self.masks[:, ov0 - wjl:k1 - wjl]),
                                     waits=[sS_tok, s32_free[sp_]] + ([ev1] if ev1 else []),
                                     inc=(s_v, 1), tag=f"Sm{h}_{qt}_{c}")
                                psS_free[0] = ev_last
                                psS_free[1] = ev_last
                            else:
                                ev_last = p.op("scalar",
                                     lambda e, b=bank, s3=s32b, k0=k0, k1=k1:
                                     e.copy(s3[:, k0:k1], b[:, :k1 - k0]),
                                     waits=[sS_tok, s32_free[sp_]],
                                     inc=(s_s, 1), tag=f"Se{h}_{qt}_{c}")
                                psS_free[0] = ev_last
                                psS_free[1] = ev_last
                        if cfg.debug and h == 0 and qt <= 1:
                            dsm = p.dma("gpsimd",
                                        APd(self.dbg_s, qt * 128 * MTOK,
                                            [[MTOK, 128], [1, keys]]),
                                        s32b[:, :keys], waits=[ev_last, (s_s, s_s.n)],
                                        sem=self.sem("at_dbg"), tag=f"dS{qt}")
                            ev_last = flat(ev_last, dsm)
                        mxt = p.op("vector", lambda e, s3=s32b, keys=keys:
                             e.reduce_max(mx[:], s3[:, :keys], axis=AX),
                             waits=[ev_last, (s_s, s_s.n)],
                             inc=(s_v, 1), tag=f"mx{h}_{qt}")
                        nbt = p.op("vector", lambda e: e.tensor_scalar(
                                nbias[:], mx[:], -SCALE, None, ALU.mult),
                             waits=[mxt], inc=(s_v, 1), tag=f"nb{h}_{qt}")
                        expt = None
                        for c in range(nch):
                            k0, k1 = c * 512, min((c + 1) * 512, keys)
                            expt = p.op("scalar", lambda e, s3=s32b, k0=k0, k1=k1, c=c:
                                 e.activation(s3[:, k0:k1], s3[:, k0:k1], ACTF.Exp,
                                              bias=nbias[:], scale=SCALE,
                                              accum_out=rsp[:, c:c + 1]),
                                 waits=[nbt] if c == 0 else (),
                                 inc=(s_s, 1) if c == nch - 1 else None,
                                 tag=f"ex{h}_{qt}_{c}")
                        rst = p.op("vector", lambda e, nch=nch:
                             e.reduce_sum(rs[:], rsp[:, :nch], axis=AX),
                             waits=[expt], inc=(s_v, 1), tag=f"rs{h}_{qt}")
                        rrt = p.op("vector", lambda e: e.reciprocal(rr[:], rs[:]),
                             waits=[rst], inc=(s_v, 1), tag=f"rr{h}_{qt}")
                        if pending_T is not None:
                            pending_T()
                            pending_T = None
                        ehi, elo = ehi2[sp_], elo2[sp_]
                        p.op("vector", lambda e, s3=s32b, ehi=ehi, keys=keys:
                             e.tensor_scalar(
                                ehi[:, :keys], s3[:, :keys], rr[:], None, ALU.mult),
                             waits=[esplit_free[sp_], rrt], tag=f"e1{h}_{qt}")
                        spl = p.op("vector",
                                   lambda e, s3=s32b, ehi=ehi, elo=elo, keys=keys:
                                   e.scalar_tensor_tensor(
                                elo[:, :keys], s3[:, :keys], rr[:], ehi[:, :keys],
                                ALU.mult, ALU.subtract),
                             inc=(s_v, 1), tag=f"e2{h}_{qt}")
                        s32_free[0] = spl
                        s32_free[1] = spl
                        if cfg.debug and h == 0 and qt <= 1:
                            dse = p.dma("gpsimd",
                                        APd(self.dbg_e, qt * 128 * MTOK,
                                            [[MTOK, 128], [1, keys]]),
                                        s32b[:, :keys], waits=[spl],
                                        sem=self.sem("at_dbg"), tag=f"dE{qt}")
                            deh = p.dma("gpsimd",
                                        APd(self.dbg_ehi, qt * 128 * MTOK,
                                            [[MTOK, 128], [1, keys]]),
                                        ehi[:, :keys], waits=[spl],
                                        sem=self.sem("at_dbg"), tag=f"dEh{qt}")
                            del_ = p.dma("gpsimd",
                                         APd(self.dbg_elo, qt * 128 * MTOK,
                                             [[MTOK, 128], [1, keys]]),
                                         elo[:, :keys], waits=[spl],
                                         sem=self.sem("at_dbg"), tag=f"dEl{qt}")
                            for si2, src2 in enumerate([mx, rs, rr]):
                                p.op("scalar", lambda e, s2=src2, si2=si2:
                                     e.copy(self.dbg_scb[:, si2:si2 + 1], s2[:]),
                                     waits=[spl], inc=(s_s, 1), tag=f"dsc{qt}_{si2}")
                            dsc = p.dma("gpsimd",
                                        APd(self.dbg_sc, qt * 128 * 4,
                                            [[4, 128], [1, 4]]),
                                        self.dbg_scb[:], waits=[(s_s, s_s.n)],
                                        sem=self.sem("at_dbg"), tag=f"dM{qt}")
                            dsc2 = dsc3 = dsc
                            s32_free[0] = flat(spl, dse)
                            s32_free[1] = s32_free[0]
                            esplit_free[0] = flat(esplit_free[0], deh, del_)
                            esplit_free[1] = flat(esplit_free[1], deh, del_)
                            self._dbg_sc_toks = [dsc, dsc2, dsc3]

                        def emit_T(h=h, qt=qt, qi=qi, J=J, spl=spl, ehi=ehi, elo=elo,
                                   sp_=sp_):
                            last_tt = None
                            for half, (src, slab) in enumerate([(ehi, eThi), (elo, eTlo)]):
                                bank = psT[half]
                                for g4 in range((J + 1 + 3) // 4):
                                    j0 = g4 * 4
                                    cnt = min(4, J + 1 - j0)
                                    bf = eT_bank_free[half]
                                    tt = None
                                    for jj in range(cnt):
                                        j = j0 + jj
                                        tt = p.op("tensor",
                                                  lambda e, s2=src, b=bank, j=j, jj=jj:
                                                  e.transpose(b[:, jj * 128:(jj + 1) * 128],
                                                              s2[:, j * 128:(j + 1) * 128],
                                                              self.ident[:]),
                                                  waits=[spl, eT_slab_free, bf]
                                                        if jj == 0 else (),
                                                  inc=(s_t, 1) if jj == cnt - 1 else None,
                                                  tag=f"T{h}_{qt}_{half}_{j}")
                                    ev = p.op("scalar",
                                              lambda e, b=bank, slab=slab, j0=j0, cnt=cnt,
                                                     qi=qi:
                                              e.copy(APv(slab, j0 * 512 + qi * 128,
                                                         [[512, cnt], [1, 128]]),
                                                     b[:, :cnt * 128]),
                                              waits=[tt], inc=(s_e, 1),
                                              tag=f"Te{h}_{qt}_{half}_{g4}")
                                    eT_bank_free[half] = ev
                                    last_tt = tt
                            esplit_free[0] = last_tt
                            esplit_free[1] = last_tt

                        pending_T = emit_T
                        heads_left = HD - h
                        tiles_left = heads_left * NT - (qg * 4 + qi + 1)
                        if pend_q:
                            tl_fb = 4 * NT - (h % 4) * NT - (qg * 4 + qi + 1)
                            take = (len(pend_q) + max(tl_fb, 1) - 1) // max(tl_fb, 1)
                            for _ in range(min(take, len(pend_q))):
                                pend_q.pop(0)()
                        if fc_chunks and tiles_left > 0:
                            take = (len(fc_chunks) + tiles_left - 1) // tiles_left
                            for _ in range(min(take, len(fc_chunks))):
                                fc_chunks.pop(0)()
                    if pending_T is not None:
                        pending_T()
                        pending_T = None
                    eT_done = eT_bank_free[1]
                    if cfg.debug and h == 0 and qg == 0:
                        det = []
                        for si, slab2 in enumerate([eThi, eTlo]):
                            det.append(p.dma(
                                "gpsimd",
                                APd(self.dbg_eT, si * 128 * MT * 512,
                                    [[MT * 512, 128], [1, MT * 512]]),
                                slab2[:], waits=[eT_done, eT_bank_free[0]],
                                sem=self.sem("at_dbg"), tag=f"dT{si}"))
                        eT_done = flat(eT_done, det)
                    Jmax = cfg.Jtile(qg * 4 + ntile_g - 1)
                    first = True
                    pv_tok = None
                    for j in range(Jmax + 1):
                        qstart = 0
                        while qstart < ntile_g and cfg.Jtile(qg * 4 + qstart) < j:
                            qstart += 1
                        qoff = qstart * 128
                        width = ntile_g * 128 - qoff
                        if width <= 0:
                            continue
                        for pi, (vs, es) in enumerate([(0, eThi), (1, eThi), (0, eTlo)]):
                            w = [eT_done, ao_free] if first else ()
                            first = False
                            pv_tok = p.op("tensor",
                                 lambda e, vh2=vh, voff=j * 256 + vs * 128, es=es, j=j,
                                        qoff=qoff, width=width,
                                        st=(j == 0 and pi == 0),
                                        sp=(j == Jmax and pi == 2):
                                 e.matmul(psPV[:, qoff:qoff + width],
                                          APv(vh2, voff, [[1, 128]]),
                                          APv(es, j * 512 + qoff, [[1, width]]),
                                          start=st, stop=sp),
                                 waits=w,
                                 inc=(s_pv, 1) if (j == Jmax and pi == 2) else None,
                                 tag=f"P{h}_{qg}_{j}_{pi}")
                    eT_slab_free = pv_tok
                    aoev = p.op("vector", lambda e, gq0=qg * 4, wd=ntile_g * 128:
                                e.tensor_copy(ao[:, gq0 * 128:gq0 * 128 + wd], psPV[:, :wd]),
                                waits=[pv_tok], inc=(s_ao, 1), tag=f"ae{h}_{qg}")
                    ao_free = aoev
                    pv_last = aoev
                aodma = p.dma("gpsimd", APd(ao_scr, h * 128 * NTOK, [[NTOK, 128], [1, NTOK]]),
                              ao[:], waits=[pv_last], sem=s_do, tag=f"ad{h}")
                ao_free = aodma
                inbuf_free[buf] = pv_last
                if h + 1 < HD:
                    while pend_q and (h + 1) % 4 == 0:
                        pend_q.pop(0)()
                    ldt = load_head(h + 1, vwait=pv_last)
            self.attn_done = flat((s_do, s_do.n), kvst["fbdone"].get(cfg.FB - 1))

    def ffn_weight_cache_mk(self, fc_stack, d, wg_scr, wu_scr, wd_scr, prev):
        nc, p, cfg = self.nc, self.p, self.cfg
        EC, OC, INTER = cfg.EC, cfg.OC, cfg.INTER
        s = fc_stack.enter_context
        W = max(INTER // 4, E)
        i80 = s(nc.sbuf_tensor("fci0", [128, W], I8))
        i8b = [i80, i80]
        bfb = [s(nc.sbuf_tensor(f"fcb{i}", [128, W], BF16)) for i in range(2)]
        s_ld = [self.sem(f"fc_ld{i}") for i in range(2)]
        s_cg = self.sem("fc_cg")
        s_cv = self.sem("fc_cv")
        s_st = [self.sem(f"fc_st{i}") for i in range(2)]
        half = INTER // 2
        st = {"i8_free": [prev, prev], "bf_free": [prev, prev], "idx": 0}

        def emit_gu(wi, hc, h2):
            w8, scr = [(d["wg8"], wg_scr), (d["wu8"], wu_scr)][wi]
            bi = st["idx"] % 2
            st["idx"] += 1
            W2 = INTER // 4
            q4 = W2 // 2
            ldt = p.dma("sync", i8b[bi][:, :W2],
                        w8[hc * 128:(hc + 1) * 128, h2 * W2:(h2 + 1) * W2],
                        waits=[st["i8_free"][bi]], sem=s_ld[bi], tag=f"fcl{wi}_{hc}_{h2}")
            c1 = p.op("gpsimd", lambda e, b=bi, q4=q4: e.tensor_copy(bfb[b][:, :q4],
                                                              i8b[b][:, :q4]),
                      waits=[ldt, st["bf_free"][bi]], inc=(s_cg, 1), tag=f"fcg{wi}_{hc}_{h2}")
            c2 = p.op("vector", lambda e, b=bi, q4=q4, W2=W2: e.tensor_copy(
                          bfb[b][:, q4:W2], i8b[b][:, q4:W2]),
                      waits=[ldt, st["bf_free"][bi]], inc=(s_cv, 1), tag=f"fcv{wi}_{hc}_{h2}")
            st["i8_free"][0] = [c1, c2, ldt]
            st["i8_free"][1] = [c1, c2, ldt]
            o0 = h2 * (OC // 4)
            t = p.dma("gpsimd",
                      APd(scr, (o0 * EC + hc) * 128 * 128,
                          [[128, 128], [EC * 128 * 128, OC // 4], [1, 128]]),
                      APv(bfb[bi], 0, [[128, OC // 4], [1, 128]]),
                      waits=[c1, c2], sem=s_st[bi], tag=f"fcs{wi}_{hc}_{h2}")
            st["bf_free"][bi] = t

        def emit_wd(oc):
            bi = st["idx"] % 2
            st["idx"] += 1
            ldt = p.dma("sync", i8b[bi][:, :E], d["wd8"][oc * 128:(oc + 1) * 128, :],
                        waits=[st["i8_free"][bi]], sem=s_ld[bi], tag=f"fcld{oc}")
            c1 = p.op("gpsimd", lambda e, b=bi: e.tensor_copy(bfb[b][:, :E // 2],
                                                              i8b[b][:, :E // 2]),
                      waits=[ldt, st["bf_free"][bi]], inc=(s_cg, 1), tag=f"fcgd{oc}")
            c2 = p.op("vector", lambda e, b=bi: e.tensor_copy(bfb[b][:, E // 2:E],
                                                              i8b[b][:, E // 2:E]),
                      waits=[ldt, st["bf_free"][bi]], inc=(s_cv, 1), tag=f"fcvd{oc}")
            st["i8_free"][0] = [c1, c2, ldt]
            st["i8_free"][1] = [c1, c2, ldt]
            toks = []
            for part in range(4):
                h0 = part * (EC // 4)
                t = p.dma("gpsimd",
                          APd(wd_scr, (h0 * OC + oc) * 128 * 128,
                              [[128, 128], [OC * 128 * 128, EC // 4], [1, 128]]),
                          APv(bfb[bi], h0 * 128, [[128, EC // 4], [1, 128]]),
                          waits=[c1, c2], sem=s_st[bi], tag=f"fcsd{oc}_{part}")
                toks.append(t)
            st["bf_free"][bi] = toks[-1]

        chunks = []
        for wi in range(2):
            for hc in range(EC):
                for h2 in range(4):
                    chunks.append(lambda wi=wi, hc=hc, h2=h2: emit_gu(wi, hc, h2))
        for oc in range(OC):
            chunks.append(lambda oc=oc: emit_wd(oc))
        self._fc_sems = (s_st,)
        return chunks

    def fc_finalize(self):
        s_st = self._fc_sems[0]
        self.fc_done = [(s_st[0], s_st[0].n), (s_st[1], s_st[1].n)]

    def oproj_phase(self, d, ao_scr, o2_scr, x1_scr, lnf_hi, lnf_lo):
        nc, p, cfg = self.nc, self.p, self.cfg
        NT, EC, HD = cfg.NT, cfg.EC, cfg.heads
        NTOK = cfg.NTOK
        prev = flat(self.attn_done, self.fc_done)
        with ExitStack() as sc2:
            s2 = sc2.enter_context
            aohi = s2(nc.sbuf_tensor("oah", [128, HD * NTOK], BF16))
            aolo = s2(nc.sbuf_tensor("oal", [128, HD * NTOK], BF16))
            aot = [s2(nc.sbuf_tensor(f"oat{i}", [128, NTOK], F32)) for i in range(2)]
            o = {}
            o["wrow"] = [s2(nc.sbuf_tensor(f"owr{i}", [128, E], F32)) for i in range(2)]
            o["wrhi"] = s2(nc.sbuf_tensor("owh", [128, E], BF16))
            o["wrlo"] = s2(nc.sbuf_tensor("owl", [128, E], BF16))
            wT2 = [[s2(nc.sbuf_tensor(f"owT{b}{h}", [128, EC * 512], BF16))
                    for h in range(2)] for b in range(2)]
            o["wT"] = wT2[0]
            o["trp"] = [s2(nc.psum_tensor(f"otp{i}", [128, 1024], BF16)) for i in range(2)]
            mm = [s2(nc.psum_tensor(f"omm{i}", [128, 512], F32)) for i in range(2)]
            evb = [s2(nc.sbuf_tensor(f"oev{i}", [128, 512], F32)) for i in range(2)]
            o["s_wld"] = [self.sem(f"pj_wld{i}") for i in range(2)]
            o["s_wv"] = self.sem("pj_wv")
            o["s_wt"] = self.sem("pj_wt")
            o["s_we"] = self.sem("pj_we")
            s_ld = [self.sem(f"o_ld{i}") for i in range(2)]
            s_v = self.sem("o_v")
            s_mm = self.sem("pj_mm")
            s_ev = self.sem("o_ev")
            s_evd = [self.sem(f"o_evd{i}") for i in range(2)]
            o["wrow_free"] = [prev, prev]
            o["bank_free"] = [prev, prev]
            o["vec_start"] = prev
            o["slab_free"] = None
            aot_free = [prev, prev]
            spl = None
            for hh in range(HD):
                buf = aot[hh % 2]
                ldt = p.dma("sync", buf[:], APd(ao_scr, hh * 128 * NTOK,
                                                [[NTOK, 128], [1, NTOK]]),
                            waits=[aot_free[hh % 2], prev], sem=s_ld[hh % 2], tag=f"oal{hh}")
                p.op("vector", lambda e, b=buf, hh=hh:
                     e.tensor_copy(aohi[:, hh * NTOK:(hh + 1) * NTOK], b[:]),
                     waits=[ldt, prev], tag=f"os1{hh}")
                spl = p.op("vector", lambda e, b=buf, hh=hh:
                           e.tensor_sub(aolo[:, hh * NTOK:(hh + 1) * NTOK], b[:],
                                        aohi[:, hh * NTOK:(hh + 1) * NTOK]),
                           inc=(s_v, 1), tag=f"os2{hh}")
                aot_free[hh % 2] = spl
            split_done = spl
            mm_free = [prev, prev]
            evb_free = [prev, prev]
            slab_mm_last = [None, None]
            for fb in range(4):
                o["wT"] = wT2[fb % 2]
                o["slab_free"] = slab_mm_last[fb % 2]
                wready = self.wT_prep(o, d["wo"], fb, "o")
                last_mm_of_fb = None
                for i in range(NT):
                    bank = mm[i % 2]
                    waits = [wready, split_done, mm_free[i % 2]]
                    last = None
                    for ec in range(HD):
                        col = ec * NTOK + i * 128
                        p.op("tensor", lambda e, b=bank, col=col, ec=ec, wt=o["wT"][0]:
                             e.matmul(b[:], aohi[:, col:col + 128],
                                      wt[:, ec * 512:(ec + 1) * 512],
                                      start=(ec == 0), stop=False),
                             waits=waits if ec == 0 else (), tag=f"om{fb}_{i}_{ec}a")
                        p.op("tensor", lambda e, b=bank, col=col, ec=ec, wt=o["wT"][1]:
                             e.matmul(b[:], aohi[:, col:col + 128],
                                      wt[:, ec * 512:(ec + 1) * 512],
                                      start=False, stop=False), tag=f"om{fb}_{i}_{ec}b")
                        last = p.op("tensor",
                             lambda e, b=bank, col=col, ec=ec, wt=o["wT"][0]:
                             e.matmul(b[:], aolo[:, col:col + 128],
                                      wt[:, ec * 512:(ec + 1) * 512],
                                      start=False, stop=(ec == HD - 1)),
                             inc=(s_mm, 1) if ec == HD - 1 else None,
                             tag=f"om{fb}_{i}_{ec}c")
                    last_mm_of_fb = last
                    eb = evb[i % 2]
                    ev = p.op("scalar", lambda e, b=bank, eb=eb: e.copy(eb[:], b[:]),
                              waits=[last, evb_free[i % 2]], inc=(s_ev, 1),
                              tag=f"oe{fb}_{i}")
                    mm_free[i % 2] = ev
                    dmat = p.dma("gpsimd",
                                 APd(o2_scr, i * 128 * E + fb * 512, [[E, 128], [1, 512]]),
                                 eb[:], waits=[ev], sem=s_evd[i % 2], tag=f"oed{fb}_{i}")
                    evb_free[i % 2] = dmat
                slab_mm_last[fb % 2] = last_mm_of_fb
            mm_all = [(s_evd[0], s_evd[0].n), (s_evd[1], s_evd[1].n)]

        with ExitStack() as sc3:
            s3 = sc3.enter_context
            o2t = [s3(nc.sbuf_tensor(f"oo2{i}", [128, E], F32)) for i in range(2)]
            xt = [s3(nc.sbuf_tensor(f"oxt{i}", [128, E], F32)) for i in range(2)]
            bns = s3(nc.sbuf_tensor("obns", [128, 24], F32))
            bno = s3(nc.sbuf_tensor("obno", [128, 2], F32))
            std = s3(nc.sbuf_tensor("ostd", [128, 1], F32))
            rstd = s3(nc.sbuf_tensor("orstd", [128, 1], F32))
            x1t = s3(nc.sbuf_tensor("ox1", [128, E], F32))
            ln3 = s3(nc.sbuf_tensor("oln3", [128, E], F32))
            lhi = s3(nc.sbuf_tensor("olh", [128, E], BF16))
            llo = s3(nc.sbuf_tensor("oll", [128, E], BF16))
            trp2 = [s3(nc.psum_tensor(f"otr2{i}", [128, 1024], BF16)) for i in range(2)]
            lfst = s3(nc.sbuf_tensor("olf", [128, 2 * E], BF16))
            s_x = [self.sem(f"o_x{i}") for i in range(2)]
            s_o2 = [self.sem(f"o_o2{i}") for i in range(2)]
            s_lf = self.sem("o_lf")
            s_v2 = self.sem("o_v2")
            s_s2 = self.sem("o_s2")
            s_t2 = self.sem("o_t2")
            s_e2 = self.sem("o_e2")
            s_d2 = self.sem("o_d2")
            mm_all0 = mm_all[0]
            o2_free = [mm_all0, mm_all0]
            xt_free = [mm_all0, mm_all0]
            bank_free = [mm_all0, mm_all0]
            lf_free = None
            x1_prev = None
            for i in range(NT):
                o2b = o2t[i % 2]
                o2ldt = p.dma("sync", o2b[:], o2_scr[i * 128:(i + 1) * 128, :],
                              waits=flat(o2_free[i % 2], mm_all), sem=s_o2[i % 2],
                              tag=f"o2l{i}")
                xb = xt[i % 2]
                xldt = p.dma("sync", xb[:], d["x_own"][i * 128:(i + 1) * 128, :],
                             waits=[xt_free[i % 2]], sem=s_x[i % 2], tag=f"oxl{i}")
                stats = self.ln_stats(bns, bno,
                                      lambda c, b=o2b: b[:, c * 512:(c + 1) * 512],
                                      E, flat(o2ldt, mm_all), s_v2, f"ob{i}")
                sq = p.op("scalar", lambda e: e.activation(std[:], bno[:, 1:2], ACTF.Sqrt,
                                                           bias=self.epsb[:]),
                          waits=flat(stats, mm_all), inc=(s_s2, 1), tag=f"osq{i}")
                rct = p.op("vector", lambda e: e.reciprocal(rstd[:], std[:]), waits=[sq],
                           inc=(s_v2, 1), tag=f"orc{i}")
                p.op("vector", lambda e, b=o2b: e.tensor_scalar(
                        x1t[:], b[:], bno[:, 0:1], rstd[:], ALU.subtract, ALU.mult),
                     waits=[x1_prev, rct], tag=f"oln2{i}")
                rest = p.op("vector", lambda e, b=xb: e.tensor_add(x1t[:], x1t[:], b[:]),
                            waits=[xldt], inc=(s_v2, 1), tag=f"ores{i}")
                o2_free[i % 2] = rest
                xt_free[i % 2] = rest
                x1d = p.dma("gpsimd", x1_scr[i * 128:(i + 1) * 128, :], x1t[:],
                            waits=[rest], sem=s_d2, tag=f"ox1d{i}")
                stats3 = self.ln_stats(bns, bno,
                                       lambda c: x1t[:, c * 512:(c + 1) * 512],
                                       E, [], s_v2, f"ob3{i}")
                sq3 = p.op("scalar", lambda e: e.activation(std[:], bno[:, 1:2], ACTF.Sqrt,
                                                            bias=self.epsb[:]),
                           waits=[stats3], inc=(s_s2, 1), tag=f"osq3{i}")
                rct3 = p.op("vector", lambda e: e.reciprocal(rstd[:], std[:]), waits=[sq3],
                            inc=(s_v2, 1), tag=f"orc3{i}")
                x1_prev = p.op("vector", lambda e: e.tensor_scalar(
                        ln3[:], x1t[:], bno[:, 0:1], rstd[:], ALU.subtract, ALU.mult),
                     waits=[x1d, rct3], inc=(s_v2, 1), tag=f"oln3{i}")
                p.op("vector", lambda e: e.tensor_copy(lhi[:], ln3[:]),
                     waits=[lf_free], tag=f"ol1{i}")
                spl = p.op("vector", lambda e: e.tensor_sub(llo[:], ln3[:], lhi[:]),
                           inc=(s_v2, 1), tag=f"ol2{i}")
                gidx = 0
                for half, (src, coff) in enumerate([(lhi, 0), (llo, E)]):
                    for g in range(EC // 4):
                        bi = gidx % 2
                        bank = trp2[bi]
                        bf = bank_free[bi]
                        gidx += 1
                        tt = None
                        for j in range(4):
                            ec = g * 4 + j
                            tt = p.op("tensor", lambda e, s4=src, b=bank, ec=ec, j=j:
                                      e.transpose(b[:, j * 128:(j + 1) * 128],
                                                  s4[:, ec * 128:(ec + 1) * 128],
                                                  self.ident[:]),
                                      waits=[spl, bf] if j == 0 else (),
                                      inc=(s_t2, 1) if j == 3 else None,
                                      tag=f"ot{i}{half}{g}{j}")
                        base = coff + g * 4 * 128
                        ev = p.op("scalar", lambda e, b=bank, base=base:
                                  e.copy(lfst[:, base:base + 512], b[:, :512]),
                                  waits=[tt], inc=(s_e2, 1), tag=f"ote{i}{half}{g}")
                        bank_free[bi] = ev
                lf_toks = []
                last_ev_tok = bank_free[(gidx - 1) % 2]
                for half, (coff, dst) in enumerate([(0, lnf_hi), (E, lnf_lo)]):
                    t = p.dma("gpsimd",
                              APd(dst, i * 128, [[NTOK, 128], [NTOK * 128, EC], [1, 128]]),
                              APv(lfst, coff, [[128, EC], [1, 128]]),
                              waits=[last_ev_tok], sem=s_lf, tag=f"olf{i}_{half}")
                    lf_toks.append(t)
                lf_free = lf_toks[-1]
            self.oproj_done = [(s_d2, s_d2.n), (s_lf, s_lf.n)]

    def ffn_phase(self, d, lnf_hi, lnf_lo, wg_scr, wu_scr, wd_scr, x1_scr):
        nc, p, cfg = self.nc, self.p, self.cfg
        EC, OC, NT = cfg.EC, cfg.OC, cfg.NT
        NTOK = cfg.NTOK
        TC = cfg.TC
        chunks = []
        t0 = 0
        while t0 < NTOK:
            tc = min(TC, NTOK - t0)
            chunks.append((t0, tc))
            t0 += tc
        prev = flat(self.oproj_done, self.fc_done)
        with ExitStack() as sc:
            s = sc.enter_context
            fuse_hi = s(nc.sbuf_tensor("ffh", [128, OC * TC], BF16))
            fuse_lo = s(nc.sbuf_tensor("ffl", [128, OC * TC], BF16))
            lnc = [s(nc.sbuf_tensor(f"fln{h}", [128, EC * TC], BF16)) for h in range(2)]
            wgt = [s(nc.sbuf_tensor(f"fwg{i}", [128, EC * 128], BF16)) for i in range(2)]
            wut = [s(nc.sbuf_tensor(f"fwu{i}", [128, EC * 128], BF16)) for i in range(2)]
            wdt = [s(nc.sbuf_tensor(f"fwd{i}", [128, OC * 128], BF16)) for i in range(2)]
            gsil = s(nc.sbuf_tensor("fgs", [128, TC], F32))
            ups = s(nc.sbuf_tensor("fup", [128, TC], F32))
            fuse32 = s(nc.sbuf_tensor("ff32", [128, TC], F32))
            dn = s(nc.sbuf_tensor("fdn", [128, 512], F32))
            dn_all = s(nc.sbuf_tensor("fdna", [128, (TC // 128) * E], F32))
            x1t = [s(nc.sbuf_tensor(f"fx1{i}", [128, E], F32)) for i in range(2)]
            outt = s(nc.sbuf_tensor("fout", [128, E], F32))
            ln4 = s(nc.sbuf_tensor("fln4", [128, E], F32))
            bns = s(nc.sbuf_tensor("fbns", [128, 24], F32))
            bno = s(nc.sbuf_tensor("fbno", [128, 2], F32))
            std = s(nc.sbuf_tensor("fstd", [128, 1], F32))
            rstd = s(nc.sbuf_tensor("frstd", [128, 1], F32))
            psG = s(nc.psum_tensor("fG", [128, 512], F32))
            psU = s(nc.psum_tensor("fU", [128, 512], F32))
            psD = s(nc.psum_tensor("fD", [128, 512], F32))
            psTr = s(nc.psum_tensor("fTr", [128, 512], F32))
            s_lnld = self.sem("f_lnld")
            s_wld = [self.sem(f"f_wld{i}") for i in range(2)]
            s_wdld = [self.sem(f"f_wdld{i}") for i in range(2)]
            s_g = self.sem("f_g")
            s_sv = self.sem("f_sv")
            s_fv = self.sem("f_fv")
            s_dmm = self.sem("f_dmm")
            s_dev = self.sem("f_dev")
            s_dtr = self.sem("f_dtr")
            s_dte = self.sem("f_dte")
            s_v = self.sem("f_v")
            s_s = self.sem("f_s")

            fcd = self.fc_done
            lnc_free = None
            wg_free = [fcd[0], fcd[1]]
            wd_free = [fcd[0], fcd[1]]
            psG_free = prev[0]
            psU_free = prev[0]
            psD_free = prev[0]
            psTr_free = prev[0]
            fuse_free = prev[0]
            dn_free = prev[0]
            dna_free = prev[0]
            x1_free = [prev[0], prev[0]]
            outt_free = prev[0]
            gu_free = prev[0]
            s_x1 = [self.sem(f"f_x1s{i}") for i in range(2)]
            for ci, (tt0, tcw) in enumerate(chunks):
                ntile_c = tcw // 128
                lnt = None
                for half, src in enumerate([lnf_hi, lnf_lo]):
                    lnt = p.dma("sync", lnc[half][:, :EC * tcw],
                                APd(src, tt0, [[NTOK, 128], [NTOK * 128, EC], [1, tcw]]),
                                waits=(list(prev) if ci == 0 else [lnc_free]), sem=s_lnld,
                                tag=f"fl{ci}_{half}")
                last_g = None
                for oi in range(OC):
                    wb = oi % 2
                    wldt = p.dma("sync", wgt[wb][:],
                                 APd(wg_scr, oi * EC * 128 * 128,
                                     [[128, 128], [128 * 128, EC], [1, 128]]),
                                 waits=[wg_free[wb]] + (prev if ci == 0 and oi < 2 else []),
                                 sem=s_wld[wb], tag=f"fg{ci}_{oi}")
                    wldt2 = p.dma("sync", wut[wb][:],
                                  APd(wu_scr, oi * EC * 128 * 128,
                                      [[128, 128], [128 * 128, EC], [1, 128]]),
                                  sem=s_wld[wb], tag=f"fu{ci}_{oi}")
                    toks = {}
                    for (ps, wt, nm, psfree) in [(psG, wgt[wb], "g", psG_free),
                                                 (psU, wut[wb], "u", psU_free)]:
                        lastmm = None
                        for hc in range(EC):
                            w = [wldt, wldt2, lnt, psfree] if hc == 0 else ()
                            p.op("tensor", lambda e, ps=ps, wt=wt, hc=hc, tcw=tcw:
                                 e.matmul(ps[:, :tcw], wt[:, hc * 128:(hc + 1) * 128],
                                          lnc[0][:, hc * tcw:(hc + 1) * tcw],
                                          start=(hc == 0), stop=False),
                                 waits=w, tag=f"f{nm}{ci}_{oi}_{hc}a")
                            lastmm = p.op("tensor", lambda e, ps=ps, wt=wt, hc=hc, tcw=tcw:
                                 e.matmul(ps[:, :tcw], wt[:, hc * 128:(hc + 1) * 128],
                                          lnc[1][:, hc * tcw:(hc + 1) * tcw],
                                          start=False, stop=(hc == EC - 1)),
                                 inc=(s_g, 1) if hc == EC - 1 else None,
                                 tag=f"f{nm}{ci}_{oi}_{hc}b")
                        toks[nm] = lastmm
                    wg_free[wb] = toks["u"]
                    last_g = toks["u"]
                    sil = p.op("scalar", lambda e, oi=oi, tcw=tcw:
                               e.activation(gsil[:, :tcw], psG[:, :tcw], ACTF.Silu,
                                            scale=self.sg_sb[:, oi:oi + 1]),
                               waits=[toks["g"], gu_free], inc=(s_sv, 1),
                               tag=f"fsl{ci}_{oi}")
                    psG_free = sil
                    up = p.op("scalar", lambda e, oi=oi, tcw=tcw:
                              e.activation(ups[:, :tcw], psU[:, :tcw], ACTF.Copy,
                                           scale=self.su_sb[:, oi:oi + 1]),
                              waits=[toks["u"]], inc=(s_sv, 1), tag=f"fue{ci}_{oi}")
                    psU_free = up
                    fw = [sil, up]
                    if oi == 0:
                        fw.append(fuse_free)
                    gu_free = p.op("vector", lambda e, tcw=tcw: e.tensor_mul(
                            fuse32[:, :tcw], gsil[:, :tcw], ups[:, :tcw]),
                         waits=fw, inc=(s_fv, 1), tag=f"fm{ci}_{oi}")
                    p.op("vector", lambda e, oi=oi, tcw=tcw:
                         e.tensor_copy(fuse_hi[:, oi * TC:oi * TC + tcw], fuse32[:, :tcw]),
                         tag=f"fh{ci}_{oi}")
                    p.op("vector", lambda e, oi=oi, tcw=tcw:
                         e.tensor_sub(fuse_lo[:, oi * TC:oi * TC + tcw], fuse32[:, :tcw],
                                      fuse_hi[:, oi * TC:oi * TC + tcw]),
                         inc=(s_fv, 1), tag=f"fl2{ci}_{oi}")
                fv = (s_fv, s_fv.n)
                lnc_free = last_g
                dmm_last = None
                for hc in range(EC):
                    wb = hc % 2
                    wdldt = p.dma("sync", wdt[wb][:],
                                  APd(wd_scr, hc * OC * 128 * 128,
                                      [[128, 128], [128 * 128, OC], [1, 128]]),
                                  waits=[wd_free[wb]] + (prev if ci == 0 and hc < 2 else []),
                                  sem=s_wdld[wb], tag=f"fw{ci}_{hc}")
                    lastd = None
                    for oi in range(OC):
                        w = [fv, wdldt, psD_free] if oi == 0 else ()
                        p.op("tensor", lambda e, oi=oi, tcw=tcw, wb=wb:
                             e.matmul(psD[:, :tcw], wdt[wb][:, oi * 128:(oi + 1) * 128],
                                      fuse_hi[:, oi * TC:oi * TC + tcw],
                                      start=(oi == 0), stop=False),
                             waits=w, tag=f"fd{ci}_{hc}_{oi}a")
                        lastd = p.op("tensor", lambda e, oi=oi, tcw=tcw, wb=wb:
                             e.matmul(psD[:, :tcw], wdt[wb][:, oi * 128:(oi + 1) * 128],
                                      fuse_lo[:, oi * TC:oi * TC + tcw],
                                      start=False, stop=(oi == OC - 1)),
                             inc=(s_dmm, 1) if oi == OC - 1 else None,
                             tag=f"fd{ci}_{hc}_{oi}b")
                    wd_free[wb] = lastd
                    dmm_last = lastd
                    dev = p.op("scalar", lambda e, hc=hc, tcw=tcw:
                               e.activation(dn[:, :tcw], psD[:, :tcw], ACTF.Copy,
                                            scale=self.sd_sb[:, hc:hc + 1]),
                               waits=[lastd, dn_free, dna_free if hc == 0 else None],
                               inc=(s_dev, 1), tag=f"fde{ci}_{hc}")
                    psD_free = dev
                    tt = None
                    for j in range(ntile_c):
                        tt = p.op("tensor", lambda e, j=j:
                                  e.transpose(psTr[:, j * 128:(j + 1) * 128],
                                              dn[:, j * 128:(j + 1) * 128], self.identf[:]),
                                  waits=[dev, psTr_free] if j == 0 else (),
                                  inc=(s_dtr, 1) if j == ntile_c - 1 else None,
                                  tag=f"fdt{ci}_{hc}_{j}")
                    dn_free = tt
                    ev = p.op("vector", lambda e, hc=hc, ntile_c=ntile_c:
                              e.tensor_copy(APv(dn_all, hc * 128, [[E, ntile_c], [1, 128]]),
                                            psTr[:, :ntile_c * 128]),
                              waits=[tt], inc=(s_dte, 1), tag=f"fte{ci}_{hc}")
                    psTr_free = ev
                dna_ready = psTr_free
                fuse_free = dmm_last
                ep_last = None
                for j in range(ntile_c):
                    gt = (tt0 // 128) + j
                    xb = x1t[j % 2]
                    xldt = p.dma("sync", xb[:], x1_scr[gt * 128:(gt + 1) * 128, :],
                                 waits=[x1_free[j % 2]], sem=s_x1[j % 2], tag=f"fx{ci}_{j}")
                    stats = self.ln_stats(bns, bno,
                                          lambda c, j=j: dn_all[:, j * E + c * 512:
                                                                j * E + (c + 1) * 512],
                                          E, [dna_ready], s_v, f"fb{ci}_{j}")
                    sq = p.op("scalar", lambda e: e.activation(std[:], bno[:, 1:2], ACTF.Sqrt,
                                                               bias=self.epsb[:]),
                              waits=[stats], inc=(s_s, 1), tag=f"fsq{ci}_{j}")
                    rct = p.op("vector", lambda e: e.reciprocal(rstd[:], std[:]), waits=[sq],
                               inc=(s_v, 1), tag=f"frc{ci}_{j}")
                    p.op("vector", lambda e, j=j: e.tensor_scalar(
                            ln4[:], dn_all[:, j * E:(j + 1) * E], bno[:, 0:1], rstd[:],
                            ALU.subtract, ALU.mult), waits=[rct], tag=f"fln4{ci}_{j}")
                    add = p.op("vector", lambda e, b=xb: e.tensor_add(outt[:], ln4[:], b[:]),
                               waits=[xldt, outt_free], inc=(s_v, 1), tag=f"fadd{ci}_{j}")
                    x1_free[j % 2] = add
                    od = p.dma("gpsimd", d["out"][gt * 128:(gt + 1) * 128, :], outt[:],
                               waits=[add], sem=self.out_sem, tag=f"fo{ci}_{j}")
                    outt_free = od
                    ep_last = od
                dna_free = ep_last


def own_row_index(cfg, par):
    rows = []
    for tau in range(max(1, cfg.NT // 2)):
        base = 512 * tau + 256 * par
        rows.extend(range(base, base + 256))
    return np.array(rows[:cfg.NTOK])


def rope_tables(T, Dh=128, base=1e6):
    inv = 1.0 / base ** (np.arange(0, Dh, 2, dtype=np.float32) / Dh)
    f = np.arange(T, dtype=np.float32)[:, None] * inv[None, :]
    cos = np.concatenate([np.cos(f), np.cos(f)], -1).astype(np.float32)
    sin = np.concatenate([np.sin(f), np.sin(f)], -1).astype(np.float32)
    return cos, sin


def stage_inputs(cfg, inputs, core):
    import ml_dtypes
    b, par = core // 2, core % 2
    x = np.ascontiguousarray(np.asarray(inputs["L0_attn_x"][b], np.float32))
    own_rows = own_row_index(cfg, par)
    m = {}
    m["x_seq"] = np.ascontiguousarray(x[:cfg.MTOK])
    m["x_own"] = np.ascontiguousarray(x[own_rows])
    m["wq"] = np.ascontiguousarray(np.asarray(inputs["L0_attn_wq"], np.float32))
    m["wk"] = np.ascontiguousarray(np.asarray(inputs["L0_attn_wk"], np.float32))
    m["wv"] = np.ascontiguousarray(np.asarray(inputs["L0_attn_wv"], np.float32))
    m["wo"] = np.ascontiguousarray(np.asarray(inputs["L0_attn_wo"], np.float32))
    m["wg8"] = np.ascontiguousarray(np.asarray(inputs["L0_ffn_w_gate"], np.int8)[:, :cfg.INTER])
    m["wu8"] = np.ascontiguousarray(np.asarray(inputs["L0_ffn_w_up"], np.int8)[:, :cfg.INTER])
    m["wd8"] = np.ascontiguousarray(np.asarray(inputs["L0_ffn_w_down"], np.int8)[:cfg.INTER])
    m["sg"] = np.ascontiguousarray(
        np.asarray(inputs["L0_ffn_s_gate"], np.float32)[:cfg.INTER].reshape(cfg.OC, 128).T)
    m["su"] = np.ascontiguousarray(
        np.asarray(inputs["L0_ffn_s_up"], np.float32)[:cfg.INTER].reshape(cfg.OC, 128).T)
    m["sd"] = np.ascontiguousarray(
        np.asarray(inputs["L0_ffn_s_down"], np.float32).reshape(cfg.EC, 128).T)
    m["ident"] = np.eye(128, dtype=np.float32).astype(ml_dtypes.bfloat16)
    m["identf"] = np.eye(128, dtype=np.float32)
    m["epsb"] = np.full((128, 1), 1e-5, np.float32)
    masks = np.zeros((3, 128, 128), np.float32)
    triln = np.triu(np.full((128, 128), -1e9, np.float32), 1)
    if par == 0:
        masks[0] = triln
        masks[1] = -1e9
        masks[2] = -1e9
    else:
        masks[2] = triln
    m["masks"] = masks.astype(ml_dtypes.bfloat16)
    cos, sin = rope_tables(2048)
    sin_signed = sin.copy()
    sin_signed[:, :64] = -sin[:, :64]
    cos_rep = np.tile(cos, (1, 4))
    sin_rep = np.tile(sin_signed, (1, 4))
    m["cos_all"] = np.ascontiguousarray(cos_rep[:cfg.MTOK])
    m["sin_all"] = np.ascontiguousarray(sin_rep[:cfg.MTOK])
    m["cos_own"] = np.ascontiguousarray(cos_rep[own_rows])
    m["sin_own"] = np.ascontiguousarray(sin_rep[own_rows])
    return m


def kernel(**inputs):
    from concourse.bass_utils import run_bass_kernel_spmd
    cfg = Cfg()
    in_maps = [stage_inputs(cfg, inputs, c) for c in range(8)]
    b = Builder(cfg)
    nc = b.build()
    res = run_bass_kernel_spmd(nc, in_maps, core_ids=list(range(8)))
    x = np.asarray(inputs["L0_attn_x"], np.float32)
    out = np.empty_like(x)
    for c in range(8):
        bq, par = c // 2, c % 2
        out[bq][own_row_index(cfg, par)] = np.asarray(res.results[c]["out"],
                                                     np.float32)
    return out


# version 13
# speedup vs baseline: 1.0392x; 1.0392x over previous
import numpy as np
from contextlib import ExitStack
import concourse.bass as bass
import concourse.mybir as mybir

F32 = mybir.dt.float32
BF16 = mybir.dt.bfloat16
I8 = mybir.dt.int8
AX = mybir.AxisListType.X
ALU = mybir.AluOpType
ACTF = mybir.ActivationFunctionType

E = 2048
D = 128
SCALE = float(1.0 / np.sqrt(np.float32(D)))


def _base_ap(t):
    if isinstance(t, bass.AP):
        return t
    return t[:]


def APv(t, off, dims):
    a = _base_ap(t)
    return bass.AP(a.tensor, a.offset + off, [list(a.ap[0])] + dims)


def APd(t, off, dims):
    a = _base_ap(t)
    return bass.AP(a.tensor, a.offset + off, dims)


class Cfg:
    def __init__(self, nt=8, mt=16, fb=4, oc=64, ffn_tc=256, debug=False):
        self.debug = debug
        self.NT = nt
        self.MT = mt
        self.FB = fb
        self.OC = oc
        self.TC = ffn_tc
        self.EC = 16
        self.heads = 4 * fb
        self.NTOK = nt * 128
        self.MTOK = mt * 128
        self.INTER = oc * 128

    def Jtile(self, qt):
        return min(4 * (qt // 2) + (qt % 2) + 2, self.MT - 1)


class Sem:
    __slots__ = ("h", "n", "name")
    def __init__(self, h, name):
        self.h = h; self.n = 0; self.name = name
    @property
    def tok(self):
        return (self, self.n)


def flat(*toks):
    out = []
    for t in toks:
        if t is None:
            continue
        if isinstance(t, list):
            out.extend(x for x in flat(*t))
        else:
            out.append(t)
    return out


class Plan:
    def __init__(self):
        self.ops = []

    def op(self, engine, fn, waits=(), inc=None, tag="", is_dma=False):
        ws = flat(*waits)
        best = {}
        for (s, v) in ws:
            if v > 0 and best.get(s, 0) < v:
                best[s] = v
        ws = list(best.items())
        rec_inc = None
        if inc is not None:
            inc[0].n += inc[1]
            rec_inc = (inc[0], inc[1], inc[0].n)
        self.ops.append((engine, fn, ws, rec_inc, tag, is_dma))
        return None if inc is None else (inc[0], inc[0].n)

    def dma(self, engine, dst, src, waits=(), sem=None, tag=""):
        return self.op(engine, lambda e, d=dst, s=src: e.dma_start(d, s),
                       waits=waits, inc=(sem, 16), tag=tag, is_dma=True)

    def barrier(self, engine, waits, tag="bar"):
        ws = flat(*waits)
        best = {}
        for (s, v) in ws:
            if v > 0 and best.get(s, 0) < v:
                best[s] = v
        self.ops.append((engine, None, list(best.items()), None, tag, False))

    def emit(self, block, chain_sems=None):
        per = {}
        for rec in self.ops:
            per.setdefault(rec[0], []).append(rec)
        for name in ("sync", "tensor", "vector", "scalar", "gpsimd"):
            if name not in per:
                continue
            records = per[name]
            chain = chain_sems.get(name) if chain_sems else None
            def body(eng, records=records, chain=chain):
                seen = {}
                cnt = 0
                last_inc = None
                for (_, fn, waits, inc, tag, is_dma) in records:
                    for (s, v) in waits:
                        if seen.get(s, 0) < v:
                            eng.wait_ge(s.h, v)
                            seen[s] = v
                    if fn is None:
                        continue
                    if chain is not None:
                        if cnt > 0:
                            eng.wait_ge(chain, cnt)
                        if last_inc is not None:
                            eng.wait_ge(*last_inc)
                    ins = fn(eng)
                    if ins is not None:
                        if inc is not None:
                            ins.then_inc(inc[0].h, inc[1])
                            if chain is not None and not is_dma:
                                last_inc = (inc[0].h, inc[2])
                        elif chain is not None and not is_dma:
                            ins.then_inc(chain, 1)
                            cnt += 1
                    else:
                        assert inc is None, f"op {tag} must return an instruction"
            getattr(block, name)(body)


class Builder:
    def __init__(self, cfg: Cfg, simsafe=False):
        self.cfg = cfg
        self.simsafe = simsafe
        self.nc = bass.Bass("TRN2", target_bir_lowering=False,
                            detect_race_conditions=simsafe)
        self.p = Plan()
        self.stack = ExitStack()
        self._semid = 0

    def sem(self, name):
        if not hasattr(self, "_sems"):
            self._sems = {}
        if name not in self._sems:
            self._semid += 1
            h = self.stack.enter_context(self.nc.semaphore(f"s{self._semid}_{name}"))
            self._sems[name] = Sem(h, name)
        return self._sems[name]

    def build(self):
        cfg, nc, p = self.cfg, self.nc, self.p
        NTOK, MTOK, INTER = cfg.NTOK, cfg.MTOK, cfg.INTER
        HD, EC, OC = cfg.heads, cfg.EC, cfg.OC

        d = {}
        d["x_seq"] = nc.dram_tensor("x_seq", [MTOK, E], F32, kind="ExternalInput").ap()
        d["x_own"] = nc.dram_tensor("x_own", [NTOK, E], F32, kind="ExternalInput").ap()
        for w in ("wq", "wk", "wv", "wo"):
            d[w] = nc.dram_tensor(w, [E, E], F32, kind="ExternalInput").ap()
        d["wg8"] = nc.dram_tensor("wg8", [E, INTER], I8, kind="ExternalInput").ap()
        d["wu8"] = nc.dram_tensor("wu8", [E, INTER], I8, kind="ExternalInput").ap()
        d["wd8"] = nc.dram_tensor("wd8", [INTER, E], I8, kind="ExternalInput").ap()
        d["sg"] = nc.dram_tensor("sg", [128, OC], F32, kind="ExternalInput").ap()
        d["su"] = nc.dram_tensor("su", [128, OC], F32, kind="ExternalInput").ap()
        d["sd"] = nc.dram_tensor("sd", [128, EC], F32, kind="ExternalInput").ap()
        d["ident"] = nc.dram_tensor("ident", [128, 128], BF16, kind="ExternalInput").ap()
        d["identf"] = nc.dram_tensor("identf", [128, 128], F32, kind="ExternalInput").ap()
        d["epsb"] = nc.dram_tensor("epsb", [128, 1], F32, kind="ExternalInput").ap()
        d["masks"] = nc.dram_tensor("masks", [3, 128, 128], BF16, kind="ExternalInput").ap()
        d["cos_own"] = nc.dram_tensor("cos_own", [NTOK, 512], F32, kind="ExternalInput").ap()
        d["sin_own"] = nc.dram_tensor("sin_own", [NTOK, 512], F32, kind="ExternalInput").ap()
        d["cos_all"] = nc.dram_tensor("cos_all", [MTOK, 512], F32, kind="ExternalInput").ap()
        d["sin_all"] = nc.dram_tensor("sin_all", [MTOK, 512], F32, kind="ExternalInput").ap()
        d["out"] = nc.dram_tensor("out", [NTOK, E], F32, kind="ExternalOutput").ap()

        kd = "ExternalOutput" if cfg.debug else "Internal"
        q_hi = nc.dram_tensor("q_scr_hi", [HD, 128, NTOK], BF16, kind=kd).ap()
        q_lo = nc.dram_tensor("q_scr_lo", [HD, 128, NTOK], BF16, kind=kd).ap()
        ao_scr = nc.dram_tensor("ao_scr", [HD, 128, NTOK], F32, kind=kd).ap()
        o2_scr = nc.dram_tensor("o2_scr", [NTOK, E], F32, kind=kd).ap()
        x1_scr = nc.dram_tensor("x1_scr", [NTOK, E], F32, kind=kd).ap()
        lnf_hi = nc.dram_tensor("lnf_hi", [EC, 128, NTOK], BF16, kind=kd).ap()
        lnf_lo = nc.dram_tensor("lnf_lo", [EC, 128, NTOK], BF16, kind=kd).ap()
        wg_scr = nc.dram_tensor("wg_scr", [OC, EC, 128, 128], BF16, kind=kd).ap()
        wu_scr = nc.dram_tensor("wu_scr", [OC, EC, 128, 128], BF16, kind=kd).ap()
        wd_scr = nc.dram_tensor("wd_scr", [EC, OC, 128, 128], BF16, kind=kd).ap()
        self.d = d

        sb = self.stack.enter_context
        self.ident = sb(nc.sbuf_tensor("c_ident", [128, 128], BF16))
        self.identf = sb(nc.sbuf_tensor("c_identf", [128, 128], F32))
        self.epsb = sb(nc.sbuf_tensor("c_eps", [128, 1], F32))
        self.masks = sb(nc.sbuf_tensor("c_masks", [128, 3 * 128], BF16))
        self.sg_sb = sb(nc.sbuf_tensor("c_sg", [128, OC], F32))
        self.su_sb = sb(nc.sbuf_tensor("c_su", [128, OC], F32))
        self.sd_sb = sb(nc.sbuf_tensor("c_sd", [128, EC], F32))
        ld0 = self.sem("ld0")
        p.dma("sync", self.ident[:], d["ident"][:], sem=ld0)
        p.dma("sync", self.identf[:], d["identf"][:], sem=ld0)
        p.dma("sync", self.epsb[:], d["epsb"][:], sem=ld0)
        p.dma("sync", self.masks[:],
              APd(d["masks"], 0, [[128, 128], [128 * 128, 3], [1, 128]]), sem=ld0)
        p.dma("sync", self.sg_sb[:], d["sg"][:], sem=ld0)
        p.dma("sync", self.su_sb[:], d["su"][:], sem=ld0)
        p.dma("sync", self.sd_sb[:], d["sd"][:], sem=ld0)
        self.consts_ready = ld0.tok
        self.out_sem = self.sem("outd")

        KSEG = 4 * 128 * NTOK
        VSEG = 4 * (cfg.NT) * 128 * 256
        FBSEG = 2 * KSEG + VSEG
        kv_in = [nc.dram_tensor(f"kvin{f}", [FBSEG], BF16).ap()
                 for f in range(cfg.FB)]
        kv_out = [nc.dram_tensor(f"kvout{f}", [2, FBSEG], BF16).ap()
                  for f in range(cfg.FB)]
        self.kv_geom = (KSEG, VSEG, FBSEG)
        lno_hi = nc.dram_tensor("lno_hi", [EC, 128, NTOK], BF16).ap()
        lno_lo = nc.dram_tensor("lno_lo", [EC, 128, NTOK], BF16).ap()
        with ExitStack() as ph:
            ln = self.ln_phase(ph, d["x_own"], cfg.NT, "own", self.consts_ready)
            s_lno = self.sem("lno")
            for half, (slab, dst) in enumerate([(ln[0], lno_hi), (ln[1], lno_lo)]):
                for ec in range(EC):
                    p.dma("gpsimd", APd(dst, ec * 128 * NTOK, [[NTOK, 128], [1, NTOK]]),
                          slab[:, ec * NTOK:(ec + 1) * NTOK],
                          waits=[ln[3]], sem=s_lno, tag=f"lno{half}_{ec}")
            lno_done = s_lno.tok
            self.q_done = self.qkv_phase(ph, ln, d["wq"], cfg.NT, "own",
                                         (q_hi, q_lo), "q")
        fc_stack = ExitStack()
        fc_chunks = self.ffn_weight_cache_mk(fc_stack, d, wg_scr, wu_scr, wd_scr,
                                             flat(self.q_done, lno_done))
        self.attn_merged(d, (lno_hi, lno_lo, NTOK, flat(lno_done, self.q_done)),
                         kv_in, kv_out, q_hi, q_lo, ao_scr, fc_chunks)
        for fn in fc_chunks:
            fn()
        fc_chunks.clear()
        self.fc_finalize()
        fc_stack.close()
        self.oproj_phase(d, ao_scr, o2_scr, x1_scr, lnf_hi, lnf_lo)
        self.ffn_phase(d, lnf_hi, lnf_lo, wg_scr, wu_scr, wd_scr, x1_scr)

        p.barrier("sync", [self.out_sem.tok], tag="final")
        chain_sems = None
        if self.simsafe:
            chain_sems = {}
            for name in ("sync", "tensor", "vector", "scalar", "gpsimd"):
                chain_sems[name] = self.stack.enter_context(
                    nc.semaphore(f"chain_{name}"))
        with nc.Block() as block:
            p.emit(block, chain_sems)
        self.stack.close()
        return nc


    def ln_stats(self, bns, bno, src_fn, width, waits, s_v, tag):
        p = self.p
        nch = (width + 511) // 512
        st = None
        for c in range(nch):
            st = p.op("vector", lambda e, c=c: e.bn_stats(bns[:, c * 6:(c + 1) * 6], src_fn(c)),
                      waits=waits if c == 0 else (),
                      inc=(s_v, 1) if c == nch - 1 else None, tag=f"{tag}bs{c}")
        return p.op("vector", lambda e, n=nch: e.bn_aggr(bno[:], bns[:, :6 * n]),
                    waits=[st], inc=(s_v, 1), tag=f"{tag}ba")

    def ln_phase(self, ph, x_dram, ntiles, name, start_tok, to_hbm=None,
                 as_chunks=False):
        nc, p, cfg = self.nc, self.p, self.cfg
        ntok = ntiles * 128
        EC = cfg.EC
        if to_hbm is None:
            slab_hi = ph.enter_context(nc.sbuf_tensor(f"ln_{name}_hi", [128, EC * ntok], BF16))
            slab_lo = ph.enter_context(nc.sbuf_tensor(f"ln_{name}_lo", [128, EC * ntok], BF16))
        sc = ExitStack() if not as_chunks else None
        with (sc if sc is not None else ExitStack()) as _sc:
            s = (_sc.enter_context if not as_chunks else ph.enter_context)
            if to_hbm is not None:
                stg = [s(nc.sbuf_tensor(f"lnstg{name}{i}", [128, 512], BF16))
                       for i in range(2)]
                s_sd = [self.sem(f"ln_sd{i}") for i in range(2)]
                stg_free = [start_tok, start_tok]
            xt = [s(nc.sbuf_tensor(f"lnx{name}{i}", [128, E], F32)) for i in range(2)]
            ln32 = s(nc.sbuf_tensor(f"ln32{name}", [128, E], F32))
            lhi = s(nc.sbuf_tensor(f"lnhi{name}", [128, E], BF16))
            llo = s(nc.sbuf_tensor(f"lnlo{name}", [128, E], BF16))
            bns = s(nc.sbuf_tensor(f"bns{name}", [128, 24], F32))
            bno = s(nc.sbuf_tensor(f"bno{name}", [128, 2], F32))
            std = s(nc.sbuf_tensor(f"std{name}", [128, 1], F32))
            rstd = s(nc.sbuf_tensor(f"rstd{name}", [128, 1], F32))
            trp = [s(nc.psum_tensor(f"lntr{name}{i}", [128, 1024], BF16)) for i in range(2)]
            s_ld = [self.sem(f"ln_ld{i}") for i in range(2)]
            s_v = self.sem("ln_v")
            s_s = self.sem("ln_s")
            s_t = self.sem("ln_t")
            s_e = self.sem("ln_e")
            xt_free = [start_tok, start_tok]
            bank_free = [start_tok, start_tok]

            def emit_tile(i):
                ldt = p.dma("sync", xt[i % 2][:], x_dram[i * 128:(i + 1) * 128, :],
                            waits=[xt_free[i % 2]], sem=s_ld[i % 2], tag=f"ln{name}x{i}")
                stats = self.ln_stats(bns, bno,
                                      lambda c, b=xt[i % 2]: b[:, c * 512:(c + 1) * 512],
                                      E, [ldt, start_tok], s_v, f"ln{name}{i}")
                sq = p.op("scalar", lambda e: e.activation(std[:], bno[:, 1:2], ACTF.Sqrt,
                                                           bias=self.epsb[:]),
                          waits=[stats, start_tok], inc=(s_s, 1), tag=f"ln{name}sq{i}")
                rct = p.op("vector", lambda e: e.reciprocal(rstd[:], std[:]), waits=[sq],
                           inc=(s_v, 1), tag=f"ln{name}rc{i}")
                apl = p.op("vector", lambda e, b=xt[i % 2]: e.tensor_scalar(
                        ln32[:], b[:], bno[:, 0:1], rstd[:], ALU.subtract, ALU.mult),
                     waits=[rct], inc=(s_v, 1), tag=f"ln{name}ap{i}")
                xt_free[i % 2] = apl
                p.op("vector", lambda e: e.tensor_copy(lhi[:], ln32[:]), tag=f"ln{name}h{i}")
                spl = p.op("vector", lambda e: e.tensor_sub(llo[:], ln32[:], lhi[:]),
                           inc=(s_v, 1), tag=f"ln{name}l{i}")
                gidx = 0
                if to_hbm is None:
                    dests = [(lhi, slab_hi), (llo, slab_lo)]
                else:
                    dests = [(lhi, to_hbm[0]), (llo, to_hbm[1])]
                for half, (src, dslab) in enumerate(dests):
                    for g in range(EC // 4):
                        bi = (i * 2 * (EC // 4) + gidx) % 2
                        bank = trp[bi]
                        bf = bank_free[bi]
                        gidx += 1
                        tt = None
                        for j in range(4):
                            ec = g * 4 + j
                            tt = p.op("tensor", lambda e, s2=src, b=bank, ec=ec, j=j:
                                      e.transpose(b[:, j * 128:(j + 1) * 128],
                                                  s2[:, ec * 128:(ec + 1) * 128],
                                                  self.ident[:]),
                                      waits=[spl, bf] if j == 0 else (),
                                      inc=(s_t, 1) if j == 3 else None,
                                      tag=f"ln{name}t{i}_{half}_{g}_{j}")
                        if to_hbm is None:
                            base = (g * 4) * ntok + i * 128
                            ev = p.op("scalar",
                                      lambda e, b=bank, ds=dslab, base=base, ntok=ntok:
                                      e.copy(APv(ds, base, [[ntok, 4], [1, 128]]),
                                             b[:, :512]),
                                      waits=[tt], inc=(s_e, 1),
                                      tag=f"ln{name}e{i}_{half}_{g}")
                            bank_free[bi] = ev
                        else:
                            sb2 = stg[bi]
                            ev = p.op("scalar",
                                      lambda e, b=bank, sb2=sb2:
                                      e.copy(sb2[:], b[:, :512]),
                                      waits=[tt, stg_free[bi]], inc=(s_e, 1),
                                      tag=f"ln{name}e{i}_{half}_{g}")
                            dmat = p.dma("gpsimd",
                                         APd(dslab, (g * 4) * 128 * ntok + i * 128,
                                             [[ntok, 128], [ntok * 128, 4], [1, 128]]),
                                         APv(sb2, 0, [[128, 4], [1, 128]]),
                                         waits=[ev], sem=s_sd[bi],
                                         tag=f"ln{name}sd{i}_{half}_{g}")
                            stg_free[bi] = dmat
                            bank_free[bi] = ev
            if as_chunks:
                chunks = [lambda i=i: emit_tile(i) for i in range(ntiles)]

                def done_fn():
                    return [(s_sd[0], s_sd[0].n), (s_sd[1], s_sd[1].n)]
                return (chunks, done_fn)
            for i in range(ntiles):
                emit_tile(i)
            if to_hbm is not None:
                ln_done = [(s_sd[0], s_sd[0].n), (s_sd[1], s_sd[1].n)]
                return (to_hbm[0], to_hbm[1], ntok, ln_done)
            ln_done = (s_e, s_e.n)
        return (slab_hi, slab_lo, ntok, ln_done)

    def wT_prep(self, o, w_dram, fb, tag):
        nc, p, cfg = self.nc, self.p, self.cfg
        EC = cfg.EC
        ev = None
        for sub in range(4):
            f0 = fb * 512 + sub * 128
            buf = o["wrow"][sub % 2]
            w = [o["wrow_free"][sub % 2]]
            if sub == 0:
                w.append(o.get("slab_free"))
            ldt = p.dma("sync", buf[:], w_dram[f0:f0 + 128, :], waits=w,
                        sem=o["s_wld"][sub % 2], tag=f"{tag}wld{fb}_{sub}")
            p.op("vector", lambda e, b=buf: e.tensor_copy(o["wrhi"][:], b[:]),
                 waits=[ldt, o.get("vec_start"), o.get("wsplit_free")],
                 tag=f"{tag}wh{fb}{sub}")
            spl = p.op("vector", lambda e, b=buf: e.tensor_sub(o["wrlo"][:], b[:], o["wrhi"][:]),
                       inc=(o["s_wv"], 1), tag=f"{tag}wl{fb}{sub}")
            o["wrow_free"][sub % 2] = spl
            if o.get("wrow_single"):
                o["wrow_free"][(sub + 1) % 2] = spl
            o["vec_start"] = None
            gidx = 0
            last_tt = [None, None]
            for half, (src, dslab) in enumerate([(o["wrhi"], o["wT"][0]),
                                                 (o["wrlo"], o["wT"][1])]):
                for g in range(EC // 4):
                    bi = gidx % 2
                    bank = o["trp"][bi]
                    bf = o["bank_free"][bi]
                    gidx += 1
                    tt = None
                    for j in range(4):
                        ec = g * 4 + j
                        tt = p.op("tensor", lambda e, s2=src, b=bank, ec=ec, j=j:
                                  e.transpose(b[:, j * 128:(j + 1) * 128],
                                              s2[:, ec * 128:(ec + 1) * 128],
                                              self.ident[:]),
                                  waits=[spl, bf] if j == 0 else (),
                                  inc=(o["s_wt"], 1) if (j == 3 or g * 4 + j == EC - 1)
                                      and j == 3 else None,
                                  tag=f"{tag}wt{fb}{sub}{half}{g}{j}")
                    base = (g * 4) * 512 + sub * 128
                    ev = p.op("scalar", lambda e, b=bank, ds=dslab, base=base:
                              e.copy(APv(ds, base, [[512, 4], [1, 128]]), b[:, :512]),
                              waits=[tt], inc=(o["s_we"], 1),
                              tag=f"{tag}we{fb}{sub}{half}{g}")
                    o["bank_free"][bi] = ev
                    if o.get("trp_single"):
                        o["bank_free"][1 - bi] = ev
                    last_tt[half] = tt
            o["wsplit_free"] = [last_tt[0], last_tt[1]]
        return ev

    def qkv_phase(self, ph, ln, w_dram, ntiles, rope, out, kind, prev=None,
                  interleave=None):
        nc, p, cfg = self.nc, self.p, self.cfg
        slab_hi, slab_lo, ntok, ln_done = ln
        ln_done = flat(ln_done, prev)
        EC, FB, MT = cfg.EC, cfg.FB, cfg.MT
        with ExitStack() as sc:
            s = sc.enter_context
            o = {}
            o["wrow"] = [s(nc.sbuf_tensor(f"{kind}wr{i}", [128, E], F32)) for i in range(2)]
            o["wrhi"] = s(nc.sbuf_tensor(f"{kind}wrh", [128, E], BF16))
            o["wrlo"] = s(nc.sbuf_tensor(f"{kind}wrl", [128, E], BF16))
            o["wT"] = [s(nc.sbuf_tensor(f"{kind}wT{h}", [128, EC * 512], BF16))
                       for h in range(2)]
            o["trp"] = [s(nc.psum_tensor(f"{kind}tp{i}", [128, 1024], BF16)) for i in range(2)]
            mm = [s(nc.psum_tensor(f"{kind}mm{i}", [128, 512], F32)) for i in range(2)]
            q32 = s(nc.sbuf_tensor(f"{kind}q32", [128, 512], F32))
            qc = s(nc.sbuf_tensor(f"{kind}qc", [128, 512], F32))
            qhib2 = [s(nc.sbuf_tensor(f"{kind}hib{i}", [128, 512], BF16)) for i in range(2)]
            qlob2 = [s(nc.sbuf_tensor(f"{kind}lob{i}", [128, 512], BF16)) for i in range(2)]
            ebuf = [s(nc.sbuf_tensor(f"{kind}eb{i}", [128, 512], BF16)) for i in range(2)]
            if rope:
                costl = [s(nc.sbuf_tensor(f"{kind}cs{i}", [128, 512], F32)) for i in range(2)]
                sintl = [s(nc.sbuf_tensor(f"{kind}sn{i}", [128, 512], F32)) for i in range(2)]
            else:
                vint = s(nc.sbuf_tensor("vint", [128, 1024], BF16))
            o["s_wld"] = [self.sem(f"pj_wld{i}") for i in range(2)]
            o["s_wv"] = self.sem("pj_wv")
            o["s_wt"] = self.sem("pj_wt")
            o["s_we"] = self.sem("pj_we")
            s_mm = self.sem("pj_mm")
            s_rv = self.sem("pj_rv")
            s_tt = self.sem("pj_tt")
            s_ee = self.sem("pj_ee")
            s_do = self.sem("pj_do")
            s_dos = [self.sem(f"pj_dos{i}") for i in range(2)]
            s_cs = [self.sem(f"pj_cs{i}") for i in range(2)]
            o["wrow_free"] = [ln_done, ln_done]
            o["bank_free"] = [ln_done, ln_done]
            o["vec_start"] = ln_done
            o["slab_free"] = None

            if rope:
                cos_d = self.d["cos_own" if rope == "own" else "cos_all"]
                sin_d = self.d["sin_own" if rope == "own" else "sin_all"]

            mm_free = [ln_done, ln_done]
            cs_free = [ln_done, ln_done]
            ebuf_free = [ln_done, ln_done]
            vint_free = ln_done
            pending_qkT = None
            qsplit_free = [None, None]
            for fb in range(FB):
                wready = self.wT_prep(o, w_dram, fb, kind)
                last_mm_of_fb = None
                for i in range(ntiles):
                    bank = mm[i % 2]
                    waits = [wready, ln_done, mm_free[i % 2]]
                    for ec in range(EC):
                        c0 = ec * ntok + i * 128
                        p.op("tensor", lambda e, b=bank, c0=c0, ec=ec, wt=o["wT"][0]:
                             e.matmul(b[:], slab_hi[:, c0:c0 + 128],
                                      wt[:, ec * 512:(ec + 1) * 512],
                                      start=(ec == 0), stop=False),
                             waits=waits if ec == 0 else (), tag=f"{kind}m{fb}_{i}_{ec}a")
                        p.op("tensor", lambda e, b=bank, c0=c0, ec=ec, wt=o["wT"][1]:
                             e.matmul(b[:], slab_hi[:, c0:c0 + 128],
                                      wt[:, ec * 512:(ec + 1) * 512],
                                      start=False, stop=False), tag=f"{kind}m{fb}_{i}_{ec}b")
                        mmt = p.op("tensor", lambda e, b=bank, c0=c0, ec=ec, wt=o["wT"][0]:
                             e.matmul(b[:], slab_lo[:, c0:c0 + 128],
                                      wt[:, ec * 512:(ec + 1) * 512],
                                      start=False, stop=(ec == EC - 1)),
                             inc=(s_mm, 1) if ec == EC - 1 else None,
                             tag=f"{kind}m{fb}_{i}_{ec}c")
                    last_mm_of_fb = mmt
                    if rope:
                        cb, sb_ = costl[i % 2], sintl[i % 2]
                        ct = p.dma("sync", cb[:], cos_d[i * 128:(i + 1) * 128, :],
                                   waits=[cs_free[i % 2]], sem=s_cs[i % 2], tag=f"{kind}c{fb}{i}")
                        st_ = p.dma("sync", sb_[:], sin_d[i * 128:(i + 1) * 128, :],
                                    sem=s_cs[i % 2], tag=f"{kind}s{fb}{i}")
                        p.op("vector", lambda e, b=bank, cb=cb: e.tensor_mul(qc[:], b[:], cb[:]),
                             waits=[mmt, ct, st_], tag=f"{kind}r1{fb}_{i}")
                        p.op("vector", lambda e, b=bank, sb2=sb_:
                             e.tensor_mul(q32[:],
                                          APd(b, 64, [b[:].ap[0], [128, 4], [-64, 2], [1, 64]]),
                                          sb2[:]),
                             tag=f"{kind}r2{fb}_{i}")
                        ropet = p.op("vector", lambda e: e.tensor_add(q32[:], q32[:], qc[:]),
                                     inc=(s_rv, 1), tag=f"{kind}r3{fb}_{i}")
                        mm_free[i % 2] = ropet
                        cs_free[i % 2] = ropet
                        qp = i % 2
                        qhib, qlob = qhib2[qp], qlob2[qp]
                        p.op("vector", lambda e, qhib=qhib: e.tensor_copy(qhib[:], q32[:]),
                             waits=[qsplit_free[qp]], tag=f"{kind}p1{fb}_{i}")
                        spl = p.op("vector",
                                   lambda e, qhib=qhib, qlob=qlob:
                                   e.tensor_sub(qlob[:], q32[:], qhib[:]),
                                   inc=(s_rv, 1), tag=f"{kind}p2{fb}_{i}")

                        def emit_qkT(fb=fb, i=i, spl=spl, qhib=qhib, qlob=qlob, qp=qp):
                            last_tt2 = None
                            for half, src in enumerate([qhib, qlob]):
                                bank_t = o["trp"][half]
                                bf = o["bank_free"][half]
                                tt = None
                                for hh in range(4):
                                    tt = p.op("tensor", lambda e, s2=src, b=bank_t, hh=hh:
                                              e.transpose(b[:, hh * 128:(hh + 1) * 128],
                                                          s2[:, hh * 128:(hh + 1) * 128],
                                                          self.ident[:]),
                                              waits=[spl, bf] if hh == 0 else (),
                                              inc=(s_tt, 1) if hh == 3 else None,
                                              tag=f"{kind}tt{fb}{i}{half}{hh}")
                                eb = ebuf[half]
                                ev = p.op("scalar", lambda e, b=bank_t, eb=eb:
                                          e.copy(eb[:], b[:, :512]),
                                          waits=[tt, ebuf_free[half]], inc=(s_ee, 1),
                                          tag=f"{kind}te{fb}{i}{half}")
                                o["bank_free"][half] = ev
                                dst = out[half]
                                dmat = p.dma("gpsimd",
                                             APd(dst, (fb * 4) * 128 * ntok + i * 128,
                                                 [[ntok, 128], [128 * ntok, 4], [1, 128]]),
                                             APv(eb, 0, [[128, 4], [1, 128]]),
                                             waits=[ev], sem=s_dos[half],
                                             tag=f"{kind}d{fb}_{i}_{half}")
                                ebuf_free[half] = dmat
                                last_tt2 = tt
                            qsplit_free[qp] = last_tt2

                        if pending_qkT is not None:
                            pending_qkT()
                        pending_qkT = emit_qkT
                        if interleave:
                            interleave.pop(0)()
                    else:
                        vw = [mmt, vint_free]
                        p.op("vector", lambda e, b=bank:
                             e.tensor_copy(APv(vint, 0, [[256, 4], [1, 128]]), b[:]),
                             waits=vw, tag=f"vp1{fb}_{i}")
                        spl = p.op("vector", lambda e, b=bank:
                                   e.tensor_sub(APv(vint, 128, [[256, 4], [1, 128]]), b[:],
                                                APv(vint, 0, [[256, 4], [1, 128]])),
                                   inc=(s_rv, 1), tag=f"vp2{fb}_{i}")
                        mm_free[i % 2] = spl
                        vint_free = p.dma("gpsimd",
                              APd(out[0], ((fb * 4) * MT + i) * 128 * 256,
                                  [[256, 128], [MT * 128 * 256, 4], [1, 256]]),
                              vint[:], waits=[spl], sem=s_do, tag=f"vd{fb}_{i}")
                if pending_qkT is not None:
                    pending_qkT()
                    pending_qkT = None
                o["slab_free"] = last_mm_of_fb
            while interleave:
                interleave.pop(0)()
            return [(s_do, s_do.n), (s_dos[0], s_dos[0].n), (s_dos[1], s_dos[1].n)]

    def kv_setup(self, sc, ln, d, kv_in, kv_out, prev):
        nc, p, cfg = self.nc, self.p, self.cfg
        EC = cfg.EC
        s = sc.enter_context
        slab_hi, slab_lo, ntok, ln_done = ln
        st = {}
        o = {}
        wr0 = s(nc.sbuf_tensor("kvwr0", [128, E], F32))
        o["wrow"] = [wr0, wr0]
        o["wrow_single"] = True
        o["wrhi"] = s(nc.sbuf_tensor("kvwrh", [128, E], BF16))
        o["wrlo"] = s(nc.sbuf_tensor("kvwrl", [128, E], BF16))
        o["wT"] = [s(nc.sbuf_tensor(f"kvwT{h}", [128, EC * 512], BF16)) for h in range(2)]
        kvtp0 = s(nc.psum_tensor("kvtp0", [128, 1024], BF16))
        o["trp"] = [kvtp0, kvtp0]
        o["trp_single"] = True
        o["s_wld"] = [self.sem(f"pj_wld{i}") for i in range(2)]
        o["s_wv"] = self.sem("pj_wv")
        o["s_wt"] = self.sem("pj_wt")
        o["s_we"] = self.sem("pj_we")
        start = flat(prev, ln_done)
        o["wrow_free"] = [start, start]
        o["bank_free"] = [start, start]
        o["vec_start"] = start
        o["slab_free"] = None
        st["o"] = o
        st["ln"] = (slab_hi, slab_lo, ntok)
        st["lnb_hi"] = [s(nc.sbuf_tensor(f"kvlh{i}", [128, EC * 128], BF16))
                        for i in range(2)]
        st["lnb_lo"] = [s(nc.sbuf_tensor(f"kvll{i}", [128, EC * 128], BF16))
                        for i in range(2)]
        st["s_lnb"] = [self.sem(f"pj_lnb{i}") for i in range(2)]
        st["lnb_free"] = [start, start]
        st["mm"] = [s(nc.psum_tensor(f"kvmm{i}", [128, 512], F32)) for i in range(2)]
        st["q32"] = s(nc.sbuf_tensor("kvq32", [128, 512], F32))
        st["qc"] = s(nc.sbuf_tensor("kvqc", [128, 512], F32))
        st["qhib"] = [s(nc.sbuf_tensor(f"kvhib{i}", [128, 512], BF16)) for i in range(2)]
        st["qlob"] = [s(nc.sbuf_tensor(f"kvlob{i}", [128, 512], BF16)) for i in range(2)]
        cs0 = s(nc.sbuf_tensor("kvcs0", [128, 512], F32))
        sn0 = s(nc.sbuf_tensor("kvsn0", [128, 512], F32))
        st["cs"] = [cs0, cs0]
        st["sn"] = [sn0, sn0]
        st["ebuf"] = [s(nc.sbuf_tensor(f"kveb{i}", [128, 512], BF16)) for i in range(2)]
        st["vint"] = s(nc.sbuf_tensor("kvvint", [128, 1024], BF16))
        st["s_mm"] = self.sem("pj_mm")
        st["s_rv"] = self.sem("pj_rv")
        st["s_tt"] = self.sem("pj_tt")
        st["s_ee"] = self.sem("pj_ee")
        st["s_do"] = self.sem("pj_do")
        st["s_dos"] = [self.sem(f"pj_dos{i}") for i in range(2)]
        st["s_cc"] = self.sem("pj_cc")
        st["mm_free"] = [start, start]
        st["cs_free"] = [start, start]
        st["ebuf_free"] = [start, start]
        st["vint_free"] = start
        st["pending"] = None
        st["qsplit_free"] = [start, start]
        st["d"] = d
        st["kv_in"] = kv_in
        st["kv_out"] = kv_out
        st["fbdone"] = {}
        return st

    def kv_chunks(self, st, fb):
        nc, p, cfg = self.nc, self.p, self.cfg
        EC, NT = cfg.EC, cfg.NT
        o = st["o"]
        slab_hi, slab_lo, ntok = st["ln"]
        KSEG, VSEG, FBSEG = self.kv_geom
        kin = st["kv_in"][fb]
        chunks = []
        store_toks = []

        def tile(kind, fb, i):
            wready = st["holder"]["wready"]
            bank = st["mm"][i % 2]
            sl = i % 2
            lnbh, lnbl = st["lnb_hi"][sl], st["lnb_lo"][sl]
            lnt = None
            for half, (scr, buf) in enumerate([(slab_hi, lnbh), (slab_lo, lnbl)]):
                lnt = p.dma("sync", buf[:],
                            APd(scr, i * 128, [[ntok, 128], [ntok * 128, EC], [1, 128]]),
                            waits=[st["lnb_free"][sl]] if half == 0 else (),
                            sem=st["s_lnb"][sl], tag=f"{kind}ln{fb}_{i}_{half}")
            waits = [wready, st["mm_free"][i % 2], lnt]
            mmt = None
            for ec in range(EC):
                c0 = ec * 128
                p.op("tensor", lambda e, b=bank, c0=c0, ec=ec, x=lnbh, wt=o["wT"][0]:
                     e.matmul(b[:], x[:, c0:c0 + 128],
                              wt[:, ec * 512:(ec + 1) * 512],
                              start=(ec == 0), stop=False),
                     waits=waits if ec == 0 else (), tag=f"{kind}m{fb}_{i}_{ec}a")
                p.op("tensor", lambda e, b=bank, c0=c0, ec=ec, x=lnbh, wt=o["wT"][1]:
                     e.matmul(b[:], x[:, c0:c0 + 128],
                              wt[:, ec * 512:(ec + 1) * 512],
                              start=False, stop=False), tag=f"{kind}m{fb}_{i}_{ec}b")
                mmt = p.op("tensor", lambda e, b=bank, c0=c0, ec=ec, x=lnbl, wt=o["wT"][0]:
                     e.matmul(b[:], x[:, c0:c0 + 128],
                              wt[:, ec * 512:(ec + 1) * 512],
                              start=False, stop=(ec == EC - 1)),
                     inc=(st["s_mm"], 1) if ec == EC - 1 else None,
                     tag=f"{kind}m{fb}_{i}_{ec}c")
            st["lnb_free"][sl] = mmt
            if kind == "k":
                cb, sb_ = st["cs"][i % 2], st["sn"][i % 2]
                ct = p.dma("sync", cb[:], self.d["cos_own"][i * 128:(i + 1) * 128, :],
                           waits=[st["cs_free"][0], st["cs_free"][1]],
                           sem=self.sem(f"pj_cs{i % 2}"), tag=f"kc{fb}{i}")
                st_2 = p.dma("sync", sb_[:], self.d["sin_own"][i * 128:(i + 1) * 128, :],
                             sem=self.sem(f"pj_cs{i % 2}"), tag=f"ks{fb}{i}")
                p.op("vector", lambda e, b=bank, cb=cb:
                     e.tensor_mul(st["qc"][:], b[:], cb[:]),
                     waits=[mmt, ct, st_2], tag=f"kr1{fb}_{i}")
                p.op("vector", lambda e, b=bank, sb2=sb_:
                     e.tensor_mul(st["q32"][:],
                                  APd(b, 64, [b[:].ap[0], [128, 4], [-64, 2], [1, 64]]),
                                  sb2[:]),
                     tag=f"kr2{fb}_{i}")
                ropet = p.op("vector", lambda e: e.tensor_add(st["q32"][:], st["q32"][:],
                                                              st["qc"][:]),
                             inc=(st["s_rv"], 1), tag=f"kr3{fb}_{i}")
                st["mm_free"][i % 2] = ropet
                st["cs_free"][i % 2] = ropet
                qp = i % 2
                qhib, qlob = st["qhib"][qp], st["qlob"][qp]
                p.op("vector", lambda e, qhib=qhib: e.tensor_copy(qhib[:], st["q32"][:]),
                     waits=[st["qsplit_free"][qp]], tag=f"kp1{fb}_{i}")
                spl = p.op("vector", lambda e, qhib=qhib, qlob=qlob:
                           e.tensor_sub(qlob[:], st["q32"][:], qhib[:]),
                           inc=(st["s_rv"], 1), tag=f"kp2{fb}_{i}")

                def emit_T(fb=fb, i=i, spl=spl, qhib=qhib, qlob=qlob, qp=qp):
                    last_tt2 = None
                    for half, src in enumerate([qhib, qlob]):
                        bank_t = o["trp"][half]
                        bf = o["bank_free"][half]
                        tt = None
                        for hh in range(4):
                            tt = p.op("tensor", lambda e, s2=src, b=bank_t, hh=hh:
                                      e.transpose(b[:, hh * 128:(hh + 1) * 128],
                                                  s2[:, hh * 128:(hh + 1) * 128],
                                                  self.ident[:]),
                                      waits=[spl, bf] if hh == 0 else (),
                                      inc=(st["s_tt"], 1) if hh == 3 else None,
                                      tag=f"ktt{fb}{i}{half}{hh}")
                        eb = st["ebuf"][half]
                        ev = p.op("scalar", lambda e, b=bank_t, eb=eb:
                                  e.copy(eb[:], b[:, :512]),
                                  waits=[tt, st["ebuf_free"][half]], inc=(st["s_ee"], 1),
                                  tag=f"kte{fb}{i}{half}")
                        o["bank_free"][half] = ev
                        if o.get("trp_single"):
                            o["bank_free"][1 - half] = ev
                        dmat = p.dma("gpsimd",
                                     APd(kin, half * KSEG + i * 128,
                                         [[ntok, 128], [128 * ntok, 4], [1, 128]]),
                                     APv(eb, 0, [[128, 4], [1, 128]]),
                                     waits=[ev], sem=st["s_dos"][half],
                                     tag=f"kd{fb}_{i}_{half}")
                        st["ebuf_free"][half] = dmat
                        store_toks.append(dmat)
                        last_tt2 = tt
                    st["qsplit_free"][qp] = last_tt2

                if st["pending"] is not None:
                    st["pending"]()
                st["pending"] = emit_T
            else:
                vint = st["vint"]
                vw = [mmt, st["vint_free"]]
                p.op("vector", lambda e, b=bank:
                     e.tensor_copy(APv(vint, 0, [[256, 4], [1, 128]]), b[:]),
                     waits=vw, tag=f"vp1{fb}_{i}")
                spl = p.op("vector", lambda e, b=bank:
                           e.tensor_sub(APv(vint, 128, [[256, 4], [1, 128]]), b[:],
                                        APv(vint, 0, [[256, 4], [1, 128]])),
                           inc=(st["s_rv"], 1), tag=f"vp2{fb}_{i}")
                st["mm_free"][i % 2] = spl
                t = p.dma("gpsimd",
                          APd(kin, 2 * KSEG + i * 128 * 256,
                              [[256, 128], [NT * 128 * 256, 4], [1, 256]]),
                          vint[:], waits=[spl], sem=st["s_do"], tag=f"vd{fb}_{i}")
                st["vint_free"] = t
                store_toks.append(t)

        st.setdefault("holder", {})

        def prep(which, fb=fb):
            w_dram = st["d"]["wk" if which == "k" else "wv"]
            st["holder"]["wready"] = self.wT_prep(o, w_dram, fb, "kv" + which)

        chunks.append(lambda: prep("k"))
        for i in range(NT):
            chunks.append(lambda i=i: tile("k", fb, i))

        def flush_and_prep_v():
            if st["pending"] is not None:
                st["pending"]()
                st["pending"] = None
            o["slab_free"] = st["mm_free"][0]
            prep("v")
        chunks.append(flush_and_prep_v)
        for i in range(NT):
            chunks.append(lambda i=i: tile("v", fb, i))

        def fin():
            o["slab_free"] = st["mm_free"][0]
            cct = p.op("gpsimd",
                       lambda e, kin=kin, ko=st["kv_out"][fb]:
                       e.collective_compute("AllGather", mybir.AluOpType.bypass,
                                            [[0, 1], [2, 3], [4, 5], [6, 7]],
                                            [kin[:]], [ko[:]]),
                       waits=list(store_toks), inc=(st["s_cc"], 1), tag=f"kvag{fb}")
            st["fbdone"][fb] = cct
        chunks.append(fin)
        return chunks

    def attn_merged(self, d, ln, kv_in, kv_out, q_hi, q_lo, ao_scr,
                    fc_chunks=None):
        nc, p, cfg = self.nc, self.p, self.cfg
        NT, MT, HD = cfg.NT, cfg.MT, cfg.heads
        NTOK, MTOK = cfg.NTOK, cfg.MTOK
        KSEG, VSEG, FBSEG = self.kv_geom
        with ExitStack() as sc:
            s = sc.enter_context
            kvst = self.kv_setup(sc, ln, d, kv_in, kv_out, self.q_done)
            qb = [[s(nc.sbuf_tensor(f"atq{b}{h}", [128, NTOK], BF16)) for h in range(2)]
                  for b in range(2)]
            kb = [[s(nc.sbuf_tensor(f"atk{b}{h}", [128, MTOK], BF16)) for h in range(2)]
                  for b in range(2)]
            vb = [s(nc.sbuf_tensor(f"atv{b}", [128, MT * 256], BF16))
                  for b in range(2)]
            ehi2 = [s(nc.sbuf_tensor(f"atehi{i}", [128, MTOK], BF16)) for i in range(2)]
            elo2 = [s(nc.sbuf_tensor(f"atelo{i}", [128, MTOK], BF16)) for i in range(2)]
            eThi = s(nc.sbuf_tensor("ateThi", [128, MT * 512], BF16))
            eTlo = s(nc.sbuf_tensor("ateTlo", [128, MT * 512], BF16))
            ao = s(nc.sbuf_tensor("atao", [128, NTOK], F32))
            mx = s(nc.sbuf_tensor("atmx", [128, 1], F32))
            nbias = s(nc.sbuf_tensor("atnb", [128, 1], F32))
            rsp = s(nc.sbuf_tensor("atrsp", [128, 4], F32))
            rs = s(nc.sbuf_tensor("atrs", [128, 1], F32))
            rr = s(nc.sbuf_tensor("atrr", [128, 1], F32))
            if cfg.debug:
                self.dbg_scb = s(nc.sbuf_tensor("atdsc", [128, 4], F32))
            s32 = [s(nc.sbuf_tensor(f"atS32{i}", [128, 2048], F32)) for i in range(2)]
            psS = [s(nc.psum_tensor(f"atS{i}", [128, 512], F32)) for i in range(2)]
            psT = [s(nc.psum_tensor(f"atT{i}", [128, 1024], BF16)) for i in range(2)]
            psPV = s(nc.psum_tensor("atPV", [128, 512], F32))
            s_ld = [self.sem(f"at_ld{i}") for i in range(2)]
            s_S = self.sem("at_S")
            s_v = self.sem("at_v")
            s_s = self.sem("at_s")
            s_t = self.sem("at_t")
            s_e = self.sem("at_e")
            s_pv = self.sem("at_pv")
            s_ao = self.sem("at_ao")
            s_do = self.sem("at_do")

            start = flat(self.q_done, ln[3])
            start1 = start
            inbuf_free = [None, None]
            NTAU = MTOK // 512

            def load_head(h):
                buf = h % 2
                fbh, hl = h // 4, h % 4
                ko = kv_out[fbh]
                w = flat(self.q_done, kvst["fbdone"].get(fbh), inbuf_free[buf])
                first = True
                for half in range(2):
                    for tau in range(NTAU):
                        p.dma("sync",
                              APd(kb[buf][half], tau * 512,
                                  [[MTOK, 128], [256, 2], [1, 256]]),
                              APd(ko, half * KSEG + hl * 128 * NTOK + tau * 256,
                                  [[NTOK, 128], [FBSEG, 2], [1, 256]]),
                              waits=w if first else (),
                              sem=s_ld[buf], tag=f"ak{half}_{h}_{tau}")
                        first = False
                t = None
                for sg in range(2):
                    for tau in range(NTAU):
                        t = p.dma("sync",
                              APd(vb[buf], sg * 256 + tau * 1024,
                                  [[MT * 256, 128], [512, 2], [1, 256]]),
                              APd(ko, 2 * KSEG + hl * NT * 128 * 256 + sg * 32768
                                  + tau * 65536,
                                  [[256, 128], [FBSEG, 2], [1, 256]]),
                              waits=w if (sg == 0 and tau == 0) else (),
                              sem=s_ld[buf], tag=f"av{sg}_{h}_{tau}")
                return t

            FBn = cfg.FB
            for fn in self.kv_chunks(kvst, 0):
                fn()
            pend_q = []
            s_qld = [self.sem(f"at_qld{i}") for i in range(2)]
            qb_free = [start1, start1]
            qtok = [None, None]

            def load_q(h):
                buf = h % 2
                t = None
                for half, qsc in enumerate([q_hi, q_lo]):
                    t = p.dma("sync", qb[buf][half][:],
                              APd(qsc, h * 128 * NTOK, [[NTOK, 128], [1, NTOK]]),
                              waits=[qb_free[buf]] if half == 0 else (),
                              sem=s_qld[buf], tag=f"aq{h}_{half}")
                qtok[buf] = t

            load_q(0)
            load_q(1)
            ldt = load_head(0)
            if cfg.debug:
                kb_dbg = nc.dram_tensor("kb_dbg", [2, 128, MTOK], BF16,
                                        kind="ExternalOutput").ap()
                vb_dbg = nc.dram_tensor("vb_dbg", [128, MT * 256], BF16,
                                        kind="ExternalOutput").ap()
                ko_dbg = nc.dram_tensor("kvout0d", [2, FBSEG], BF16,
                                        kind="ExternalOutput").ap()
                ki_dbg = nc.dram_tensor("kvin0d", [FBSEG], BF16,
                                        kind="ExternalOutput").ap()
                self.dbg_s = nc.dram_tensor("dbg_s", [2, 128, MTOK], F32,
                                            kind="ExternalOutput").ap()
                self.dbg_e = nc.dram_tensor("dbg_e", [2, 128, MTOK], F32,
                                            kind="ExternalOutput").ap()
                self.dbg_ehi = nc.dram_tensor("dbg_ehi", [2, 128, MTOK], BF16,
                                              kind="ExternalOutput").ap()
                self.dbg_elo = nc.dram_tensor("dbg_elo", [2, 128, MTOK], BF16,
                                              kind="ExternalOutput").ap()
                self.dbg_sc = nc.dram_tensor("dbg_sc", [2, 128, 4], F32,
                                             kind="ExternalOutput").ap()
                s_dbg = self.sem("at_dbg")
                dw = [(s_ld[0], s_ld[0].n)]
                for hf in range(2):
                    p.dma("gpsimd", APd(kb_dbg, hf * 128 * MTOK,
                                        [[MTOK, 128], [1, MTOK]]),
                          kb[0][hf][:], waits=dw, sem=s_dbg, tag=f"dbgk{hf}")
                p.dma("gpsimd", APd(vb_dbg, 0, [[MT * 256, 128], [1, MT * 256]]),
                      vb[0][:], waits=dw, sem=s_dbg, tag="dbgv")
                p.dma("gpsimd", APd(ko_dbg, 0, [[FBSEG, 2], [1, FBSEG]]),
                      APd(kv_out[0], 0, [[FBSEG, 2], [1, FBSEG]]),
                      waits=dw, sem=s_dbg, tag="dbgko")
                p.dma("gpsimd", APd(ki_dbg, 0, [[1, 1], [1, FBSEG]]),
                      APd(kv_in[0], 0, [[1, 1], [1, FBSEG]]),
                      waits=dw, sem=s_dbg, tag="dbgki")
            pending_T = None
            esplit_free = [None, None]
            psS_free = [start1]
            psS_free.append(psS_free[0])
            s32_free = [start1, start1]
            eT_bank_free = [start1, start1]
            eT_slab_free = None
            ao_free = None
            tidx = 0
            for h in range(HD):
                if h % 4 == 0 and h // 4 + 1 < FBn:
                    pend_q = self.kv_chunks(kvst, h // 4 + 1)
                buf = h % 2
                kh, vh = kb[buf], vb[buf]
                ngroups = (NT + 3) // 4
                pv_last = None
                for qg in range(ngroups):
                    ntile_g = min(4, NT - qg * 4)
                    for qi in range(ntile_g):
                        qt = qg * 4 + qi
                        J = cfg.Jtile(qt)
                        keys = (J + 1) * 128
                        nch = (keys + 511) // 512
                        sp_ = tidx % 2
                        s32b = s32[sp_]
                        tidx += 1
                        ev_last = None
                        for c in range(nch):
                            k0, k1 = c * 512, min((c + 1) * 512, keys)
                            bank = psS[c % 2]
                            sS_tok = None
                            for pi, (xs, ws) in enumerate([(0, 0), (1, 0), (0, 1)]):
                                w = []
                                if pi == 0:
                                    w = [psS_free[c % 2]]
                                    if c == 0:
                                        w.append(ldt)
                                        if qt == 0:
                                            w.append(qtok[buf])
                                sS_tok = p.op("tensor",
                                     lambda e, b=bank, q2=qb[buf][xs], k2=kh[ws],
                                            q0=qt * 128, k0=k0, k1=k1,
                                            st=(pi == 0), sp2=(pi == 2):
                                     e.matmul(b[:, :k1 - k0], q2[:, q0:q0 + 128],
                                              k2[:, k0:k1], start=st, stop=sp2),
                                     waits=w,
                                     inc=(s_S, 1) if pi == 2 else None,
                                     tag=f"S{h}_{qt}_{c}_{pi}")
                                head_S_last = sS_tok
                            wjl = (J - 2) * 128
                            if k0 >= wjl or k1 > wjl:
                                ov0 = max(k0, wjl)
                                ev1 = None
                                if ov0 > k0:
                                    ev1 = p.op("scalar",
                                         lambda e, b=bank, s3=s32b, k0=k0, ov0=ov0:
                                         e.copy(s3[:, k0:ov0], b[:, :ov0 - k0]),
                                         waits=[sS_tok, s32_free[sp_]],
                                         inc=(s_s, 1), tag=f"Se{h}_{qt}_{c}")
                                ev_last = p.op("vector",
                                     lambda e, b=bank, s3=s32b, k0=k0, ov0=ov0, k1=k1,
                                            wjl=wjl:
                                     e.tensor_add(s3[:, ov0:k1],
                                                  b[:, ov0 - k0:k1 - k0],
                                                  self.masks[:, ov0 - wjl:k1 - wjl]),
                                     waits=[sS_tok, s32_free[sp_]] + ([ev1] if ev1 else []),
                                     inc=(s_v, 1), tag=f"Sm{h}_{qt}_{c}")
                                psS_free[c % 2] = flat(ev_last, ev1)
                            else:
                                ev_last = p.op("scalar",
                                     lambda e, b=bank, s3=s32b, k0=k0, k1=k1:
                                     e.copy(s3[:, k0:k1], b[:, :k1 - k0]),
                                     waits=[sS_tok, s32_free[sp_]],
                                     inc=(s_s, 1), tag=f"Se{h}_{qt}_{c}")
                                psS_free[c % 2] = ev_last
                        if cfg.debug and h == 0 and qt <= 1:
                            dsm = p.dma("gpsimd",
                                        APd(self.dbg_s, qt * 128 * MTOK,
                                            [[MTOK, 128], [1, keys]]),
                                        s32b[:, :keys], waits=[ev_last, (s_s, s_s.n)],
                                        sem=self.sem(f"at_dbgS{qt}"), tag=f"dS{qt}")
                            ev_last = flat(ev_last, dsm)
                        mxt = p.op("vector", lambda e, s3=s32b, keys=keys:
                             e.reduce_max(mx[:], s3[:, :keys], axis=AX),
                             waits=[ev_last, (s_s, s_s.n)],
                             inc=(s_v, 1), tag=f"mx{h}_{qt}")
                        nbt = p.op("vector", lambda e: e.tensor_scalar(
                                nbias[:], mx[:], -SCALE, None, ALU.mult),
                             waits=[mxt], inc=(s_v, 1), tag=f"nb{h}_{qt}")
                        expt = None
                        for c in range(nch):
                            k0, k1 = c * 512, min((c + 1) * 512, keys)
                            expt = p.op("scalar", lambda e, s3=s32b, k0=k0, k1=k1, c=c:
                                 e.activation(s3[:, k0:k1], s3[:, k0:k1], ACTF.Exp,
                                              bias=nbias[:], scale=SCALE,
                                              accum_out=rsp[:, c:c + 1]),
                                 waits=[nbt] if c == 0 else (),
                                 inc=(s_s, 1) if c == nch - 1 else None,
                                 tag=f"ex{h}_{qt}_{c}")
                        rst = p.op("vector", lambda e, nch=nch:
                             e.reduce_sum(rs[:], rsp[:, :nch], axis=AX),
                             waits=[expt], inc=(s_v, 1), tag=f"rs{h}_{qt}")
                        rrt = p.op("vector", lambda e: e.reciprocal(rr[:], rs[:]),
                             waits=[rst], inc=(s_v, 1), tag=f"rr{h}_{qt}")
                        if pending_T is not None:
                            pending_T()
                            pending_T = None
                        ehi, elo = ehi2[sp_], elo2[sp_]
                        p.op("vector", lambda e, s3=s32b, ehi=ehi, keys=keys:
                             e.tensor_scalar(
                                ehi[:, :keys], s3[:, :keys], rr[:], None, ALU.mult),
                             waits=[esplit_free[sp_], rrt], tag=f"e1{h}_{qt}")
                        spl = p.op("vector",
                                   lambda e, s3=s32b, ehi=ehi, elo=elo, keys=keys:
                                   e.scalar_tensor_tensor(
                                elo[:, :keys], s3[:, :keys], rr[:], ehi[:, :keys],
                                ALU.mult, ALU.subtract),
                             inc=(s_v, 1), tag=f"e2{h}_{qt}")
                        s32_free[sp_] = spl
                        if cfg.debug and h == 0 and qt <= 1:
                            dse = p.dma("gpsimd",
                                        APd(self.dbg_e, qt * 128 * MTOK,
                                            [[MTOK, 128], [1, keys]]),
                                        s32b[:, :keys], waits=[spl],
                                        sem=self.sem(f"at_dbgE{qt}"), tag=f"dE{qt}")
                            deh = p.dma("gpsimd",
                                        APd(self.dbg_ehi, qt * 128 * MTOK,
                                            [[MTOK, 128], [1, keys]]),
                                        ehi[:, :keys], waits=[spl],
                                        sem=self.sem(f"at_dbgH{qt}"), tag=f"dEh{qt}")
                            del_ = p.dma("gpsimd",
                                         APd(self.dbg_elo, qt * 128 * MTOK,
                                             [[MTOK, 128], [1, keys]]),
                                         elo[:, :keys], waits=[spl],
                                         sem=self.sem(f"at_dbgL{qt}"), tag=f"dEl{qt}")
                            for si2, src2 in enumerate([mx, rs, rr, rr]):
                                p.op("scalar", lambda e, s2=src2, si2=si2:
                                     e.copy(self.dbg_scb[:, si2:si2 + 1], s2[:]),
                                     waits=[spl, getattr(self, "_dbg_scb_free", None)]
                                           if si2 == 0 else [spl],
                                     inc=(s_s, 1), tag=f"dsc{qt}_{si2}")
                            dsc = p.dma("gpsimd",
                                        APd(self.dbg_sc, qt * 128 * 4,
                                            [[4, 128], [1, 4]]),
                                        self.dbg_scb[:], waits=[(s_s, s_s.n)],
                                        sem=self.sem("at_dbg"), tag=f"dM{qt}")
                            dsc2 = dsc3 = dsc
                            self._dbg_scb_free = dsc
                            s32_free[sp_] = flat(spl, dse)
                            esplit_free[0] = flat(esplit_free[0], deh, del_)
                            esplit_free[1] = flat(esplit_free[1], deh, del_)
                            self._dbg_sc_toks = [dsc, dsc2, dsc3]

                        def emit_T(h=h, qt=qt, qi=qi, J=J, spl=spl, ehi=ehi, elo=elo,
                                   sp_=sp_):
                            last_tt = None
                            for half, (src, slab) in enumerate([(ehi, eThi), (elo, eTlo)]):
                                bank = psT[half]
                                for g4 in range((J + 1 + 3) // 4):
                                    j0 = g4 * 4
                                    cnt = min(4, J + 1 - j0)
                                    bf = eT_bank_free[half]
                                    tt = None
                                    for jj in range(cnt):
                                        j = j0 + jj
                                        tt = p.op("tensor",
                                                  lambda e, s2=src, b=bank, j=j, jj=jj:
                                                  e.transpose(b[:, jj * 128:(jj + 1) * 128],
                                                              s2[:, j * 128:(j + 1) * 128],
                                                              self.ident[:]),
                                                  waits=[spl, eT_slab_free, bf]
                                                        if jj == 0 else (),
                                                  inc=(s_t, 1) if jj == cnt - 1 else None,
                                                  tag=f"T{h}_{qt}_{half}_{j}")
                                    ev = p.op("scalar",
                                              lambda e, b=bank, slab=slab, j0=j0, cnt=cnt,
                                                     qi=qi:
                                              e.copy(APv(slab, j0 * 512 + qi * 128,
                                                         [[512, cnt], [1, 128]]),
                                                     b[:, :cnt * 128]),
                                              waits=[tt], inc=(s_e, 1),
                                              tag=f"Te{h}_{qt}_{half}_{g4}")
                                    eT_bank_free[half] = ev
                                    last_tt = tt
                            esplit_free[sp_] = last_tt

                        pending_T = emit_T
                        heads_left = HD - h
                        tiles_left = heads_left * NT - (qg * 4 + qi + 1)
                        if pend_q:
                            tl_fb = 4 * NT - (h % 4) * NT - (qg * 4 + qi + 1)
                            take = (len(pend_q) + max(tl_fb, 1) - 1) // max(tl_fb, 1)
                            for _ in range(min(take, len(pend_q))):
                                pend_q.pop(0)()
                        if fc_chunks and tiles_left > 0:
                            take = (len(fc_chunks) + tiles_left - 1) // tiles_left
                            for _ in range(min(take, len(fc_chunks))):
                                fc_chunks.pop(0)()
                    if pending_T is not None:
                        pending_T()
                        pending_T = None
                    eT_done = eT_bank_free[1]
                    Jmax = cfg.Jtile(qg * 4 + ntile_g - 1)
                    first = True
                    pv_tok = None
                    for j in range(Jmax + 1):
                        qstart = 0
                        while qstart < ntile_g and cfg.Jtile(qg * 4 + qstart) < j:
                            qstart += 1
                        qoff = qstart * 128
                        width = ntile_g * 128 - qoff
                        if width <= 0:
                            continue
                        for pi, (vs, es) in enumerate([(0, eThi), (1, eThi), (0, eTlo)]):
                            w = [eT_done, ao_free] if first else ()
                            first = False
                            pv_tok = p.op("tensor",
                                 lambda e, vh2=vh, voff=j * 256 + vs * 128, es=es, j=j,
                                        qoff=qoff, width=width,
                                        st=(j == 0 and pi == 0),
                                        sp=(j == Jmax and pi == 2):
                                 e.matmul(psPV[:, qoff:qoff + width],
                                          APv(vh2, voff, [[1, 128]]),
                                          APv(es, j * 512 + qoff, [[1, width]]),
                                          start=st, stop=sp),
                                 waits=w,
                                 inc=(s_pv, 1) if (j == Jmax and pi == 2) else None,
                                 tag=f"P{h}_{qg}_{j}_{pi}")
                    eT_slab_free = pv_tok
                    aoev = p.op("vector", lambda e, gq0=qg * 4, wd=ntile_g * 128:
                                e.tensor_copy(ao[:, gq0 * 128:gq0 * 128 + wd], psPV[:, :wd]),
                                waits=[pv_tok], inc=(s_ao, 1), tag=f"ae{h}_{qg}")
                    ao_free = aoev
                    pv_last = aoev
                aodma = p.dma("gpsimd", APd(ao_scr, h * 128 * NTOK, [[NTOK, 128], [1, NTOK]]),
                              ao[:], waits=[pv_last], sem=s_do, tag=f"ad{h}")
                ao_free = aodma
                inbuf_free[buf] = pv_last
                qb_free[buf] = head_S_last
                if h + 2 < HD:
                    load_q(h + 2)
                if h + 1 < HD:
                    while pend_q and (h + 1) % 4 == 0:
                        pend_q.pop(0)()
                    ldt = load_head(h + 1)
            self.attn_done = flat((s_do, s_do.n), kvst["fbdone"].get(cfg.FB - 1))

    def ffn_weight_cache_mk(self, fc_stack, d, wg_scr, wu_scr, wd_scr, prev):
        nc, p, cfg = self.nc, self.p, self.cfg
        EC, OC, INTER = cfg.EC, cfg.OC, cfg.INTER
        s = fc_stack.enter_context
        W = max(INTER // 4, E)
        i80 = s(nc.sbuf_tensor("fci0", [128, W], I8))
        i8b = [i80, i80]
        bfb = [s(nc.sbuf_tensor(f"fcb{i}", [128, W], BF16)) for i in range(2)]
        s_ld = [self.sem(f"fc_ld{i}") for i in range(2)]
        s_cg = self.sem("fc_cg")
        s_cv = self.sem("fc_cv")
        s_st = [self.sem(f"fc_st{i}") for i in range(2)]
        half = INTER // 2
        st = {"i8_free": [prev, prev], "bf_free": [prev, prev], "idx": 0}

        def emit_gu(wi, hc, h2):
            w8, scr = [(d["wg8"], wg_scr), (d["wu8"], wu_scr)][wi]
            bi = st["idx"] % 2
            st["idx"] += 1
            W2 = INTER // 4
            q4 = W2 // 2
            ldt = p.dma("sync", i8b[bi][:, :W2],
                        w8[hc * 128:(hc + 1) * 128, h2 * W2:(h2 + 1) * W2],
                        waits=[st["i8_free"][bi]], sem=s_ld[bi], tag=f"fcl{wi}_{hc}_{h2}")
            c1 = p.op("gpsimd", lambda e, b=bi, q4=q4: e.tensor_copy(bfb[b][:, :q4],
                                                              i8b[b][:, :q4]),
                      waits=[ldt, st["bf_free"][bi]], inc=(s_cg, 1), tag=f"fcg{wi}_{hc}_{h2}")
            c2 = p.op("vector", lambda e, b=bi, q4=q4, W2=W2: e.tensor_copy(
                          bfb[b][:, q4:W2], i8b[b][:, q4:W2]),
                      waits=[ldt, st["bf_free"][bi]], inc=(s_cv, 1), tag=f"fcv{wi}_{hc}_{h2}")
            st["i8_free"][0] = [c1, c2, ldt]
            st["i8_free"][1] = [c1, c2, ldt]
            o0 = h2 * (OC // 4)
            t = p.dma("gpsimd",
                      APd(scr, (o0 * EC + hc) * 128 * 128,
                          [[128, 128], [EC * 128 * 128, OC // 4], [1, 128]]),
                      APv(bfb[bi], 0, [[128, OC // 4], [1, 128]]),
                      waits=[c1, c2], sem=s_st[bi], tag=f"fcs{wi}_{hc}_{h2}")
            st["bf_free"][bi] = t

        def emit_wd(oc):
            bi = st["idx"] % 2
            st["idx"] += 1
            ldt = p.dma("sync", i8b[bi][:, :E], d["wd8"][oc * 128:(oc + 1) * 128, :],
                        waits=[st["i8_free"][bi]], sem=s_ld[bi], tag=f"fcld{oc}")
            c1 = p.op("gpsimd", lambda e, b=bi: e.tensor_copy(bfb[b][:, :E // 2],
                                                              i8b[b][:, :E // 2]),
                      waits=[ldt, st["bf_free"][bi]], inc=(s_cg, 1), tag=f"fcgd{oc}")
            c2 = p.op("vector", lambda e, b=bi: e.tensor_copy(bfb[b][:, E // 2:E],
                                                              i8b[b][:, E // 2:E]),
                      waits=[ldt, st["bf_free"][bi]], inc=(s_cv, 1), tag=f"fcvd{oc}")
            st["i8_free"][0] = [c1, c2, ldt]
            st["i8_free"][1] = [c1, c2, ldt]
            toks = []
            for part in range(4):
                h0 = part * (EC // 4)
                t = p.dma("gpsimd",
                          APd(wd_scr, (h0 * OC + oc) * 128 * 128,
                              [[128, 128], [OC * 128 * 128, EC // 4], [1, 128]]),
                          APv(bfb[bi], h0 * 128, [[128, EC // 4], [1, 128]]),
                          waits=[c1, c2], sem=s_st[bi], tag=f"fcsd{oc}_{part}")
                toks.append(t)
            st["bf_free"][bi] = toks[-1]

        chunks = []
        for wi in range(2):
            for hc in range(EC):
                for h2 in range(4):
                    chunks.append(lambda wi=wi, hc=hc, h2=h2: emit_gu(wi, hc, h2))
        for oc in range(OC):
            chunks.append(lambda oc=oc: emit_wd(oc))
        self._fc_sems = (s_st,)
        return chunks

    def fc_finalize(self):
        s_st = self._fc_sems[0]
        self.fc_done = [(s_st[0], s_st[0].n), (s_st[1], s_st[1].n)]

    def oproj_phase(self, d, ao_scr, o2_scr, x1_scr, lnf_hi, lnf_lo):
        nc, p, cfg = self.nc, self.p, self.cfg
        NT, EC, HD = cfg.NT, cfg.EC, cfg.heads
        NTOK = cfg.NTOK
        prev = flat(self.attn_done, self.fc_done)
        with ExitStack() as sc2:
            s2 = sc2.enter_context
            aohi = s2(nc.sbuf_tensor("oah", [128, HD * NTOK], BF16))
            aolo = s2(nc.sbuf_tensor("oal", [128, HD * NTOK], BF16))
            aot = [s2(nc.sbuf_tensor(f"oat{i}", [128, NTOK], F32)) for i in range(2)]
            o = {}
            o["wrow"] = [s2(nc.sbuf_tensor(f"owr{i}", [128, E], F32)) for i in range(2)]
            o["wrhi"] = s2(nc.sbuf_tensor("owh", [128, E], BF16))
            o["wrlo"] = s2(nc.sbuf_tensor("owl", [128, E], BF16))
            wT2 = [[s2(nc.sbuf_tensor(f"owT{b}{h}", [128, EC * 512], BF16))
                    for h in range(2)] for b in range(2)]
            o["wT"] = wT2[0]
            o["trp"] = [s2(nc.psum_tensor(f"otp{i}", [128, 1024], BF16)) for i in range(2)]
            mm = [s2(nc.psum_tensor(f"omm{i}", [128, 512], F32)) for i in range(2)]
            evb = [s2(nc.sbuf_tensor(f"oev{i}", [128, 512], F32)) for i in range(2)]
            o["s_wld"] = [self.sem(f"pj_wld{i}") for i in range(2)]
            o["s_wv"] = self.sem("pj_wv")
            o["s_wt"] = self.sem("pj_wt")
            o["s_we"] = self.sem("pj_we")
            s_ld = [self.sem(f"o_ld{i}") for i in range(2)]
            s_v = self.sem("o_v")
            s_mm = self.sem("pj_mm")
            s_ev = self.sem("o_ev")
            s_evd = [self.sem(f"o_evd{i}") for i in range(2)]
            o["wrow_free"] = [prev, prev]
            o["bank_free"] = [prev, prev]
            o["vec_start"] = prev
            o["slab_free"] = None
            aot_free = [prev, prev]
            spl = None
            for hh in range(HD):
                buf = aot[hh % 2]
                ldt = p.dma("sync", buf[:], APd(ao_scr, hh * 128 * NTOK,
                                                [[NTOK, 128], [1, NTOK]]),
                            waits=[aot_free[hh % 2], prev], sem=s_ld[hh % 2], tag=f"oal{hh}")
                p.op("vector", lambda e, b=buf, hh=hh:
                     e.tensor_copy(aohi[:, hh * NTOK:(hh + 1) * NTOK], b[:]),
                     waits=[ldt, prev], tag=f"os1{hh}")
                spl = p.op("vector", lambda e, b=buf, hh=hh:
                           e.tensor_sub(aolo[:, hh * NTOK:(hh + 1) * NTOK], b[:],
                                        aohi[:, hh * NTOK:(hh + 1) * NTOK]),
                           inc=(s_v, 1), tag=f"os2{hh}")
                aot_free[hh % 2] = spl
            split_done = spl
            mm_free = [prev, prev]
            evb_free = [prev, prev]
            slab_mm_last = [None, None]
            for fb in range(4):
                o["wT"] = wT2[fb % 2]
                o["slab_free"] = slab_mm_last[fb % 2]
                wready = self.wT_prep(o, d["wo"], fb, "o")
                last_mm_of_fb = None
                for i in range(NT):
                    bank = mm[i % 2]
                    waits = [wready, split_done, mm_free[i % 2]]
                    last = None
                    for ec in range(HD):
                        col = ec * NTOK + i * 128
                        p.op("tensor", lambda e, b=bank, col=col, ec=ec, wt=o["wT"][0]:
                             e.matmul(b[:], aohi[:, col:col + 128],
                                      wt[:, ec * 512:(ec + 1) * 512],
                                      start=(ec == 0), stop=False),
                             waits=waits if ec == 0 else (), tag=f"om{fb}_{i}_{ec}a")
                        p.op("tensor", lambda e, b=bank, col=col, ec=ec, wt=o["wT"][1]:
                             e.matmul(b[:], aohi[:, col:col + 128],
                                      wt[:, ec * 512:(ec + 1) * 512],
                                      start=False, stop=False), tag=f"om{fb}_{i}_{ec}b")
                        last = p.op("tensor",
                             lambda e, b=bank, col=col, ec=ec, wt=o["wT"][0]:
                             e.matmul(b[:], aolo[:, col:col + 128],
                                      wt[:, ec * 512:(ec + 1) * 512],
                                      start=False, stop=(ec == HD - 1)),
                             inc=(s_mm, 1) if ec == HD - 1 else None,
                             tag=f"om{fb}_{i}_{ec}c")
                    last_mm_of_fb = last
                    eb = evb[i % 2]
                    ev = p.op("scalar", lambda e, b=bank, eb=eb: e.copy(eb[:], b[:]),
                              waits=[last, evb_free[i % 2]], inc=(s_ev, 1),
                              tag=f"oe{fb}_{i}")
                    mm_free[i % 2] = ev
                    dmat = p.dma("gpsimd",
                                 APd(o2_scr, i * 128 * E + fb * 512, [[E, 128], [1, 512]]),
                                 eb[:], waits=[ev], sem=s_evd[i % 2], tag=f"oed{fb}_{i}")
                    evb_free[i % 2] = dmat
                slab_mm_last[fb % 2] = last_mm_of_fb
            mm_all = [(s_evd[0], s_evd[0].n), (s_evd[1], s_evd[1].n)]

        with ExitStack() as sc3:
            s3 = sc3.enter_context
            o2t = [s3(nc.sbuf_tensor(f"oo2{i}", [128, E], F32)) for i in range(2)]
            xt = [s3(nc.sbuf_tensor(f"oxt{i}", [128, E], F32)) for i in range(2)]
            bns = s3(nc.sbuf_tensor("obns", [128, 24], F32))
            bno = s3(nc.sbuf_tensor("obno", [128, 2], F32))
            std = s3(nc.sbuf_tensor("ostd", [128, 1], F32))
            rstd = s3(nc.sbuf_tensor("orstd", [128, 1], F32))
            x1t = s3(nc.sbuf_tensor("ox1", [128, E], F32))
            ln3 = s3(nc.sbuf_tensor("oln3", [128, E], F32))
            lhi = s3(nc.sbuf_tensor("olh", [128, E], BF16))
            llo = s3(nc.sbuf_tensor("oll", [128, E], BF16))
            trp2 = [s3(nc.psum_tensor(f"otr2{i}", [128, 1024], BF16)) for i in range(2)]
            lfst = s3(nc.sbuf_tensor("olf", [128, 2 * E], BF16))
            s_x = [self.sem(f"o_x{i}") for i in range(2)]
            s_o2 = [self.sem(f"o_o2{i}") for i in range(2)]
            s_lf = self.sem("o_lf")
            s_v2 = self.sem("o_v2")
            s_s2 = self.sem("o_s2")
            s_t2 = self.sem("o_t2")
            s_e2 = self.sem("o_e2")
            s_d2 = self.sem("o_d2")
            mm_all0 = mm_all[0]
            o2_free = [mm_all0, mm_all0]
            xt_free = [mm_all0, mm_all0]
            bank_free = [mm_all0, mm_all0]
            lf_free = None
            x1_prev = None
            for i in range(NT):
                o2b = o2t[i % 2]
                o2ldt = p.dma("sync", o2b[:], o2_scr[i * 128:(i + 1) * 128, :],
                              waits=flat(o2_free[i % 2], mm_all), sem=s_o2[i % 2],
                              tag=f"o2l{i}")
                xb = xt[i % 2]
                xldt = p.dma("sync", xb[:], d["x_own"][i * 128:(i + 1) * 128, :],
                             waits=[xt_free[i % 2]], sem=s_x[i % 2], tag=f"oxl{i}")
                stats = self.ln_stats(bns, bno,
                                      lambda c, b=o2b: b[:, c * 512:(c + 1) * 512],
                                      E, flat(o2ldt, mm_all), s_v2, f"ob{i}")
                sq = p.op("scalar", lambda e: e.activation(std[:], bno[:, 1:2], ACTF.Sqrt,
                                                           bias=self.epsb[:]),
                          waits=flat(stats, mm_all), inc=(s_s2, 1), tag=f"osq{i}")
                rct = p.op("vector", lambda e: e.reciprocal(rstd[:], std[:]), waits=[sq],
                           inc=(s_v2, 1), tag=f"orc{i}")
                p.op("vector", lambda e, b=o2b: e.tensor_scalar(
                        x1t[:], b[:], bno[:, 0:1], rstd[:], ALU.subtract, ALU.mult),
                     waits=[x1_prev, rct], tag=f"oln2{i}")
                rest = p.op("vector", lambda e, b=xb: e.tensor_add(x1t[:], x1t[:], b[:]),
                            waits=[xldt], inc=(s_v2, 1), tag=f"ores{i}")
                o2_free[i % 2] = rest
                xt_free[i % 2] = rest
                x1d = p.dma("gpsimd", x1_scr[i * 128:(i + 1) * 128, :], x1t[:],
                            waits=[rest], sem=s_d2, tag=f"ox1d{i}")
                stats3 = self.ln_stats(bns, bno,
                                       lambda c: x1t[:, c * 512:(c + 1) * 512],
                                       E, [], s_v2, f"ob3{i}")
                sq3 = p.op("scalar", lambda e: e.activation(std[:], bno[:, 1:2], ACTF.Sqrt,
                                                            bias=self.epsb[:]),
                           waits=[stats3], inc=(s_s2, 1), tag=f"osq3{i}")
                rct3 = p.op("vector", lambda e: e.reciprocal(rstd[:], std[:]), waits=[sq3],
                            inc=(s_v2, 1), tag=f"orc3{i}")
                x1_prev = p.op("vector", lambda e: e.tensor_scalar(
                        ln3[:], x1t[:], bno[:, 0:1], rstd[:], ALU.subtract, ALU.mult),
                     waits=[x1d, rct3], inc=(s_v2, 1), tag=f"oln3{i}")
                p.op("vector", lambda e: e.tensor_copy(lhi[:], ln3[:]),
                     waits=[lf_free], tag=f"ol1{i}")
                spl = p.op("vector", lambda e: e.tensor_sub(llo[:], ln3[:], lhi[:]),
                           inc=(s_v2, 1), tag=f"ol2{i}")
                gidx = 0
                for half, (src, coff) in enumerate([(lhi, 0), (llo, E)]):
                    for g in range(EC // 4):
                        bi = gidx % 2
                        bank = trp2[bi]
                        bf = bank_free[bi]
                        gidx += 1
                        tt = None
                        for j in range(4):
                            ec = g * 4 + j
                            tt = p.op("tensor", lambda e, s4=src, b=bank, ec=ec, j=j:
                                      e.transpose(b[:, j * 128:(j + 1) * 128],
                                                  s4[:, ec * 128:(ec + 1) * 128],
                                                  self.ident[:]),
                                      waits=[spl, bf] if j == 0 else (),
                                      inc=(s_t2, 1) if j == 3 else None,
                                      tag=f"ot{i}{half}{g}{j}")
                        base = coff + g * 4 * 128
                        ev = p.op("scalar", lambda e, b=bank, base=base:
                                  e.copy(lfst[:, base:base + 512], b[:, :512]),
                                  waits=[tt], inc=(s_e2, 1), tag=f"ote{i}{half}{g}")
                        bank_free[bi] = ev
                lf_toks = []
                last_ev_tok = bank_free[(gidx - 1) % 2]
                for half, (coff, dst) in enumerate([(0, lnf_hi), (E, lnf_lo)]):
                    t = p.dma("gpsimd",
                              APd(dst, i * 128, [[NTOK, 128], [NTOK * 128, EC], [1, 128]]),
                              APv(lfst, coff, [[128, EC], [1, 128]]),
                              waits=[last_ev_tok], sem=s_lf, tag=f"olf{i}_{half}")
                    lf_toks.append(t)
                lf_free = lf_toks[-1]
            self.oproj_done = [(s_d2, s_d2.n), (s_lf, s_lf.n)]

    def ffn_phase(self, d, lnf_hi, lnf_lo, wg_scr, wu_scr, wd_scr, x1_scr):
        nc, p, cfg = self.nc, self.p, self.cfg
        EC, OC, NT = cfg.EC, cfg.OC, cfg.NT
        NTOK = cfg.NTOK
        TC = cfg.TC
        chunks = []
        t0 = 0
        while t0 < NTOK:
            tc = min(TC, NTOK - t0)
            chunks.append((t0, tc))
            t0 += tc
        prev = flat(self.oproj_done, self.fc_done)
        with ExitStack() as sc:
            s = sc.enter_context
            fuse_hi = s(nc.sbuf_tensor("ffh", [128, OC * TC], BF16))
            fuse_lo = s(nc.sbuf_tensor("ffl", [128, OC * TC], BF16))
            lnc = [s(nc.sbuf_tensor(f"fln{h}", [128, EC * TC], BF16)) for h in range(2)]
            wgt = [s(nc.sbuf_tensor(f"fwg{i}", [128, EC * 128], BF16)) for i in range(2)]
            wut = [s(nc.sbuf_tensor(f"fwu{i}", [128, EC * 128], BF16)) for i in range(2)]
            wdt = [s(nc.sbuf_tensor(f"fwd{i}", [128, OC * 128], BF16)) for i in range(2)]
            gsil = s(nc.sbuf_tensor("fgs", [128, TC], F32))
            ups = s(nc.sbuf_tensor("fup", [128, TC], F32))
            fuse32 = s(nc.sbuf_tensor("ff32", [128, TC], F32))
            dn = s(nc.sbuf_tensor("fdn", [128, 512], F32))
            dn_all = s(nc.sbuf_tensor("fdna", [128, (TC // 128) * E], F32))
            x1t = [s(nc.sbuf_tensor(f"fx1{i}", [128, E], F32)) for i in range(2)]
            outt = s(nc.sbuf_tensor("fout", [128, E], F32))
            ln4 = s(nc.sbuf_tensor("fln4", [128, E], F32))
            bns = s(nc.sbuf_tensor("fbns", [128, 24], F32))
            bno = s(nc.sbuf_tensor("fbno", [128, 2], F32))
            std = s(nc.sbuf_tensor("fstd", [128, 1], F32))
            rstd = s(nc.sbuf_tensor("frstd", [128, 1], F32))
            psG = s(nc.psum_tensor("fG", [128, 512], F32))
            psU = s(nc.psum_tensor("fU", [128, 512], F32))
            psD = s(nc.psum_tensor("fD", [128, 512], F32))
            psTr = s(nc.psum_tensor("fTr", [128, 512], F32))
            s_lnld = self.sem("f_lnld")
            s_wld = [self.sem(f"f_wld{i}") for i in range(2)]
            s_wdld = [self.sem(f"f_wdld{i}") for i in range(2)]
            s_g = self.sem("f_g")
            s_sv = self.sem("f_sv")
            s_fv = self.sem("f_fv")
            s_dmm = self.sem("f_dmm")
            s_dev = self.sem("f_dev")
            s_dtr = self.sem("f_dtr")
            s_dte = self.sem("f_dte")
            s_v = self.sem("f_v")
            s_s = self.sem("f_s")

            fcd = self.fc_done
            lnc_free = None
            wg_free = [fcd[0], fcd[1]]
            wd_free = [fcd[0], fcd[1]]
            psG_free = prev[0]
            psU_free = prev[0]
            psD_free = prev[0]
            psTr_free = prev[0]
            fuse_free = prev[0]
            dn_free = prev[0]
            dna_free = prev[0]
            x1_free = [prev[0], prev[0]]
            outt_free = prev[0]
            gu_free = prev[0]
            s_x1 = [self.sem(f"f_x1s{i}") for i in range(2)]
            for ci, (tt0, tcw) in enumerate(chunks):
                ntile_c = tcw // 128
                lnt = None
                for half, src in enumerate([lnf_hi, lnf_lo]):
                    lnt = p.dma("sync", lnc[half][:, :EC * tcw],
                                APd(src, tt0, [[NTOK, 128], [NTOK * 128, EC], [1, tcw]]),
                                waits=(list(prev) if ci == 0 else [lnc_free]), sem=s_lnld,
                                tag=f"fl{ci}_{half}")
                last_g = None
                for oi in range(OC):
                    wb = oi % 2
                    wldt = p.dma("sync", wgt[wb][:],
                                 APd(wg_scr, oi * EC * 128 * 128,
                                     [[128, 128], [128 * 128, EC], [1, 128]]),
                                 waits=[wg_free[wb]] + (prev if ci == 0 and oi < 2 else []),
                                 sem=s_wld[wb], tag=f"fg{ci}_{oi}")
                    wldt2 = p.dma("sync", wut[wb][:],
                                  APd(wu_scr, oi * EC * 128 * 128,
                                      [[128, 128], [128 * 128, EC], [1, 128]]),
                                  sem=s_wld[wb], tag=f"fu{ci}_{oi}")
                    toks = {}
                    for (ps, wt, nm, psfree) in [(psG, wgt[wb], "g", psG_free),
                                                 (psU, wut[wb], "u", psU_free)]:
                        lastmm = None
                        for hc in range(EC):
                            w = [wldt, wldt2, lnt, psfree] if hc == 0 else ()
                            p.op("tensor", lambda e, ps=ps, wt=wt, hc=hc, tcw=tcw:
                                 e.matmul(ps[:, :tcw], wt[:, hc * 128:(hc + 1) * 128],
                                          lnc[0][:, hc * tcw:(hc + 1) * tcw],
                                          start=(hc == 0), stop=False),
                                 waits=w, tag=f"f{nm}{ci}_{oi}_{hc}a")
                            lastmm = p.op("tensor", lambda e, ps=ps, wt=wt, hc=hc, tcw=tcw:
                                 e.matmul(ps[:, :tcw], wt[:, hc * 128:(hc + 1) * 128],
                                          lnc[1][:, hc * tcw:(hc + 1) * tcw],
                                          start=False, stop=(hc == EC - 1)),
                                 inc=(s_g, 1) if hc == EC - 1 else None,
                                 tag=f"f{nm}{ci}_{oi}_{hc}b")
                        toks[nm] = lastmm
                    wg_free[wb] = toks["u"]
                    last_g = toks["u"]
                    sil = p.op("scalar", lambda e, oi=oi, tcw=tcw:
                               e.activation(gsil[:, :tcw], psG[:, :tcw], ACTF.Silu,
                                            scale=self.sg_sb[:, oi:oi + 1]),
                               waits=[toks["g"], gu_free], inc=(s_sv, 1),
                               tag=f"fsl{ci}_{oi}")
                    psG_free = sil
                    up = p.op("scalar", lambda e, oi=oi, tcw=tcw:
                              e.activation(ups[:, :tcw], psU[:, :tcw], ACTF.Copy,
                                           scale=self.su_sb[:, oi:oi + 1]),
                              waits=[toks["u"]], inc=(s_sv, 1), tag=f"fue{ci}_{oi}")
                    psU_free = up
                    fw = [sil, up]
                    if oi == 0:
                        fw.append(fuse_free)
                    gu_free = p.op("vector", lambda e, tcw=tcw: e.tensor_mul(
                            fuse32[:, :tcw], gsil[:, :tcw], ups[:, :tcw]),
                         waits=fw, inc=(s_fv, 1), tag=f"fm{ci}_{oi}")
                    p.op("vector", lambda e, oi=oi, tcw=tcw:
                         e.tensor_copy(fuse_hi[:, oi * TC:oi * TC + tcw], fuse32[:, :tcw]),
                         tag=f"fh{ci}_{oi}")
                    p.op("vector", lambda e, oi=oi, tcw=tcw:
                         e.tensor_sub(fuse_lo[:, oi * TC:oi * TC + tcw], fuse32[:, :tcw],
                                      fuse_hi[:, oi * TC:oi * TC + tcw]),
                         inc=(s_fv, 1), tag=f"fl2{ci}_{oi}")
                fv = (s_fv, s_fv.n)
                lnc_free = last_g
                dmm_last = None
                for hc in range(EC):
                    wb = hc % 2
                    wdldt = p.dma("sync", wdt[wb][:],
                                  APd(wd_scr, hc * OC * 128 * 128,
                                      [[128, 128], [128 * 128, OC], [1, 128]]),
                                  waits=[wd_free[wb]] + (prev if ci == 0 and hc < 2 else []),
                                  sem=s_wdld[wb], tag=f"fw{ci}_{hc}")
                    lastd = None
                    for oi in range(OC):
                        w = [fv, wdldt, psD_free] if oi == 0 else ()
                        p.op("tensor", lambda e, oi=oi, tcw=tcw, wb=wb:
                             e.matmul(psD[:, :tcw], wdt[wb][:, oi * 128:(oi + 1) * 128],
                                      fuse_hi[:, oi * TC:oi * TC + tcw],
                                      start=(oi == 0), stop=False),
                             waits=w, tag=f"fd{ci}_{hc}_{oi}a")
                        lastd = p.op("tensor", lambda e, oi=oi, tcw=tcw, wb=wb:
                             e.matmul(psD[:, :tcw], wdt[wb][:, oi * 128:(oi + 1) * 128],
                                      fuse_lo[:, oi * TC:oi * TC + tcw],
                                      start=False, stop=(oi == OC - 1)),
                             inc=(s_dmm, 1) if oi == OC - 1 else None,
                             tag=f"fd{ci}_{hc}_{oi}b")
                    wd_free[wb] = lastd
                    dmm_last = lastd
                    dev = p.op("scalar", lambda e, hc=hc, tcw=tcw:
                               e.activation(dn[:, :tcw], psD[:, :tcw], ACTF.Copy,
                                            scale=self.sd_sb[:, hc:hc + 1]),
                               waits=[lastd, dn_free, dna_free if hc == 0 else None],
                               inc=(s_dev, 1), tag=f"fde{ci}_{hc}")
                    psD_free = dev
                    tt = None
                    for j in range(ntile_c):
                        tt = p.op("tensor", lambda e, j=j:
                                  e.transpose(psTr[:, j * 128:(j + 1) * 128],
                                              dn[:, j * 128:(j + 1) * 128], self.identf[:]),
                                  waits=[dev, psTr_free] if j == 0 else (),
                                  inc=(s_dtr, 1) if j == ntile_c - 1 else None,
                                  tag=f"fdt{ci}_{hc}_{j}")
                    dn_free = tt
                    ev = p.op("vector", lambda e, hc=hc, ntile_c=ntile_c:
                              e.tensor_copy(APv(dn_all, hc * 128, [[E, ntile_c], [1, 128]]),
                                            psTr[:, :ntile_c * 128]),
                              waits=[tt], inc=(s_dte, 1), tag=f"fte{ci}_{hc}")
                    psTr_free = ev
                dna_ready = psTr_free
                fuse_free = dmm_last
                ep_last = None
                for j in range(ntile_c):
                    gt = (tt0 // 128) + j
                    xb = x1t[j % 2]
                    xldt = p.dma("sync", xb[:], x1_scr[gt * 128:(gt + 1) * 128, :],
                                 waits=[x1_free[j % 2]], sem=s_x1[j % 2], tag=f"fx{ci}_{j}")
                    stats = self.ln_stats(bns, bno,
                                          lambda c, j=j: dn_all[:, j * E + c * 512:
                                                                j * E + (c + 1) * 512],
                                          E, [dna_ready], s_v, f"fb{ci}_{j}")
                    sq = p.op("scalar", lambda e: e.activation(std[:], bno[:, 1:2], ACTF.Sqrt,
                                                               bias=self.epsb[:]),
                              waits=[stats], inc=(s_s, 1), tag=f"fsq{ci}_{j}")
                    rct = p.op("vector", lambda e: e.reciprocal(rstd[:], std[:]), waits=[sq],
                               inc=(s_v, 1), tag=f"frc{ci}_{j}")
                    p.op("vector", lambda e, j=j: e.tensor_scalar(
                            ln4[:], dn_all[:, j * E:(j + 1) * E], bno[:, 0:1], rstd[:],
                            ALU.subtract, ALU.mult), waits=[rct], tag=f"fln4{ci}_{j}")
                    add = p.op("vector", lambda e, b=xb: e.tensor_add(outt[:], ln4[:], b[:]),
                               waits=[xldt, outt_free], inc=(s_v, 1), tag=f"fadd{ci}_{j}")
                    x1_free[j % 2] = add
                    od = p.dma("gpsimd", d["out"][gt * 128:(gt + 1) * 128, :], outt[:],
                               waits=[add], sem=self.out_sem, tag=f"fo{ci}_{j}")
                    outt_free = od
                    ep_last = od
                dna_free = ep_last


def own_row_index(cfg, par):
    rows = []
    for tau in range(max(1, cfg.NT // 2)):
        base = 512 * tau + 256 * par
        rows.extend(range(base, base + 256))
    return np.array(rows[:cfg.NTOK])


def rope_tables(T, Dh=128, base=1e6):
    inv = 1.0 / base ** (np.arange(0, Dh, 2, dtype=np.float32) / Dh)
    f = np.arange(T, dtype=np.float32)[:, None] * inv[None, :]
    cos = np.concatenate([np.cos(f), np.cos(f)], -1).astype(np.float32)
    sin = np.concatenate([np.sin(f), np.sin(f)], -1).astype(np.float32)
    return cos, sin


def stage_inputs(cfg, inputs, core):
    import ml_dtypes
    b, par = core // 2, core % 2
    x = np.ascontiguousarray(np.asarray(inputs["L0_attn_x"][b], np.float32))
    own_rows = own_row_index(cfg, par)
    m = {}
    m["x_seq"] = np.ascontiguousarray(x[:cfg.MTOK])
    m["x_own"] = np.ascontiguousarray(x[own_rows])
    m["wq"] = np.ascontiguousarray(np.asarray(inputs["L0_attn_wq"], np.float32))
    m["wk"] = np.ascontiguousarray(np.asarray(inputs["L0_attn_wk"], np.float32))
    m["wv"] = np.ascontiguousarray(np.asarray(inputs["L0_attn_wv"], np.float32))
    m["wo"] = np.ascontiguousarray(np.asarray(inputs["L0_attn_wo"], np.float32))
    m["wg8"] = np.ascontiguousarray(np.asarray(inputs["L0_ffn_w_gate"], np.int8)[:, :cfg.INTER])
    m["wu8"] = np.ascontiguousarray(np.asarray(inputs["L0_ffn_w_up"], np.int8)[:, :cfg.INTER])
    m["wd8"] = np.ascontiguousarray(np.asarray(inputs["L0_ffn_w_down"], np.int8)[:cfg.INTER])
    m["sg"] = np.ascontiguousarray(
        np.asarray(inputs["L0_ffn_s_gate"], np.float32)[:cfg.INTER].reshape(cfg.OC, 128).T)
    m["su"] = np.ascontiguousarray(
        np.asarray(inputs["L0_ffn_s_up"], np.float32)[:cfg.INTER].reshape(cfg.OC, 128).T)
    m["sd"] = np.ascontiguousarray(
        np.asarray(inputs["L0_ffn_s_down"], np.float32).reshape(cfg.EC, 128).T)
    m["ident"] = np.eye(128, dtype=np.float32).astype(ml_dtypes.bfloat16)
    m["identf"] = np.eye(128, dtype=np.float32)
    m["epsb"] = np.full((128, 1), 1e-5, np.float32)
    masks = np.zeros((3, 128, 128), np.float32)
    triln = np.triu(np.full((128, 128), -1e9, np.float32), 1)
    if par == 0:
        masks[0] = triln
        masks[1] = -1e9
        masks[2] = -1e9
    else:
        masks[2] = triln
    m["masks"] = masks.astype(ml_dtypes.bfloat16)
    cos, sin = rope_tables(2048)
    sin_signed = sin.copy()
    sin_signed[:, :64] = -sin[:, :64]
    cos_rep = np.tile(cos, (1, 4))
    sin_rep = np.tile(sin_signed, (1, 4))
    m["cos_all"] = np.ascontiguousarray(cos_rep[:cfg.MTOK])
    m["sin_all"] = np.ascontiguousarray(sin_rep[:cfg.MTOK])
    m["cos_own"] = np.ascontiguousarray(cos_rep[own_rows])
    m["sin_own"] = np.ascontiguousarray(sin_rep[own_rows])
    return m


def kernel(**inputs):
    from concourse.bass_utils import run_bass_kernel_spmd
    cfg = Cfg()
    in_maps = [stage_inputs(cfg, inputs, c) for c in range(8)]
    b = Builder(cfg)
    nc = b.build()
    res = run_bass_kernel_spmd(nc, in_maps, core_ids=list(range(8)))
    x = np.asarray(inputs["L0_attn_x"], np.float32)
    out = np.empty_like(x)
    for c in range(8):
        bq, par = c // 2, c % 2
        out[bq][own_row_index(cfg, par)] = np.asarray(res.results[c]["out"],
                                                     np.float32)
    return out


# version 14 (final)
# speedup vs baseline: 1.1037x; 1.0621x over previous
import numpy as np
from contextlib import ExitStack
import concourse.bass as bass
import concourse.mybir as mybir

F32 = mybir.dt.float32
BF16 = mybir.dt.bfloat16
I8 = mybir.dt.int8
AX = mybir.AxisListType.X
ALU = mybir.AluOpType
ACTF = mybir.ActivationFunctionType

E = 2048
D = 128
SCALE = float(1.0 / np.sqrt(np.float32(D)))


def _base_ap(t):
    if isinstance(t, bass.AP):
        return t
    return t[:]


def APv(t, off, dims):
    a = _base_ap(t)
    return bass.AP(a.tensor, a.offset + off, [list(a.ap[0])] + dims)


def APd(t, off, dims):
    a = _base_ap(t)
    return bass.AP(a.tensor, a.offset + off, dims)


class Cfg:
    def __init__(self, nt=8, mt=16, fb=4, oc=64, ffn_tc=256, debug=False):
        self.debug = debug
        self.NT = nt
        self.MT = mt
        self.FB = fb
        self.OC = oc
        self.TC = ffn_tc
        self.EC = 16
        self.heads = 4 * fb
        self.NTOK = nt * 128
        self.MTOK = mt * 128
        self.INTER = oc * 128

    def Jtile(self, qt):
        return min(4 * (qt // 2) + (qt % 2) + 2, self.MT - 1)


class Sem:
    __slots__ = ("h", "n", "name")
    def __init__(self, h, name):
        self.h = h; self.n = 0; self.name = name
    @property
    def tok(self):
        return (self, self.n)


def flat(*toks):
    out = []
    for t in toks:
        if t is None:
            continue
        if isinstance(t, list):
            out.extend(x for x in flat(*t))
        else:
            out.append(t)
    return out


class Plan:
    def __init__(self):
        self.ops = []

    def op(self, engine, fn, waits=(), inc=None, tag="", is_dma=False):
        ws = flat(*waits)
        best = {}
        for (s, v) in ws:
            if v > 0 and best.get(s, 0) < v:
                best[s] = v
        ws = list(best.items())
        rec_inc = None
        if inc is not None:
            inc[0].n += inc[1]
            rec_inc = (inc[0], inc[1], inc[0].n)
        self.ops.append((engine, fn, ws, rec_inc, tag, is_dma))
        return None if inc is None else (inc[0], inc[0].n)

    def dma(self, engine, dst, src, waits=(), sem=None, tag=""):
        return self.op(engine, lambda e, d=dst, s=src: e.dma_start(d, s),
                       waits=waits, inc=(sem, 16), tag=tag, is_dma=True)

    def barrier(self, engine, waits, tag="bar"):
        ws = flat(*waits)
        best = {}
        for (s, v) in ws:
            if v > 0 and best.get(s, 0) < v:
                best[s] = v
        self.ops.append((engine, None, list(best.items()), None, tag, False))

    def emit(self, block, chain_sems=None):
        per = {}
        for rec in self.ops:
            per.setdefault(rec[0], []).append(rec)
        for name in ("sync", "tensor", "vector", "scalar", "gpsimd"):
            if name not in per:
                continue
            records = per[name]
            chain = chain_sems.get(name) if chain_sems else None
            def body(eng, records=records, chain=chain):
                seen = {}
                cnt = 0
                last_inc = None
                for (_, fn, waits, inc, tag, is_dma) in records:
                    for (s, v) in waits:
                        if seen.get(s, 0) < v:
                            eng.wait_ge(s.h, v)
                            seen[s] = v
                    if fn is None:
                        continue
                    if chain is not None:
                        if cnt > 0:
                            eng.wait_ge(chain, cnt)
                        if last_inc is not None:
                            eng.wait_ge(*last_inc)
                    ins = fn(eng)
                    if ins is not None:
                        if inc is not None:
                            ins.then_inc(inc[0].h, inc[1])
                            if chain is not None and not is_dma:
                                last_inc = (inc[0].h, inc[2])
                        elif chain is not None and not is_dma:
                            ins.then_inc(chain, 1)
                            cnt += 1
                    else:
                        assert inc is None, f"op {tag} must return an instruction"
            getattr(block, name)(body)


class Builder:
    def __init__(self, cfg: Cfg, simsafe=False):
        self.cfg = cfg
        self.simsafe = simsafe
        self.nc = bass.Bass("TRN2", target_bir_lowering=False,
                            detect_race_conditions=simsafe)
        self.p = Plan()
        self.stack = ExitStack()
        self._semid = 0

    def sem(self, name):
        if not hasattr(self, "_sems"):
            self._sems = {}
        if name not in self._sems:
            self._semid += 1
            h = self.stack.enter_context(self.nc.semaphore(f"s{self._semid}_{name}"))
            self._sems[name] = Sem(h, name)
        return self._sems[name]

    def build(self):
        cfg, nc, p = self.cfg, self.nc, self.p
        NTOK, MTOK, INTER = cfg.NTOK, cfg.MTOK, cfg.INTER
        HD, EC, OC = cfg.heads, cfg.EC, cfg.OC

        d = {}
        d["x_seq"] = nc.dram_tensor("x_seq", [MTOK, E], F32, kind="ExternalInput").ap()
        d["x_own"] = nc.dram_tensor("x_own", [NTOK, E], F32, kind="ExternalInput").ap()
        for w in ("wq", "wk", "wv", "wo"):
            d[w] = nc.dram_tensor(w, [E, E], F32, kind="ExternalInput").ap()
        d["wg8"] = nc.dram_tensor("wg8", [E, INTER], I8, kind="ExternalInput").ap()
        d["wu8"] = nc.dram_tensor("wu8", [E, INTER], I8, kind="ExternalInput").ap()
        d["wd8"] = nc.dram_tensor("wd8", [INTER, E], I8, kind="ExternalInput").ap()
        d["sg"] = nc.dram_tensor("sg", [128, OC], F32, kind="ExternalInput").ap()
        d["su"] = nc.dram_tensor("su", [128, OC], F32, kind="ExternalInput").ap()
        d["sd"] = nc.dram_tensor("sd", [128, EC], F32, kind="ExternalInput").ap()
        d["ident"] = nc.dram_tensor("ident", [128, 128], BF16, kind="ExternalInput").ap()
        d["identf"] = nc.dram_tensor("identf", [128, 128], F32, kind="ExternalInput").ap()
        d["epsb"] = nc.dram_tensor("epsb", [128, 1], F32, kind="ExternalInput").ap()
        d["masks"] = nc.dram_tensor("masks", [3, 128, 128], BF16, kind="ExternalInput").ap()
        d["cos_own"] = nc.dram_tensor("cos_own", [NTOK, 512], F32, kind="ExternalInput").ap()
        d["sin_own"] = nc.dram_tensor("sin_own", [NTOK, 512], F32, kind="ExternalInput").ap()
        d["cos_all"] = nc.dram_tensor("cos_all", [MTOK, 512], F32, kind="ExternalInput").ap()
        d["sin_all"] = nc.dram_tensor("sin_all", [MTOK, 512], F32, kind="ExternalInput").ap()
        d["out"] = nc.dram_tensor("out", [NTOK, E], F32, kind="ExternalOutput").ap()

        kd = "ExternalOutput" if cfg.debug else "Internal"
        q_hi = nc.dram_tensor("q_scr_hi", [HD, 128, NTOK], BF16, kind=kd).ap()
        q_lo = nc.dram_tensor("q_scr_lo", [HD, 128, NTOK], BF16, kind=kd).ap()
        ao_scr = nc.dram_tensor("ao_scr", [HD, 128, NTOK], F32, kind=kd).ap()
        o2_scr = nc.dram_tensor("o2_scr", [NTOK, E], F32, kind=kd).ap()
        x1_scr = nc.dram_tensor("x1_scr", [NTOK, E], F32, kind=kd).ap()
        lnf_hi = nc.dram_tensor("lnf_hi", [EC, 128, NTOK], BF16, kind=kd).ap()
        lnf_lo = nc.dram_tensor("lnf_lo", [EC, 128, NTOK], BF16, kind=kd).ap()
        wg_scr = nc.dram_tensor("wg_scr", [OC, EC, 128, 128], BF16, kind=kd).ap()
        wu_scr = nc.dram_tensor("wu_scr", [OC, EC, 128, 128], BF16, kind=kd).ap()
        wd_scr = nc.dram_tensor("wd_scr", [EC, OC, 128, 128], BF16, kind=kd).ap()
        self.d = d

        sb = self.stack.enter_context
        self.ident = sb(nc.sbuf_tensor("c_ident", [128, 128], BF16))
        self.identf = sb(nc.sbuf_tensor("c_identf", [128, 128], F32))
        self.epsb = sb(nc.sbuf_tensor("c_eps", [128, 1], F32))
        self.masks = sb(nc.sbuf_tensor("c_masks", [128, 3 * 128], BF16))
        self.sg_sb = sb(nc.sbuf_tensor("c_sg", [128, OC], F32))
        self.su_sb = sb(nc.sbuf_tensor("c_su", [128, OC], F32))
        self.sd_sb = sb(nc.sbuf_tensor("c_sd", [128, EC], F32))
        ld0 = self.sem("ld0")
        p.dma("sync", self.ident[:], d["ident"][:], sem=ld0)
        p.dma("sync", self.identf[:], d["identf"][:], sem=ld0)
        p.dma("sync", self.epsb[:], d["epsb"][:], sem=ld0)
        p.dma("sync", self.masks[:],
              APd(d["masks"], 0, [[128, 128], [128 * 128, 3], [1, 128]]), sem=ld0)
        p.dma("sync", self.sg_sb[:], d["sg"][:], sem=ld0)
        p.dma("sync", self.su_sb[:], d["su"][:], sem=ld0)
        p.dma("sync", self.sd_sb[:], d["sd"][:], sem=ld0)
        self.consts_ready = ld0.tok
        self.out_sem = self.sem("outd")

        KSEG = 4 * 128 * NTOK
        VSEG = 4 * (cfg.NT) * 128 * 256
        FBSEG = 2 * KSEG + VSEG
        kv_in = [nc.dram_tensor(f"kvin{f}", [FBSEG], BF16).ap()
                 for f in range(cfg.FB)]
        kv_out = [nc.dram_tensor(f"kvout{f}", [2, FBSEG], BF16).ap()
                  for f in range(cfg.FB)]
        self.kv_geom = (KSEG, VSEG, FBSEG)
        lno_hi = nc.dram_tensor("lno_hi", [EC, 128, NTOK], BF16).ap()
        lno_lo = nc.dram_tensor("lno_lo", [EC, 128, NTOK], BF16).ap()
        with ExitStack() as ph:
            ln = self.ln_phase(ph, d["x_own"], cfg.NT, "own", self.consts_ready)
            s_lno = self.sem("lno")
            for half, (slab, dst) in enumerate([(ln[0], lno_hi), (ln[1], lno_lo)]):
                for ec in range(EC):
                    p.dma("gpsimd", APd(dst, ec * 128 * NTOK, [[NTOK, 128], [1, NTOK]]),
                          slab[:, ec * NTOK:(ec + 1) * NTOK],
                          waits=[ln[3]], sem=s_lno, tag=f"lno{half}_{ec}")
            lno_done = s_lno.tok
            self.q_done = self.qkv_phase(ph, ln, d["wq"], cfg.NT, "own",
                                         (q_hi, q_lo), "q")
        fc_stack = ExitStack()
        fc_chunks = self.ffn_weight_cache_mk(fc_stack, d, wg_scr, wu_scr, wd_scr,
                                             flat(self.q_done, lno_done))
        self.attn_merged(d, (lno_hi, lno_lo, NTOK, flat(lno_done, self.q_done)),
                         kv_in, kv_out, q_hi, q_lo, ao_scr, fc_chunks)
        for fn in fc_chunks:
            fn()
        fc_chunks.clear()
        self.fc_finalize()
        fc_stack.close()
        self.oproj_phase(d, ao_scr, o2_scr, x1_scr, lnf_hi, lnf_lo)
        self.ffn_phase(d, lnf_hi, lnf_lo, wg_scr, wu_scr, wd_scr, x1_scr)

        p.barrier("sync", [self.out_sem.tok], tag="final")
        chain_sems = None
        if self.simsafe:
            chain_sems = {}
            for name in ("sync", "tensor", "vector", "scalar", "gpsimd"):
                chain_sems[name] = self.stack.enter_context(
                    nc.semaphore(f"chain_{name}"))
        with nc.Block() as block:
            p.emit(block, chain_sems)
        self.stack.close()
        return nc


    def ln_stats(self, bns, bno, src_fn, width, waits, s_v, tag):
        p = self.p
        nch = (width + 511) // 512
        st = None
        for c in range(nch):
            st = p.op("vector", lambda e, c=c: e.bn_stats(bns[:, c * 6:(c + 1) * 6], src_fn(c)),
                      waits=waits if c == 0 else (),
                      inc=(s_v, 1) if c == nch - 1 else None, tag=f"{tag}bs{c}")
        return p.op("vector", lambda e, n=nch: e.bn_aggr(bno[:], bns[:, :6 * n]),
                    waits=[st], inc=(s_v, 1), tag=f"{tag}ba")

    def ln_phase(self, ph, x_dram, ntiles, name, start_tok, to_hbm=None,
                 as_chunks=False):
        nc, p, cfg = self.nc, self.p, self.cfg
        ntok = ntiles * 128
        EC = cfg.EC
        if to_hbm is None:
            slab_hi = ph.enter_context(nc.sbuf_tensor(f"ln_{name}_hi", [128, EC * ntok], BF16))
            slab_lo = ph.enter_context(nc.sbuf_tensor(f"ln_{name}_lo", [128, EC * ntok], BF16))
        sc = ExitStack() if not as_chunks else None
        with (sc if sc is not None else ExitStack()) as _sc:
            s = (_sc.enter_context if not as_chunks else ph.enter_context)
            if to_hbm is not None:
                stg = [s(nc.sbuf_tensor(f"lnstg{name}{i}", [128, 512], BF16))
                       for i in range(2)]
                s_sd = [self.sem(f"ln_sd{i}") for i in range(2)]
                stg_free = [start_tok, start_tok]
            xt = [s(nc.sbuf_tensor(f"lnx{name}{i}", [128, E], F32)) for i in range(2)]
            ln32 = s(nc.sbuf_tensor(f"ln32{name}", [128, E], F32))
            lhi = s(nc.sbuf_tensor(f"lnhi{name}", [128, E], BF16))
            llo = s(nc.sbuf_tensor(f"lnlo{name}", [128, E], BF16))
            bns = s(nc.sbuf_tensor(f"bns{name}", [128, 24], F32))
            bno = s(nc.sbuf_tensor(f"bno{name}", [128, 2], F32))
            std = s(nc.sbuf_tensor(f"std{name}", [128, 1], F32))
            rstd = s(nc.sbuf_tensor(f"rstd{name}", [128, 1], F32))
            trp = [s(nc.psum_tensor(f"lntr{name}{i}", [128, 1024], BF16)) for i in range(2)]
            s_ld = [self.sem(f"ln_ld{i}") for i in range(2)]
            s_v = self.sem("ln_v")
            s_s = self.sem("ln_s")
            s_t = self.sem("ln_t")
            s_e = self.sem("ln_e")
            xt_free = [start_tok, start_tok]
            bank_free = [start_tok, start_tok]

            def emit_tile(i):
                ldt = p.dma("sync", xt[i % 2][:], x_dram[i * 128:(i + 1) * 128, :],
                            waits=[xt_free[i % 2]], sem=s_ld[i % 2], tag=f"ln{name}x{i}")
                stats = self.ln_stats(bns, bno,
                                      lambda c, b=xt[i % 2]: b[:, c * 512:(c + 1) * 512],
                                      E, [ldt, start_tok], s_v, f"ln{name}{i}")
                sq = p.op("scalar", lambda e: e.activation(std[:], bno[:, 1:2], ACTF.Sqrt,
                                                           bias=self.epsb[:]),
                          waits=[stats, start_tok], inc=(s_s, 1), tag=f"ln{name}sq{i}")
                rct = p.op("vector", lambda e: e.reciprocal(rstd[:], std[:]), waits=[sq],
                           inc=(s_v, 1), tag=f"ln{name}rc{i}")
                apl = p.op("vector", lambda e, b=xt[i % 2]: e.tensor_scalar(
                        ln32[:], b[:], bno[:, 0:1], rstd[:], ALU.subtract, ALU.mult),
                     waits=[rct], inc=(s_v, 1), tag=f"ln{name}ap{i}")
                xt_free[i % 2] = apl
                p.op("vector", lambda e: e.tensor_copy(lhi[:], ln32[:]), tag=f"ln{name}h{i}")
                spl = p.op("vector", lambda e: e.tensor_sub(llo[:], ln32[:], lhi[:]),
                           inc=(s_v, 1), tag=f"ln{name}l{i}")
                gidx = 0
                if to_hbm is None:
                    dests = [(lhi, slab_hi), (llo, slab_lo)]
                else:
                    dests = [(lhi, to_hbm[0]), (llo, to_hbm[1])]
                for half, (src, dslab) in enumerate(dests):
                    for g in range(EC // 4):
                        bi = (i * 2 * (EC // 4) + gidx) % 2
                        bank = trp[bi]
                        bf = bank_free[bi]
                        gidx += 1
                        tt = None
                        for j in range(4):
                            ec = g * 4 + j
                            tt = p.op("tensor", lambda e, s2=src, b=bank, ec=ec, j=j:
                                      e.transpose(b[:, j * 128:(j + 1) * 128],
                                                  s2[:, ec * 128:(ec + 1) * 128],
                                                  self.ident[:]),
                                      waits=[spl, bf] if j == 0 else (),
                                      inc=(s_t, 1) if j == 3 else None,
                                      tag=f"ln{name}t{i}_{half}_{g}_{j}")
                        if to_hbm is None:
                            base = (g * 4) * ntok + i * 128
                            ev = p.op("scalar",
                                      lambda e, b=bank, ds=dslab, base=base, ntok=ntok:
                                      e.copy(APv(ds, base, [[ntok, 4], [1, 128]]),
                                             b[:, :512]),
                                      waits=[tt], inc=(s_e, 1),
                                      tag=f"ln{name}e{i}_{half}_{g}")
                            bank_free[bi] = ev
                        else:
                            sb2 = stg[bi]
                            ev = p.op("scalar",
                                      lambda e, b=bank, sb2=sb2:
                                      e.copy(sb2[:], b[:, :512]),
                                      waits=[tt, stg_free[bi]], inc=(s_e, 1),
                                      tag=f"ln{name}e{i}_{half}_{g}")
                            dmat = p.dma("gpsimd",
                                         APd(dslab, (g * 4) * 128 * ntok + i * 128,
                                             [[ntok, 128], [ntok * 128, 4], [1, 128]]),
                                         APv(sb2, 0, [[128, 4], [1, 128]]),
                                         waits=[ev], sem=s_sd[bi],
                                         tag=f"ln{name}sd{i}_{half}_{g}")
                            stg_free[bi] = dmat
                            bank_free[bi] = ev
            if as_chunks:
                chunks = [lambda i=i: emit_tile(i) for i in range(ntiles)]

                def done_fn():
                    return [(s_sd[0], s_sd[0].n), (s_sd[1], s_sd[1].n)]
                return (chunks, done_fn)
            for i in range(ntiles):
                emit_tile(i)
            if to_hbm is not None:
                ln_done = [(s_sd[0], s_sd[0].n), (s_sd[1], s_sd[1].n)]
                return (to_hbm[0], to_hbm[1], ntok, ln_done)
            ln_done = (s_e, s_e.n)
        return (slab_hi, slab_lo, ntok, ln_done)

    def wT_prep(self, o, w_dram, fb, tag):
        nc, p, cfg = self.nc, self.p, self.cfg
        EC = cfg.EC
        ev = None
        for sub in range(4):
            f0 = fb * 512 + sub * 128
            buf = o["wrow"][sub % 2]
            w = [o["wrow_free"][sub % 2]]
            if sub == 0:
                w.append(o.get("slab_free"))
            ldt = p.dma("sync", buf[:], w_dram[f0:f0 + 128, :], waits=w,
                        sem=o["s_wld"][sub % 2], tag=f"{tag}wld{fb}_{sub}")
            p.op("vector", lambda e, b=buf: e.tensor_copy(o["wrhi"][:], b[:]),
                 waits=[ldt, o.get("vec_start"), o.get("wsplit_free")],
                 tag=f"{tag}wh{fb}{sub}")
            spl = p.op("vector", lambda e, b=buf: e.tensor_sub(o["wrlo"][:], b[:], o["wrhi"][:]),
                       inc=(o["s_wv"], 1), tag=f"{tag}wl{fb}{sub}")
            o["wrow_free"][sub % 2] = spl
            if o.get("wrow_single"):
                o["wrow_free"][(sub + 1) % 2] = spl
            o["vec_start"] = None
            gidx = 0
            last_tt = [None, None]
            for half, (src, dslab) in enumerate([(o["wrhi"], o["wT"][0]),
                                                 (o["wrlo"], o["wT"][1])]):
                for g in range(EC // 4):
                    bi = gidx % 2
                    bank = o["trp"][bi]
                    bf = o["bank_free"][bi]
                    gidx += 1
                    tt = None
                    for j in range(4):
                        ec = g * 4 + j
                        tt = p.op("tensor", lambda e, s2=src, b=bank, ec=ec, j=j:
                                  e.transpose(b[:, j * 128:(j + 1) * 128],
                                              s2[:, ec * 128:(ec + 1) * 128],
                                              self.ident[:]),
                                  waits=[spl, bf] if j == 0 else (),
                                  inc=(o["s_wt"], 1) if (j == 3 or g * 4 + j == EC - 1)
                                      and j == 3 else None,
                                  tag=f"{tag}wt{fb}{sub}{half}{g}{j}")
                    base = (g * 4) * 512 + sub * 128
                    ev = p.op("scalar", lambda e, b=bank, ds=dslab, base=base:
                              e.copy(APv(ds, base, [[512, 4], [1, 128]]), b[:, :512]),
                              waits=[tt], inc=(o["s_we"], 1),
                              tag=f"{tag}we{fb}{sub}{half}{g}")
                    o["bank_free"][bi] = ev
                    if o.get("trp_single"):
                        o["bank_free"][1 - bi] = ev
                    last_tt[half] = tt
            o["wsplit_free"] = [last_tt[0], last_tt[1]]
        return ev

    def qkv_phase(self, ph, ln, w_dram, ntiles, rope, out, kind, prev=None,
                  interleave=None):
        nc, p, cfg = self.nc, self.p, self.cfg
        slab_hi, slab_lo, ntok, ln_done = ln
        ln_done = flat(ln_done, prev)
        EC, FB, MT = cfg.EC, cfg.FB, cfg.MT
        with ExitStack() as sc:
            s = sc.enter_context
            o = {}
            o["wrow"] = [s(nc.sbuf_tensor(f"{kind}wr{i}", [128, E], F32)) for i in range(2)]
            o["wrhi"] = s(nc.sbuf_tensor(f"{kind}wrh", [128, E], BF16))
            o["wrlo"] = s(nc.sbuf_tensor(f"{kind}wrl", [128, E], BF16))
            o["wT"] = [s(nc.sbuf_tensor(f"{kind}wT{h}", [128, EC * 512], BF16))
                       for h in range(2)]
            o["trp"] = [s(nc.psum_tensor(f"{kind}tp{i}", [128, 1024], BF16)) for i in range(2)]
            mm = [s(nc.psum_tensor(f"{kind}mm{i}", [128, 512], F32)) for i in range(2)]
            q32 = s(nc.sbuf_tensor(f"{kind}q32", [128, 512], F32))
            qc = s(nc.sbuf_tensor(f"{kind}qc", [128, 512], F32))
            qhib2 = [s(nc.sbuf_tensor(f"{kind}hib{i}", [128, 512], BF16)) for i in range(2)]
            qlob2 = [s(nc.sbuf_tensor(f"{kind}lob{i}", [128, 512], BF16)) for i in range(2)]
            ebuf = [s(nc.sbuf_tensor(f"{kind}eb{i}", [128, 512], BF16)) for i in range(2)]
            if rope:
                costl = [s(nc.sbuf_tensor(f"{kind}cs{i}", [128, 512], F32)) for i in range(2)]
                sintl = [s(nc.sbuf_tensor(f"{kind}sn{i}", [128, 512], F32)) for i in range(2)]
            else:
                vint = s(nc.sbuf_tensor("vint", [128, 1024], BF16))
            o["s_wld"] = [self.sem(f"pj_wld{i}") for i in range(2)]
            o["s_wv"] = self.sem("pj_wv")
            o["s_wt"] = self.sem("pj_wt")
            o["s_we"] = self.sem("pj_we")
            s_mm = self.sem("pj_mm")
            s_rv = self.sem("pj_rv")
            s_tt = self.sem("pj_tt")
            s_ee = self.sem("pj_ee")
            s_do = self.sem("pj_do")
            s_dos = [self.sem(f"pj_dos{i}") for i in range(2)]
            s_cs = [self.sem(f"pj_cs{i}") for i in range(2)]
            o["wrow_free"] = [ln_done, ln_done]
            o["bank_free"] = [ln_done, ln_done]
            o["vec_start"] = ln_done
            o["slab_free"] = None

            if rope:
                cos_d = self.d["cos_own" if rope == "own" else "cos_all"]
                sin_d = self.d["sin_own" if rope == "own" else "sin_all"]

            mm_free = [ln_done, ln_done]
            cs_free = [ln_done, ln_done]
            ebuf_free = [ln_done, ln_done]
            vint_free = ln_done
            pending_qkT = None
            qsplit_free = [None, None]
            for fb in range(FB):
                wready = self.wT_prep(o, w_dram, fb, kind)
                last_mm_of_fb = None
                for i in range(ntiles):
                    bank = mm[i % 2]
                    waits = [wready, ln_done, mm_free[i % 2]]
                    for ec in range(EC):
                        c0 = ec * ntok + i * 128
                        p.op("tensor", lambda e, b=bank, c0=c0, ec=ec, wt=o["wT"][0]:
                             e.matmul(b[:], slab_hi[:, c0:c0 + 128],
                                      wt[:, ec * 512:(ec + 1) * 512],
                                      start=(ec == 0), stop=False),
                             waits=waits if ec == 0 else (), tag=f"{kind}m{fb}_{i}_{ec}a")
                        p.op("tensor", lambda e, b=bank, c0=c0, ec=ec, wt=o["wT"][1]:
                             e.matmul(b[:], slab_hi[:, c0:c0 + 128],
                                      wt[:, ec * 512:(ec + 1) * 512],
                                      start=False, stop=False), tag=f"{kind}m{fb}_{i}_{ec}b")
                        mmt = p.op("tensor", lambda e, b=bank, c0=c0, ec=ec, wt=o["wT"][0]:
                             e.matmul(b[:], slab_lo[:, c0:c0 + 128],
                                      wt[:, ec * 512:(ec + 1) * 512],
                                      start=False, stop=(ec == EC - 1)),
                             inc=(s_mm, 1) if ec == EC - 1 else None,
                             tag=f"{kind}m{fb}_{i}_{ec}c")
                    last_mm_of_fb = mmt
                    if rope:
                        cb, sb_ = costl[i % 2], sintl[i % 2]
                        ct = p.dma("sync", cb[:], cos_d[i * 128:(i + 1) * 128, :],
                                   waits=[cs_free[i % 2]], sem=s_cs[i % 2], tag=f"{kind}c{fb}{i}")
                        st_ = p.dma("sync", sb_[:], sin_d[i * 128:(i + 1) * 128, :],
                                    sem=s_cs[i % 2], tag=f"{kind}s{fb}{i}")
                        p.op("vector", lambda e, b=bank, cb=cb: e.tensor_mul(qc[:], b[:], cb[:]),
                             waits=[mmt, ct, st_], tag=f"{kind}r1{fb}_{i}")
                        p.op("vector", lambda e, b=bank, sb2=sb_:
                             e.tensor_mul(q32[:],
                                          APd(b, 64, [b[:].ap[0], [128, 4], [-64, 2], [1, 64]]),
                                          sb2[:]),
                             tag=f"{kind}r2{fb}_{i}")
                        ropet = p.op("vector", lambda e: e.tensor_add(q32[:], q32[:], qc[:]),
                                     inc=(s_rv, 1), tag=f"{kind}r3{fb}_{i}")
                        mm_free[i % 2] = ropet
                        cs_free[i % 2] = ropet
                        qp = i % 2
                        qhib, qlob = qhib2[qp], qlob2[qp]
                        p.op("vector", lambda e, qhib=qhib: e.tensor_copy(qhib[:], q32[:]),
                             waits=[qsplit_free[qp]], tag=f"{kind}p1{fb}_{i}")
                        spl = p.op("vector",
                                   lambda e, qhib=qhib, qlob=qlob:
                                   e.tensor_sub(qlob[:], q32[:], qhib[:]),
                                   inc=(s_rv, 1), tag=f"{kind}p2{fb}_{i}")

                        def emit_qkT(fb=fb, i=i, spl=spl, qhib=qhib, qlob=qlob, qp=qp):
                            last_tt2 = None
                            for half, src in enumerate([qhib, qlob]):
                                bank_t = o["trp"][half]
                                bf = o["bank_free"][half]
                                tt = None
                                for hh in range(4):
                                    tt = p.op("tensor", lambda e, s2=src, b=bank_t, hh=hh:
                                              e.transpose(b[:, hh * 128:(hh + 1) * 128],
                                                          s2[:, hh * 128:(hh + 1) * 128],
                                                          self.ident[:]),
                                              waits=[spl, bf] if hh == 0 else (),
                                              inc=(s_tt, 1) if hh == 3 else None,
                                              tag=f"{kind}tt{fb}{i}{half}{hh}")
                                eb = ebuf[half]
                                ev = p.op("scalar", lambda e, b=bank_t, eb=eb:
                                          e.copy(eb[:], b[:, :512]),
                                          waits=[tt, ebuf_free[half]], inc=(s_ee, 1),
                                          tag=f"{kind}te{fb}{i}{half}")
                                o["bank_free"][half] = ev
                                dst = out[half]
                                dmat = p.dma("gpsimd",
                                             APd(dst, (fb * 4) * 128 * ntok + i * 128,
                                                 [[ntok, 128], [128 * ntok, 4], [1, 128]]),
                                             APv(eb, 0, [[128, 4], [1, 128]]),
                                             waits=[ev], sem=s_dos[half],
                                             tag=f"{kind}d{fb}_{i}_{half}")
                                ebuf_free[half] = dmat
                                last_tt2 = tt
                            qsplit_free[qp] = last_tt2

                        if pending_qkT is not None:
                            pending_qkT()
                        pending_qkT = emit_qkT
                        if interleave:
                            interleave.pop(0)()
                    else:
                        vw = [mmt, vint_free]
                        p.op("vector", lambda e, b=bank:
                             e.tensor_copy(APv(vint, 0, [[256, 4], [1, 128]]), b[:]),
                             waits=vw, tag=f"vp1{fb}_{i}")
                        spl = p.op("vector", lambda e, b=bank:
                                   e.tensor_sub(APv(vint, 128, [[256, 4], [1, 128]]), b[:],
                                                APv(vint, 0, [[256, 4], [1, 128]])),
                                   inc=(s_rv, 1), tag=f"vp2{fb}_{i}")
                        mm_free[i % 2] = spl
                        vint_free = p.dma("gpsimd",
                              APd(out[0], ((fb * 4) * MT + i) * 128 * 256,
                                  [[256, 128], [MT * 128 * 256, 4], [1, 256]]),
                              vint[:], waits=[spl], sem=s_do, tag=f"vd{fb}_{i}")
                if pending_qkT is not None:
                    pending_qkT()
                    pending_qkT = None
                o["slab_free"] = last_mm_of_fb
            while interleave:
                interleave.pop(0)()
            return [(s_do, s_do.n), (s_dos[0], s_dos[0].n), (s_dos[1], s_dos[1].n)]

    def kv_setup(self, sc, ln, d, kv_in, kv_out, prev):
        nc, p, cfg = self.nc, self.p, self.cfg
        EC = cfg.EC
        s = sc.enter_context
        slab_hi, slab_lo, ntok, ln_done = ln
        st = {}
        o = {}
        wr0 = s(nc.sbuf_tensor("kvwr0", [128, E], F32))
        o["wrow"] = [wr0, wr0]
        o["wrow_single"] = True
        o["wrhi"] = s(nc.sbuf_tensor("kvwrh", [128, E], BF16))
        o["wrlo"] = s(nc.sbuf_tensor("kvwrl", [128, E], BF16))
        o["wT"] = [s(nc.sbuf_tensor(f"kvwT{h}", [128, EC * 512], BF16)) for h in range(2)]
        kvtp0 = s(nc.psum_tensor("kvtp0", [128, 1024], BF16))
        o["trp"] = [kvtp0, kvtp0]
        o["trp_single"] = True
        o["s_wld"] = [self.sem(f"pj_wld{i}") for i in range(2)]
        o["s_wv"] = self.sem("pj_wv")
        o["s_wt"] = self.sem("pj_wt")
        o["s_we"] = self.sem("pj_we")
        start = flat(prev, ln_done)
        o["wrow_free"] = [start, start]
        o["bank_free"] = [start, start]
        o["vec_start"] = start
        o["slab_free"] = None
        st["o"] = o
        st["ln"] = (slab_hi, slab_lo, ntok)
        st["lnb_hi"] = [s(nc.sbuf_tensor(f"kvlh{i}", [128, EC * 128], BF16))
                        for i in range(2)]
        st["lnb_lo"] = [s(nc.sbuf_tensor(f"kvll{i}", [128, EC * 128], BF16))
                        for i in range(2)]
        st["s_lnb"] = [self.sem(f"pj_lnb{i}") for i in range(2)]
        st["lnb_free"] = [start, start]
        st["mm"] = [s(nc.psum_tensor(f"kvmm{i}", [128, 512], F32)) for i in range(2)]
        st["q32"] = s(nc.sbuf_tensor("kvq32", [128, 512], F32))
        st["qc"] = s(nc.sbuf_tensor("kvqc", [128, 512], F32))
        st["qhib"] = [s(nc.sbuf_tensor(f"kvhib{i}", [128, 512], BF16)) for i in range(2)]
        st["qlob"] = [s(nc.sbuf_tensor(f"kvlob{i}", [128, 512], BF16)) for i in range(2)]
        cs0 = s(nc.sbuf_tensor("kvcs0", [128, 512], F32))
        sn0 = s(nc.sbuf_tensor("kvsn0", [128, 512], F32))
        st["cs"] = [cs0, cs0]
        st["sn"] = [sn0, sn0]
        st["ebuf"] = [s(nc.sbuf_tensor(f"kveb{i}", [128, 512], BF16)) for i in range(2)]
        st["vint"] = s(nc.sbuf_tensor("kvvint", [128, 1024], BF16))
        st["s_mm"] = self.sem("pj_mm")
        st["s_rv"] = self.sem("pj_rv")
        st["s_tt"] = self.sem("pj_tt")
        st["s_ee"] = self.sem("pj_ee")
        st["s_do"] = self.sem("pj_do")
        st["s_dos"] = [self.sem(f"pj_dos{i}") for i in range(2)]
        st["s_cc"] = self.sem("pj_cc")
        st["mm_free"] = [start, start]
        st["cs_free"] = [start, start]
        st["ebuf_free"] = [start, start]
        st["vint_free"] = start
        st["pending"] = None
        st["qsplit_free"] = [start, start]
        st["d"] = d
        st["kv_in"] = kv_in
        st["kv_out"] = kv_out
        st["fbdone"] = {}
        return st

    def kv_chunks(self, st, fb):
        nc, p, cfg = self.nc, self.p, self.cfg
        EC, NT = cfg.EC, cfg.NT
        o = st["o"]
        slab_hi, slab_lo, ntok = st["ln"]
        KSEG, VSEG, FBSEG = self.kv_geom
        kin = st["kv_in"][fb]
        chunks = []
        store_toks = []

        def tile(kind, fb, i):
            wready = st["holder"]["wready"]
            bank = st["mm"][i % 2]
            sl = i % 2
            lnbh, lnbl = st["lnb_hi"][sl], st["lnb_lo"][sl]
            lnt = None
            for half, (scr, buf) in enumerate([(slab_hi, lnbh), (slab_lo, lnbl)]):
                lnt = p.dma("sync", buf[:],
                            APd(scr, i * 128, [[ntok, 128], [ntok * 128, EC], [1, 128]]),
                            waits=[st["lnb_free"][sl]] if half == 0 else (),
                            sem=st["s_lnb"][sl], tag=f"{kind}ln{fb}_{i}_{half}")
            waits = [wready, st["mm_free"][i % 2], lnt]
            mmt = None
            for ec in range(EC):
                c0 = ec * 128
                p.op("tensor", lambda e, b=bank, c0=c0, ec=ec, x=lnbh, wt=o["wT"][0]:
                     e.matmul(b[:], x[:, c0:c0 + 128],
                              wt[:, ec * 512:(ec + 1) * 512],
                              start=(ec == 0), stop=False),
                     waits=waits if ec == 0 else (), tag=f"{kind}m{fb}_{i}_{ec}a")
                p.op("tensor", lambda e, b=bank, c0=c0, ec=ec, x=lnbh, wt=o["wT"][1]:
                     e.matmul(b[:], x[:, c0:c0 + 128],
                              wt[:, ec * 512:(ec + 1) * 512],
                              start=False, stop=False), tag=f"{kind}m{fb}_{i}_{ec}b")
                mmt = p.op("tensor", lambda e, b=bank, c0=c0, ec=ec, x=lnbl, wt=o["wT"][0]:
                     e.matmul(b[:], x[:, c0:c0 + 128],
                              wt[:, ec * 512:(ec + 1) * 512],
                              start=False, stop=(ec == EC - 1)),
                     inc=(st["s_mm"], 1) if ec == EC - 1 else None,
                     tag=f"{kind}m{fb}_{i}_{ec}c")
            st["lnb_free"][sl] = mmt
            if kind == "k":
                cb, sb_ = st["cs"][i % 2], st["sn"][i % 2]
                ct = p.dma("sync", cb[:], self.d["cos_own"][i * 128:(i + 1) * 128, :],
                           waits=[st["cs_free"][0], st["cs_free"][1]],
                           sem=self.sem(f"pj_cs{i % 2}"), tag=f"kc{fb}{i}")
                st_2 = p.dma("sync", sb_[:], self.d["sin_own"][i * 128:(i + 1) * 128, :],
                             sem=self.sem(f"pj_cs{i % 2}"), tag=f"ks{fb}{i}")
                p.op("vector", lambda e, b=bank, cb=cb:
                     e.tensor_mul(st["qc"][:], b[:], cb[:]),
                     waits=[mmt, ct, st_2], tag=f"kr1{fb}_{i}")
                p.op("vector", lambda e, b=bank, sb2=sb_:
                     e.tensor_mul(st["q32"][:],
                                  APd(b, 64, [b[:].ap[0], [128, 4], [-64, 2], [1, 64]]),
                                  sb2[:]),
                     tag=f"kr2{fb}_{i}")
                ropet = p.op("vector", lambda e: e.tensor_add(st["q32"][:], st["q32"][:],
                                                              st["qc"][:]),
                             inc=(st["s_rv"], 1), tag=f"kr3{fb}_{i}")
                st["mm_free"][i % 2] = ropet
                st["cs_free"][i % 2] = ropet
                qp = i % 2
                qhib, qlob = st["qhib"][qp], st["qlob"][qp]
                p.op("vector", lambda e, qhib=qhib: e.tensor_copy(qhib[:], st["q32"][:]),
                     waits=[st["qsplit_free"][qp]], tag=f"kp1{fb}_{i}")
                spl = p.op("vector", lambda e, qhib=qhib, qlob=qlob:
                           e.tensor_sub(qlob[:], st["q32"][:], qhib[:]),
                           inc=(st["s_rv"], 1), tag=f"kp2{fb}_{i}")

                def emit_T(fb=fb, i=i, spl=spl, qhib=qhib, qlob=qlob, qp=qp):
                    last_tt2 = None
                    for half, src in enumerate([qhib, qlob]):
                        bank_t = o["trp"][half]
                        bf = o["bank_free"][half]
                        tt = None
                        for hh in range(4):
                            tt = p.op("tensor", lambda e, s2=src, b=bank_t, hh=hh:
                                      e.transpose(b[:, hh * 128:(hh + 1) * 128],
                                                  s2[:, hh * 128:(hh + 1) * 128],
                                                  self.ident[:]),
                                      waits=[spl, bf] if hh == 0 else (),
                                      inc=(st["s_tt"], 1) if hh == 3 else None,
                                      tag=f"ktt{fb}{i}{half}{hh}")
                        eb = st["ebuf"][half]
                        ev = p.op("scalar", lambda e, b=bank_t, eb=eb:
                                  e.copy(eb[:], b[:, :512]),
                                  waits=[tt, st["ebuf_free"][half]], inc=(st["s_ee"], 1),
                                  tag=f"kte{fb}{i}{half}")
                        o["bank_free"][half] = ev
                        if o.get("trp_single"):
                            o["bank_free"][1 - half] = ev
                        dmat = p.dma("gpsimd",
                                     APd(kin, half * KSEG + i * 128,
                                         [[ntok, 128], [128 * ntok, 4], [1, 128]]),
                                     APv(eb, 0, [[128, 4], [1, 128]]),
                                     waits=[ev], sem=st["s_dos"][half],
                                     tag=f"kd{fb}_{i}_{half}")
                        st["ebuf_free"][half] = dmat
                        store_toks.append(dmat)
                        last_tt2 = tt
                    st["qsplit_free"][qp] = last_tt2

                if st["pending"] is not None:
                    st["pending"]()
                st["pending"] = emit_T
            else:
                vint = st["vint"]
                vw = [mmt, st["vint_free"]]
                p.op("vector", lambda e, b=bank:
                     e.tensor_copy(APv(vint, 0, [[256, 4], [1, 128]]), b[:]),
                     waits=vw, tag=f"vp1{fb}_{i}")
                spl = p.op("vector", lambda e, b=bank:
                           e.tensor_sub(APv(vint, 128, [[256, 4], [1, 128]]), b[:],
                                        APv(vint, 0, [[256, 4], [1, 128]])),
                           inc=(st["s_rv"], 1), tag=f"vp2{fb}_{i}")
                st["mm_free"][i % 2] = spl
                t = p.dma("gpsimd",
                          APd(kin, 2 * KSEG + i * 128 * 256,
                              [[256, 128], [NT * 128 * 256, 4], [1, 256]]),
                          vint[:], waits=[spl], sem=st["s_do"], tag=f"vd{fb}_{i}")
                st["vint_free"] = t
                store_toks.append(t)

        st.setdefault("holder", {})

        def prep(which, fb=fb):
            w_dram = st["d"]["wk" if which == "k" else "wv"]
            st["holder"]["wready"] = self.wT_prep(o, w_dram, fb, "kv" + which)

        chunks.append(lambda: prep("k"))
        for i in range(NT):
            chunks.append(lambda i=i: tile("k", fb, i))

        def flush_and_prep_v():
            if st["pending"] is not None:
                st["pending"]()
                st["pending"] = None
            o["slab_free"] = st["mm_free"][0]
            prep("v")
        chunks.append(flush_and_prep_v)
        for i in range(NT):
            chunks.append(lambda i=i: tile("v", fb, i))

        def fin():
            o["slab_free"] = st["mm_free"][0]
            cct = p.op("gpsimd",
                       lambda e, kin=kin, ko=st["kv_out"][fb]:
                       e.collective_compute("AllGather", mybir.AluOpType.bypass,
                                            [[0, 1], [2, 3], [4, 5], [6, 7]],
                                            [kin[:]], [ko[:]]),
                       waits=list(store_toks), inc=(st["s_cc"], 1), tag=f"kvag{fb}")
            st["fbdone"][fb] = cct
        chunks.append(fin)
        return chunks

    def attn_merged(self, d, ln, kv_in, kv_out, q_hi, q_lo, ao_scr,
                    fc_chunks=None):
        nc, p, cfg = self.nc, self.p, self.cfg
        NT, MT, HD = cfg.NT, cfg.MT, cfg.heads
        NTOK, MTOK = cfg.NTOK, cfg.MTOK
        KSEG, VSEG, FBSEG = self.kv_geom
        with ExitStack() as sc:
            s = sc.enter_context
            kvst = self.kv_setup(sc, ln, d, kv_in, kv_out, self.q_done)
            qb = [[s(nc.sbuf_tensor(f"atq{b}{h}", [128, NTOK], BF16)) for h in range(2)]
                  for b in range(2)]
            kb = [[s(nc.sbuf_tensor(f"atk{b}{h}", [128, MTOK], BF16)) for h in range(2)]
                  for b in range(2)]
            vb = [s(nc.sbuf_tensor(f"atv{b}", [128, MT * 256], BF16))
                  for b in range(2)]
            ehi2 = [s(nc.sbuf_tensor(f"atehi{i}", [128, MTOK], BF16)) for i in range(2)]
            elo2 = [s(nc.sbuf_tensor(f"atelo{i}", [128, MTOK], BF16)) for i in range(2)]
            eThi2 = [s(nc.sbuf_tensor(f"ateThi{i}", [128, MT * 256], BF16))
                     for i in range(2)]
            eTlo2 = [s(nc.sbuf_tensor(f"ateTlo{i}", [128, MT * 256], BF16))
                     for i in range(2)]
            ao = s(nc.sbuf_tensor("atao", [128, NTOK], F32))
            mx2 = [s(nc.sbuf_tensor(f"atmx{i}", [128, 1], F32)) for i in range(2)]
            nb2 = [s(nc.sbuf_tensor(f"atnb{i}", [128, 1], F32)) for i in range(2)]
            rsp = s(nc.sbuf_tensor("atrsp", [128, 8], F32))
            rs2 = [s(nc.sbuf_tensor(f"atrs{i}", [128, 1], F32)) for i in range(2)]
            rr2 = [s(nc.sbuf_tensor(f"atrr{i}", [128, 1], F32)) for i in range(2)]
            if cfg.debug:
                self.dbg_scb = s(nc.sbuf_tensor("atdsc", [128, 4], F32))
            s32 = [s(nc.sbuf_tensor(f"atS32{i}", [128, 2048], F32)) for i in range(2)]
            psS = [s(nc.psum_tensor(f"atS{i}", [128, 512], F32)) for i in range(2)]
            psT = [s(nc.psum_tensor(f"atT{i}", [128, 1024], BF16)) for i in range(2)]
            psPV = s(nc.psum_tensor("atPV", [128, 512], F32))
            s_ld = [self.sem(f"at_ld{i}") for i in range(2)]
            s_S = self.sem("at_S")
            s_v = self.sem("at_v")
            s_s = self.sem("at_s")
            s_t = self.sem("at_t")
            s_e = self.sem("at_e")
            s_pv = self.sem("at_pv")
            s_ao = self.sem("at_ao")
            s_do = self.sem("at_do")

            start = flat(self.q_done, ln[3])
            start1 = start
            inbuf_free = [None, None]
            NTAU = MTOK // 512

            def load_head(h):
                buf = h % 2
                fbh, hl = h // 4, h % 4
                ko = kv_out[fbh]
                w = flat(self.q_done, kvst["fbdone"].get(fbh), inbuf_free[buf])
                first = True
                for half in range(2):
                    for tau in range(NTAU):
                        p.dma("sync",
                              APd(kb[buf][half], tau * 512,
                                  [[MTOK, 128], [256, 2], [1, 256]]),
                              APd(ko, half * KSEG + hl * 128 * NTOK + tau * 256,
                                  [[NTOK, 128], [FBSEG, 2], [1, 256]]),
                              waits=w if first else (),
                              sem=s_ld[buf], tag=f"ak{half}_{h}_{tau}")
                        first = False
                t = None
                for sg in range(2):
                    for tau in range(NTAU):
                        t = p.dma("sync",
                              APd(vb[buf], sg * 256 + tau * 1024,
                                  [[MT * 256, 128], [512, 2], [1, 256]]),
                              APd(ko, 2 * KSEG + hl * NT * 128 * 256 + sg * 32768
                                  + tau * 65536,
                                  [[256, 128], [FBSEG, 2], [1, 256]]),
                              waits=w if (sg == 0 and tau == 0) else (),
                              sem=s_ld[buf], tag=f"av{sg}_{h}_{tau}")
                return t

            FBn = cfg.FB
            for fn in self.kv_chunks(kvst, 0):
                fn()
            pend_q = []
            s_qld = [self.sem(f"at_qld{i}") for i in range(2)]
            qb_free = [start1, start1]
            qtok = [None, None]

            def load_q(h):
                buf = h % 2
                t = None
                for half, qsc in enumerate([q_hi, q_lo]):
                    t = p.dma("sync", qb[buf][half][:],
                              APd(qsc, h * 128 * NTOK, [[NTOK, 128], [1, NTOK]]),
                              waits=[qb_free[buf]] if half == 0 else (),
                              sem=s_qld[buf], tag=f"aq{h}_{half}")
                qtok[buf] = t

            load_q(0)
            load_q(1)
            ldt = load_head(0)
            if cfg.debug:
                kb_dbg = nc.dram_tensor("kb_dbg", [2, 128, MTOK], BF16,
                                        kind="ExternalOutput").ap()
                vb_dbg = nc.dram_tensor("vb_dbg", [128, MT * 256], BF16,
                                        kind="ExternalOutput").ap()
                ko_dbg = nc.dram_tensor("kvout0d", [2, FBSEG], BF16,
                                        kind="ExternalOutput").ap()
                ki_dbg = nc.dram_tensor("kvin0d", [FBSEG], BF16,
                                        kind="ExternalOutput").ap()
                self.dbg_s = nc.dram_tensor("dbg_s", [2, 128, MTOK], F32,
                                            kind="ExternalOutput").ap()
                self.dbg_e = nc.dram_tensor("dbg_e", [2, 128, MTOK], F32,
                                            kind="ExternalOutput").ap()
                self.dbg_ehi = nc.dram_tensor("dbg_ehi", [2, 128, MTOK], BF16,
                                              kind="ExternalOutput").ap()
                self.dbg_elo = nc.dram_tensor("dbg_elo", [2, 128, MTOK], BF16,
                                              kind="ExternalOutput").ap()
                self.dbg_sc = nc.dram_tensor("dbg_sc", [2, 128, 4], F32,
                                             kind="ExternalOutput").ap()
                s_dbg = self.sem("at_dbg")
                dw = [(s_ld[0], s_ld[0].n)]
                for hf in range(2):
                    p.dma("gpsimd", APd(kb_dbg, hf * 128 * MTOK,
                                        [[MTOK, 128], [1, MTOK]]),
                          kb[0][hf][:], waits=dw, sem=s_dbg, tag=f"dbgk{hf}")
                p.dma("gpsimd", APd(vb_dbg, 0, [[MT * 256, 128], [1, MT * 256]]),
                      vb[0][:], waits=dw, sem=s_dbg, tag="dbgv")
                p.dma("gpsimd", APd(ko_dbg, 0, [[FBSEG, 2], [1, FBSEG]]),
                      APd(kv_out[0], 0, [[FBSEG, 2], [1, FBSEG]]),
                      waits=dw, sem=s_dbg, tag="dbgko")
                p.dma("gpsimd", APd(ki_dbg, 0, [[1, 1], [1, FBSEG]]),
                      APd(kv_in[0], 0, [[1, 1], [1, FBSEG]]),
                      waits=dw, sem=s_dbg, tag="dbgki")
            pending_Ts = []
            esplit_free = [None, None]
            psS_free = [start1]
            psS_free.append(psS_free[0])
            s32_free = [start1, start1]
            sm_free = [start1, start1]
            rst_free = [start1, start1]
            eT_bank_free = [start1, start1]
            eT_slab_free = [None, None]
            ao_free = None
            gidx = 0
            tidx = 0
            for h in range(HD):
                if h % 4 == 0 and h // 4 + 1 < FBn:
                    pend_q = self.kv_chunks(kvst, h // 4 + 1)
                buf = h % 2
                kh, vh = kb[buf], vb[buf]
                GS = 2
                ngroups = (NT + GS - 1) // GS
                pv_last = None
                for qg in range(ngroups):
                    ntile_g = min(GS, NT - qg * GS)
                    slab_i = gidx % 2
                    gidx += 1
                    for qi in range(ntile_g):
                        qt = qg * GS + qi
                        J = cfg.Jtile(qt)
                        keys = (J + 1) * 128
                        nch = (keys + 511) // 512
                        sp_ = tidx % 2
                        s32b = s32[sp_]
                        tidx += 1
                        ev_last = None
                        evs = []
                        for c in range(nch):
                            k0, k1 = c * 512, min((c + 1) * 512, keys)
                            bank = psS[c % 2]
                            sS_tok = None
                            for pi, (xs, ws) in enumerate([(0, 0), (1, 0), (0, 1)]):
                                w = []
                                if pi == 0:
                                    w = [psS_free[c % 2]]
                                    if c == 0:
                                        w.append(ldt)
                                        if qt == 0:
                                            w.append(qtok[buf])
                                sS_tok = p.op("tensor",
                                     lambda e, b=bank, q2=qb[buf][xs], k2=kh[ws],
                                            q0=qt * 128, k0=k0, k1=k1,
                                            st=(pi == 0), sp2=(pi == 2):
                                     e.matmul(b[:, :k1 - k0], q2[:, q0:q0 + 128],
                                              k2[:, k0:k1], start=st, stop=sp2),
                                     waits=w,
                                     inc=(s_S, 1) if pi == 2 else None,
                                     tag=f"S{h}_{qt}_{c}_{pi}")
                                head_S_last = sS_tok
                            wjl = (J - 2) * 128
                            if k0 >= wjl or k1 > wjl:
                                ov0 = max(k0, wjl)
                                ev1 = None
                                if ov0 > k0:
                                    ev1 = p.op("scalar",
                                         lambda e, b=bank, s3=s32b, k0=k0, ov0=ov0:
                                         e.copy(s3[:, k0:ov0], b[:, :ov0 - k0]),
                                         waits=[sS_tok, s32_free[sp_]],
                                         inc=(s_s, 1), tag=f"Se{h}_{qt}_{c}")
                                ev_last = p.op("vector",
                                     lambda e, b=bank, s3=s32b, k0=k0, ov0=ov0, k1=k1,
                                            wjl=wjl:
                                     e.tensor_add(s3[:, ov0:k1],
                                                  b[:, ov0 - k0:k1 - k0],
                                                  self.masks[:, ov0 - wjl:k1 - wjl]),
                                     waits=[sS_tok, s32_free[sp_]] + ([ev1] if ev1 else []),
                                     inc=(s_v, 1), tag=f"Sm{h}_{qt}_{c}")
                                psS_free[c % 2] = flat(ev_last, ev1)
                                evs.append(psS_free[c % 2])
                            else:
                                ev_last = p.op("scalar",
                                     lambda e, b=bank, s3=s32b, k0=k0, k1=k1:
                                     e.copy(s3[:, k0:k1], b[:, :k1 - k0]),
                                     waits=[sS_tok, s32_free[sp_]],
                                     inc=(s_s, 1), tag=f"Se{h}_{qt}_{c}")
                                psS_free[c % 2] = ev_last
                                evs.append(ev_last)
                        if cfg.debug and h == 0 and qt <= 1:
                            dsm = p.dma("gpsimd",
                                        APd(self.dbg_s, qt * 128 * MTOK,
                                            [[MTOK, 128], [1, keys]]),
                                        s32b[:, :keys], waits=[ev_last, (s_s, s_s.n)],
                                        sem=self.sem(f"at_dbgS{qt}"), tag=f"dS{qt}")
                            evs.append(dsm)
                        mxb, nbb = mx2[sp_], nb2[sp_]
                        rsb, rrb = rs2[sp_], rr2[sp_]
                        mxt = p.op("vector", lambda e, s3=s32b, keys=keys, mxb=mxb:
                             e.reduce_max(mxb[:], s3[:, :keys], axis=AX),
                             waits=evs,
                             inc=(s_v, 1), tag=f"mx{h}_{qt}")
                        nbt = p.op("vector", lambda e, mxb=mxb, nbb=nbb: e.tensor_scalar(
                                nbb[:], mxb[:], -SCALE, None, ALU.mult),
                             waits=[mxt, sm_free[sp_]], inc=(s_v, 1), tag=f"nb{h}_{qt}")
                        expt = None
                        for c in range(nch):
                            k0, k1 = c * 512, min((c + 1) * 512, keys)
                            expt = p.op("scalar", lambda e, s3=s32b, k0=k0, k1=k1, c=c,
                                        nbb=nbb, r0=sp_ * 4:
                                 e.activation(s3[:, k0:k1], s3[:, k0:k1], ACTF.Exp,
                                              bias=nbb[:], scale=SCALE,
                                              accum_out=rsp[:, r0 + c:r0 + c + 1]),
                                 waits=[nbt, rst_free[sp_]] if c == 0 else (),
                                 inc=(s_s, 1) if c == nch - 1 else None,
                                 tag=f"ex{h}_{qt}_{c}")
                        sm_free[sp_] = expt
                        rst = p.op("vector", lambda e, nch=nch, rsb=rsb, r0=sp_ * 4:
                             e.reduce_sum(rsb[:], rsp[:, r0:r0 + nch], axis=AX),
                             waits=[expt], inc=(s_v, 1), tag=f"rs{h}_{qt}")
                        rst_free[sp_] = rst
                        rrt = p.op("vector", lambda e, rsb=rsb, rrb=rrb:
                             e.reciprocal(rrb[:], rsb[:]),
                             waits=[rst], inc=(s_v, 1), tag=f"rr{h}_{qt}")
                        ehi, elo = ehi2[sp_], elo2[sp_]
                        dbg_ef = (self._dbg_efree.pop(sp_, None)
                                  if cfg.debug and hasattr(self, "_dbg_efree") else None)
                        p.op("vector", lambda e, s3=s32b, ehi=ehi, keys=keys, rrb=rrb:
                             e.tensor_scalar(
                                ehi[:, :keys], s3[:, :keys], rrb[:], None, ALU.mult),
                             waits=[esplit_free[sp_], rrt, dbg_ef], tag=f"e1{h}_{qt}")
                        spl = p.op("vector",
                                   lambda e, s3=s32b, ehi=ehi, elo=elo, keys=keys,
                                          rrb=rrb:
                                   e.scalar_tensor_tensor(
                                elo[:, :keys], s3[:, :keys], rrb[:], ehi[:, :keys],
                                ALU.mult, ALU.subtract),
                             inc=(s_v, 1), tag=f"e2{h}_{qt}")
                        s32_free[sp_] = spl
                        if cfg.debug and h == 0 and qt <= 1:
                            dse = p.dma("gpsimd",
                                        APd(self.dbg_e, qt * 128 * MTOK,
                                            [[MTOK, 128], [1, keys]]),
                                        s32b[:, :keys], waits=[spl],
                                        sem=self.sem(f"at_dbgE{qt}"), tag=f"dE{qt}")
                            deh = p.dma("gpsimd",
                                        APd(self.dbg_ehi, qt * 128 * MTOK,
                                            [[MTOK, 128], [1, keys]]),
                                        ehi[:, :keys], waits=[spl],
                                        sem=self.sem(f"at_dbgH{qt}"), tag=f"dEh{qt}")
                            del_ = p.dma("gpsimd",
                                         APd(self.dbg_elo, qt * 128 * MTOK,
                                             [[MTOK, 128], [1, keys]]),
                                         elo[:, :keys], waits=[spl],
                                         sem=self.sem(f"at_dbgL{qt}"), tag=f"dEl{qt}")
                            for si2, src2 in enumerate([mxb, rsb, rrb, rrb]):
                                p.op("scalar", lambda e, s2=src2, si2=si2:
                                     e.copy(self.dbg_scb[:, si2:si2 + 1], s2[:]),
                                     waits=[spl, getattr(self, "_dbg_scb_free", None)]
                                           if si2 == 0 else [spl],
                                     inc=(s_s, 1), tag=f"dsc{qt}_{si2}")
                            dsc = p.dma("gpsimd",
                                        APd(self.dbg_sc, qt * 128 * 4,
                                            [[4, 128], [1, 4]]),
                                        self.dbg_scb[:], waits=[(s_s, s_s.n)],
                                        sem=self.sem("at_dbg"), tag=f"dM{qt}")
                            dsc2 = dsc3 = dsc
                            self._dbg_scb_free = dsc
                            s32_free[sp_] = flat(spl, dse)
                            if not hasattr(self, "_dbg_efree"):
                                self._dbg_efree = {}
                            self._dbg_efree[sp_] = [deh, del_]
                            self._dbg_sc_toks = [dsc, dsc2, dsc3]

                        def emit_T(h=h, qt=qt, qi=qi, J=J, spl=spl, ehi=ehi, elo=elo,
                                   sp_=sp_, slab_i=slab_i):
                            last_tt = None
                            for half, (src, slab) in enumerate(
                                    [(ehi, eThi2[slab_i]), (elo, eTlo2[slab_i])]):
                                bank = psT[half]
                                for g4 in range((J + 1 + 3) // 4):
                                    j0 = g4 * 4
                                    cnt = min(4, J + 1 - j0)
                                    bf = eT_bank_free[half]
                                    tt = None
                                    for jj in range(cnt):
                                        j = j0 + jj
                                        tt = p.op("tensor",
                                                  lambda e, s2=src, b=bank, j=j, jj=jj:
                                                  e.transpose(b[:, jj * 128:(jj + 1) * 128],
                                                              s2[:, j * 128:(j + 1) * 128],
                                                              self.ident[:]),
                                                  waits=[spl, eT_slab_free[slab_i], bf]
                                                        if jj == 0 else (),
                                                  inc=(s_t, 1) if jj == cnt - 1 else None,
                                                  tag=f"T{h}_{qt}_{half}_{j}")
                                    ev = p.op("scalar",
                                              lambda e, b=bank, slab=slab, j0=j0, cnt=cnt,
                                                     qi=qi:
                                              e.copy(APv(slab, j0 * 256 + qi * 128,
                                                         [[256, cnt], [1, 128]]),
                                                     b[:, :cnt * 128]),
                                              waits=[tt], inc=(s_e, 1),
                                              tag=f"Te{h}_{qt}_{half}_{g4}")
                                    eT_bank_free[half] = ev
                                    last_tt = tt
                            esplit_free[sp_] = last_tt

                        pending_Ts.append(emit_T)
                        heads_left = HD - h
                        tiles_left = heads_left * NT - (qt + 1)
                        if pend_q:
                            tl_fb = 4 * NT - (h % 4) * NT - (qt + 1)
                            take = (len(pend_q) + max(tl_fb, 1) - 1) // max(tl_fb, 1)
                            for _ in range(min(take, len(pend_q))):
                                pend_q.pop(0)()
                        if fc_chunks and tiles_left > 0:
                            take = (len(fc_chunks) + tiles_left - 1) // tiles_left
                            for _ in range(min(take, len(fc_chunks))):
                                fc_chunks.pop(0)()
                    for fT in pending_Ts:
                        fT()
                    pending_Ts = []
                    eT_done = eT_bank_free[1]
                    Jmax = cfg.Jtile(qg * GS + ntile_g - 1)
                    first = True
                    pv_tok = None
                    for j in range(Jmax + 1):
                        qstart = 0
                        while qstart < ntile_g and cfg.Jtile(qg * GS + qstart) < j:
                            qstart += 1
                        qoff = qstart * 128
                        width = ntile_g * 128 - qoff
                        if width <= 0:
                            continue
                        for pi, (vs, es) in enumerate(
                                [(0, eThi2[slab_i]), (1, eThi2[slab_i]),
                                 (0, eTlo2[slab_i])]):
                            w = [eT_done, ao_free] if first else ()
                            first = False
                            pv_tok = p.op("tensor",
                                 lambda e, vh2=vh, voff=j * 256 + vs * 128, es=es, j=j,
                                        qoff=qoff, width=width,
                                        st=(j == 0 and pi == 0),
                                        sp=(j == Jmax and pi == 2):
                                 e.matmul(psPV[:, qoff:qoff + width],
                                          APv(vh2, voff, [[1, 128]]),
                                          APv(es, j * 256 + qoff, [[1, width]]),
                                          start=st, stop=sp),
                                 waits=w,
                                 inc=(s_pv, 1) if (j == Jmax and pi == 2) else None,
                                 tag=f"P{h}_{qg}_{j}_{pi}")
                    eT_slab_free[slab_i] = pv_tok
                    aoev = p.op("vector", lambda e, gq0=qg * GS, wd=ntile_g * 128:
                                e.tensor_copy(ao[:, gq0 * 128:gq0 * 128 + wd], psPV[:, :wd]),
                                waits=[pv_tok], inc=(s_ao, 1), tag=f"ae{h}_{qg}")
                    ao_free = aoev
                    pv_last = aoev
                aodma = p.dma("gpsimd", APd(ao_scr, h * 128 * NTOK, [[NTOK, 128], [1, NTOK]]),
                              ao[:], waits=[pv_last], sem=s_do, tag=f"ad{h}")
                ao_free = aodma
                inbuf_free[buf] = pv_last
                qb_free[buf] = head_S_last
                if h + 2 < HD:
                    load_q(h + 2)
                if h + 1 < HD:
                    while pend_q and (h + 1) % 4 == 0:
                        pend_q.pop(0)()
                    ldt = load_head(h + 1)
            self.attn_done = flat((s_do, s_do.n), kvst["fbdone"].get(cfg.FB - 1))

    def ffn_weight_cache_mk(self, fc_stack, d, wg_scr, wu_scr, wd_scr, prev):
        nc, p, cfg = self.nc, self.p, self.cfg
        EC, OC, INTER = cfg.EC, cfg.OC, cfg.INTER
        s = fc_stack.enter_context
        W = max(INTER // 4, E)
        i80 = s(nc.sbuf_tensor("fci0", [128, W], I8))
        i8b = [i80, i80]
        bfb = [s(nc.sbuf_tensor(f"fcb{i}", [128, W], BF16)) for i in range(2)]
        s_ld = [self.sem(f"fc_ld{i}") for i in range(2)]
        s_cg = self.sem("fc_cg")
        s_cv = self.sem("fc_cv")
        s_st = [self.sem(f"fc_st{i}") for i in range(2)]
        half = INTER // 2
        st = {"i8_free": [prev, prev], "bf_free": [prev, prev], "idx": 0}

        def emit_gu(wi, hc, h2):
            w8, scr = [(d["wg8"], wg_scr), (d["wu8"], wu_scr)][wi]
            bi = st["idx"] % 2
            st["idx"] += 1
            W2 = INTER // 4
            q4 = W2 // 2
            ldt = p.dma("sync", i8b[bi][:, :W2],
                        w8[hc * 128:(hc + 1) * 128, h2 * W2:(h2 + 1) * W2],
                        waits=[st["i8_free"][bi]], sem=s_ld[bi], tag=f"fcl{wi}_{hc}_{h2}")
            c1 = p.op("gpsimd", lambda e, b=bi, q4=q4: e.tensor_copy(bfb[b][:, :q4],
                                                              i8b[b][:, :q4]),
                      waits=[ldt, st["bf_free"][bi]], inc=(s_cg, 1), tag=f"fcg{wi}_{hc}_{h2}")
            c2 = p.op("vector", lambda e, b=bi, q4=q4, W2=W2: e.tensor_copy(
                          bfb[b][:, q4:W2], i8b[b][:, q4:W2]),
                      waits=[ldt, st["bf_free"][bi]], inc=(s_cv, 1), tag=f"fcv{wi}_{hc}_{h2}")
            st["i8_free"][0] = [c1, c2, ldt]
            st["i8_free"][1] = [c1, c2, ldt]
            o0 = h2 * (OC // 4)
            t = p.dma("gpsimd",
                      APd(scr, (o0 * EC + hc) * 128 * 128,
                          [[128, 128], [EC * 128 * 128, OC // 4], [1, 128]]),
                      APv(bfb[bi], 0, [[128, OC // 4], [1, 128]]),
                      waits=[c1, c2], sem=s_st[bi], tag=f"fcs{wi}_{hc}_{h2}")
            st["bf_free"][bi] = t

        def emit_wd(oc):
            bi = st["idx"] % 2
            st["idx"] += 1
            ldt = p.dma("sync", i8b[bi][:, :E], d["wd8"][oc * 128:(oc + 1) * 128, :],
                        waits=[st["i8_free"][bi]], sem=s_ld[bi], tag=f"fcld{oc}")
            c1 = p.op("gpsimd", lambda e, b=bi: e.tensor_copy(bfb[b][:, :E // 2],
                                                              i8b[b][:, :E // 2]),
                      waits=[ldt, st["bf_free"][bi]], inc=(s_cg, 1), tag=f"fcgd{oc}")
            c2 = p.op("vector", lambda e, b=bi: e.tensor_copy(bfb[b][:, E // 2:E],
                                                              i8b[b][:, E // 2:E]),
                      waits=[ldt, st["bf_free"][bi]], inc=(s_cv, 1), tag=f"fcvd{oc}")
            st["i8_free"][0] = [c1, c2, ldt]
            st["i8_free"][1] = [c1, c2, ldt]
            toks = []
            for part in range(4):
                h0 = part * (EC // 4)
                t = p.dma("gpsimd",
                          APd(wd_scr, (h0 * OC + oc) * 128 * 128,
                              [[128, 128], [OC * 128 * 128, EC // 4], [1, 128]]),
                          APv(bfb[bi], h0 * 128, [[128, EC // 4], [1, 128]]),
                          waits=[c1, c2], sem=s_st[bi], tag=f"fcsd{oc}_{part}")
                toks.append(t)
            st["bf_free"][bi] = toks[-1]

        chunks = []
        for wi in range(2):
            for hc in range(EC):
                for h2 in range(4):
                    chunks.append(lambda wi=wi, hc=hc, h2=h2: emit_gu(wi, hc, h2))
        for oc in range(OC):
            chunks.append(lambda oc=oc: emit_wd(oc))
        self._fc_sems = (s_st,)
        return chunks

    def fc_finalize(self):
        s_st = self._fc_sems[0]
        self.fc_done = [(s_st[0], s_st[0].n), (s_st[1], s_st[1].n)]

    def oproj_phase(self, d, ao_scr, o2_scr, x1_scr, lnf_hi, lnf_lo):
        nc, p, cfg = self.nc, self.p, self.cfg
        NT, EC, HD = cfg.NT, cfg.EC, cfg.heads
        NTOK = cfg.NTOK
        prev = flat(self.attn_done, self.fc_done)
        with ExitStack() as sc2:
            s2 = sc2.enter_context
            aohi = s2(nc.sbuf_tensor("oah", [128, HD * NTOK], BF16))
            aolo = s2(nc.sbuf_tensor("oal", [128, HD * NTOK], BF16))
            aot = [s2(nc.sbuf_tensor(f"oat{i}", [128, NTOK], F32)) for i in range(2)]
            o = {}
            o["wrow"] = [s2(nc.sbuf_tensor(f"owr{i}", [128, E], F32)) for i in range(2)]
            o["wrhi"] = s2(nc.sbuf_tensor("owh", [128, E], BF16))
            o["wrlo"] = s2(nc.sbuf_tensor("owl", [128, E], BF16))
            wT2 = [[s2(nc.sbuf_tensor(f"owT{b}{h}", [128, EC * 512], BF16))
                    for h in range(2)] for b in range(2)]
            o["wT"] = wT2[0]
            o["trp"] = [s2(nc.psum_tensor(f"otp{i}", [128, 1024], BF16)) for i in range(2)]
            mm = [s2(nc.psum_tensor(f"omm{i}", [128, 512], F32)) for i in range(2)]
            evb = [s2(nc.sbuf_tensor(f"oev{i}", [128, 512], F32)) for i in range(2)]
            o["s_wld"] = [self.sem(f"pj_wld{i}") for i in range(2)]
            o["s_wv"] = self.sem("pj_wv")
            o["s_wt"] = self.sem("pj_wt")
            o["s_we"] = self.sem("pj_we")
            s_ld = [self.sem(f"o_ld{i}") for i in range(2)]
            s_v = self.sem("o_v")
            s_mm = self.sem("pj_mm")
            s_ev = self.sem("o_ev")
            s_evd = [self.sem(f"o_evd{i}") for i in range(2)]
            o["wrow_free"] = [prev, prev]
            o["bank_free"] = [prev, prev]
            o["vec_start"] = prev
            o["slab_free"] = None
            aot_free = [prev, prev]
            spl = None
            for hh in range(HD):
                buf = aot[hh % 2]
                ldt = p.dma("sync", buf[:], APd(ao_scr, hh * 128 * NTOK,
                                                [[NTOK, 128], [1, NTOK]]),
                            waits=[aot_free[hh % 2], prev], sem=s_ld[hh % 2], tag=f"oal{hh}")
                p.op("vector", lambda e, b=buf, hh=hh:
                     e.tensor_copy(aohi[:, hh * NTOK:(hh + 1) * NTOK], b[:]),
                     waits=[ldt, prev], tag=f"os1{hh}")
                spl = p.op("vector", lambda e, b=buf, hh=hh:
                           e.tensor_sub(aolo[:, hh * NTOK:(hh + 1) * NTOK], b[:],
                                        aohi[:, hh * NTOK:(hh + 1) * NTOK]),
                           inc=(s_v, 1), tag=f"os2{hh}")
                aot_free[hh % 2] = spl
            split_done = spl
            mm_free = [prev, prev]
            evb_free = [prev, prev]
            slab_mm_last = [None, None]
            for fb in range(4):
                o["wT"] = wT2[fb % 2]
                o["slab_free"] = slab_mm_last[fb % 2]
                wready = self.wT_prep(o, d["wo"], fb, "o")
                last_mm_of_fb = None
                for i in range(NT):
                    bank = mm[i % 2]
                    waits = [wready, split_done, mm_free[i % 2]]
                    last = None
                    for ec in range(HD):
                        col = ec * NTOK + i * 128
                        p.op("tensor", lambda e, b=bank, col=col, ec=ec, wt=o["wT"][0]:
                             e.matmul(b[:], aohi[:, col:col + 128],
                                      wt[:, ec * 512:(ec + 1) * 512],
                                      start=(ec == 0), stop=False),
                             waits=waits if ec == 0 else (), tag=f"om{fb}_{i}_{ec}a")
                        p.op("tensor", lambda e, b=bank, col=col, ec=ec, wt=o["wT"][1]:
                             e.matmul(b[:], aohi[:, col:col + 128],
                                      wt[:, ec * 512:(ec + 1) * 512],
                                      start=False, stop=False), tag=f"om{fb}_{i}_{ec}b")
                        last = p.op("tensor",
                             lambda e, b=bank, col=col, ec=ec, wt=o["wT"][0]:
                             e.matmul(b[:], aolo[:, col:col + 128],
                                      wt[:, ec * 512:(ec + 1) * 512],
                                      start=False, stop=(ec == HD - 1)),
                             inc=(s_mm, 1) if ec == HD - 1 else None,
                             tag=f"om{fb}_{i}_{ec}c")
                    last_mm_of_fb = last
                    eb = evb[i % 2]
                    ev = p.op("scalar", lambda e, b=bank, eb=eb: e.copy(eb[:], b[:]),
                              waits=[last, evb_free[i % 2]], inc=(s_ev, 1),
                              tag=f"oe{fb}_{i}")
                    mm_free[i % 2] = ev
                    dmat = p.dma("gpsimd",
                                 APd(o2_scr, i * 128 * E + fb * 512, [[E, 128], [1, 512]]),
                                 eb[:], waits=[ev], sem=s_evd[i % 2], tag=f"oed{fb}_{i}")
                    evb_free[i % 2] = dmat
                slab_mm_last[fb % 2] = last_mm_of_fb
            mm_all = [(s_evd[0], s_evd[0].n), (s_evd[1], s_evd[1].n)]

        with ExitStack() as sc3:
            s3 = sc3.enter_context
            o2t = [s3(nc.sbuf_tensor(f"oo2{i}", [128, E], F32)) for i in range(2)]
            xt = [s3(nc.sbuf_tensor(f"oxt{i}", [128, E], F32)) for i in range(2)]
            bns = s3(nc.sbuf_tensor("obns", [128, 24], F32))
            bno = s3(nc.sbuf_tensor("obno", [128, 2], F32))
            std = s3(nc.sbuf_tensor("ostd", [128, 1], F32))
            rstd = s3(nc.sbuf_tensor("orstd", [128, 1], F32))
            x1t = s3(nc.sbuf_tensor("ox1", [128, E], F32))
            ln3 = s3(nc.sbuf_tensor("oln3", [128, E], F32))
            lhi = s3(nc.sbuf_tensor("olh", [128, E], BF16))
            llo = s3(nc.sbuf_tensor("oll", [128, E], BF16))
            trp2 = [s3(nc.psum_tensor(f"otr2{i}", [128, 1024], BF16)) for i in range(2)]
            lfst = s3(nc.sbuf_tensor("olf", [128, 2 * E], BF16))
            s_x = [self.sem(f"o_x{i}") for i in range(2)]
            s_o2 = [self.sem(f"o_o2{i}") for i in range(2)]
            s_lf = self.sem("o_lf")
            s_v2 = self.sem("o_v2")
            s_s2 = self.sem("o_s2")
            s_t2 = self.sem("o_t2")
            s_e2 = self.sem("o_e2")
            s_d2 = self.sem("o_d2")
            mm_all0 = mm_all[0]
            o2_free = [mm_all0, mm_all0]
            xt_free = [mm_all0, mm_all0]
            bank_free = [mm_all0, mm_all0]
            lf_free = None
            x1_prev = None
            for i in range(NT):
                o2b = o2t[i % 2]
                o2ldt = p.dma("sync", o2b[:], o2_scr[i * 128:(i + 1) * 128, :],
                              waits=flat(o2_free[i % 2], mm_all), sem=s_o2[i % 2],
                              tag=f"o2l{i}")
                xb = xt[i % 2]
                xldt = p.dma("sync", xb[:], d["x_own"][i * 128:(i + 1) * 128, :],
                             waits=[xt_free[i % 2]], sem=s_x[i % 2], tag=f"oxl{i}")
                stats = self.ln_stats(bns, bno,
                                      lambda c, b=o2b: b[:, c * 512:(c + 1) * 512],
                                      E, flat(o2ldt, mm_all), s_v2, f"ob{i}")
                sq = p.op("scalar", lambda e: e.activation(std[:], bno[:, 1:2], ACTF.Sqrt,
                                                           bias=self.epsb[:]),
                          waits=flat(stats, mm_all), inc=(s_s2, 1), tag=f"osq{i}")
                rct = p.op("vector", lambda e: e.reciprocal(rstd[:], std[:]), waits=[sq],
                           inc=(s_v2, 1), tag=f"orc{i}")
                p.op("vector", lambda e, b=o2b: e.tensor_scalar(
                        x1t[:], b[:], bno[:, 0:1], rstd[:], ALU.subtract, ALU.mult),
                     waits=[x1_prev, rct], tag=f"oln2{i}")
                rest = p.op("vector", lambda e, b=xb: e.tensor_add(x1t[:], x1t[:], b[:]),
                            waits=[xldt], inc=(s_v2, 1), tag=f"ores{i}")
                o2_free[i % 2] = rest
                xt_free[i % 2] = rest
                x1d = p.dma("gpsimd", x1_scr[i * 128:(i + 1) * 128, :], x1t[:],
                            waits=[rest], sem=s_d2, tag=f"ox1d{i}")
                stats3 = self.ln_stats(bns, bno,
                                       lambda c: x1t[:, c * 512:(c + 1) * 512],
                                       E, [], s_v2, f"ob3{i}")
                sq3 = p.op("scalar", lambda e: e.activation(std[:], bno[:, 1:2], ACTF.Sqrt,
                                                            bias=self.epsb[:]),
                           waits=[stats3], inc=(s_s2, 1), tag=f"osq3{i}")
                rct3 = p.op("vector", lambda e: e.reciprocal(rstd[:], std[:]), waits=[sq3],
                            inc=(s_v2, 1), tag=f"orc3{i}")
                x1_prev = p.op("vector", lambda e: e.tensor_scalar(
                        ln3[:], x1t[:], bno[:, 0:1], rstd[:], ALU.subtract, ALU.mult),
                     waits=[x1d, rct3], inc=(s_v2, 1), tag=f"oln3{i}")
                p.op("vector", lambda e: e.tensor_copy(lhi[:], ln3[:]),
                     waits=[lf_free], tag=f"ol1{i}")
                spl = p.op("vector", lambda e: e.tensor_sub(llo[:], ln3[:], lhi[:]),
                           inc=(s_v2, 1), tag=f"ol2{i}")
                gidx = 0
                for half, (src, coff) in enumerate([(lhi, 0), (llo, E)]):
                    for g in range(EC // 4):
                        bi = gidx % 2
                        bank = trp2[bi]
                        bf = bank_free[bi]
                        gidx += 1
                        tt = None
                        for j in range(4):
                            ec = g * 4 + j
                            tt = p.op("tensor", lambda e, s4=src, b=bank, ec=ec, j=j:
                                      e.transpose(b[:, j * 128:(j + 1) * 128],
                                                  s4[:, ec * 128:(ec + 1) * 128],
                                                  self.ident[:]),
                                      waits=[spl, bf] if j == 0 else (),
                                      inc=(s_t2, 1) if j == 3 else None,
                                      tag=f"ot{i}{half}{g}{j}")
                        base = coff + g * 4 * 128
                        ev = p.op("scalar", lambda e, b=bank, base=base:
                                  e.copy(lfst[:, base:base + 512], b[:, :512]),
                                  waits=[tt], inc=(s_e2, 1), tag=f"ote{i}{half}{g}")
                        bank_free[bi] = ev
                lf_toks = []
                last_ev_tok = bank_free[(gidx - 1) % 2]
                for half, (coff, dst) in enumerate([(0, lnf_hi), (E, lnf_lo)]):
                    t = p.dma("gpsimd",
                              APd(dst, i * 128, [[NTOK, 128], [NTOK * 128, EC], [1, 128]]),
                              APv(lfst, coff, [[128, EC], [1, 128]]),
                              waits=[last_ev_tok], sem=s_lf, tag=f"olf{i}_{half}")
                    lf_toks.append(t)
                lf_free = lf_toks[-1]
            self.oproj_done = [(s_d2, s_d2.n), (s_lf, s_lf.n)]

    def ffn_phase(self, d, lnf_hi, lnf_lo, wg_scr, wu_scr, wd_scr, x1_scr):
        nc, p, cfg = self.nc, self.p, self.cfg
        EC, OC, NT = cfg.EC, cfg.OC, cfg.NT
        NTOK = cfg.NTOK
        TC = cfg.TC
        chunks = []
        t0 = 0
        while t0 < NTOK:
            tc = min(TC, NTOK - t0)
            chunks.append((t0, tc))
            t0 += tc
        prev = flat(self.oproj_done, self.fc_done)
        with ExitStack() as sc:
            s = sc.enter_context
            fuse_hi = s(nc.sbuf_tensor("ffh", [128, OC * TC], BF16))
            fuse_lo = s(nc.sbuf_tensor("ffl", [128, OC * TC], BF16))
            lnc = [s(nc.sbuf_tensor(f"fln{h}", [128, EC * TC], BF16)) for h in range(2)]
            wgt = [s(nc.sbuf_tensor(f"fwg{i}", [128, EC * 128], BF16)) for i in range(2)]
            wut = [s(nc.sbuf_tensor(f"fwu{i}", [128, EC * 128], BF16)) for i in range(2)]
            wdt = [s(nc.sbuf_tensor(f"fwd{i}", [128, OC * 128], BF16)) for i in range(2)]
            gsil = s(nc.sbuf_tensor("fgs", [128, TC], F32))
            ups = s(nc.sbuf_tensor("fup", [128, TC], F32))
            fuse32 = s(nc.sbuf_tensor("ff32", [128, TC], F32))
            dn = s(nc.sbuf_tensor("fdn", [128, 512], F32))
            dn_all = s(nc.sbuf_tensor("fdna", [128, (TC // 128) * E], F32))
            x1t = [s(nc.sbuf_tensor(f"fx1{i}", [128, E], F32)) for i in range(2)]
            outt = s(nc.sbuf_tensor("fout", [128, E], F32))
            ln4 = s(nc.sbuf_tensor("fln4", [128, E], F32))
            bns = s(nc.sbuf_tensor("fbns", [128, 24], F32))
            bno = s(nc.sbuf_tensor("fbno", [128, 2], F32))
            std = s(nc.sbuf_tensor("fstd", [128, 1], F32))
            rstd = s(nc.sbuf_tensor("frstd", [128, 1], F32))
            psG = s(nc.psum_tensor("fG", [128, 512], F32))
            psU = s(nc.psum_tensor("fU", [128, 512], F32))
            psD = s(nc.psum_tensor("fD", [128, 512], F32))
            psTr = s(nc.psum_tensor("fTr", [128, 512], F32))
            s_lnld = self.sem("f_lnld")
            s_wld = [self.sem(f"f_wld{i}") for i in range(2)]
            s_wdld = [self.sem(f"f_wdld{i}") for i in range(2)]
            s_g = self.sem("f_g")
            s_sv = self.sem("f_sv")
            s_fv = self.sem("f_fv")
            s_dmm = self.sem("f_dmm")
            s_dev = self.sem("f_dev")
            s_dtr = self.sem("f_dtr")
            s_dte = self.sem("f_dte")
            s_v = self.sem("f_v")
            s_s = self.sem("f_s")

            fcd = self.fc_done
            lnc_free = None
            wg_free = [fcd[0], fcd[1]]
            wd_free = [fcd[0], fcd[1]]
            psG_free = prev[0]
            psU_free = prev[0]
            psD_free = prev[0]
            psTr_free = prev[0]
            fuse_free = prev[0]
            dn_free = prev[0]
            dna_free = prev[0]
            x1_free = [prev[0], prev[0]]
            outt_free = prev[0]
            gu_free = prev[0]
            s_x1 = [self.sem(f"f_x1s{i}") for i in range(2)]
            for ci, (tt0, tcw) in enumerate(chunks):
                ntile_c = tcw // 128
                lnt = None
                for half, src in enumerate([lnf_hi, lnf_lo]):
                    lnt = p.dma("sync", lnc[half][:, :EC * tcw],
                                APd(src, tt0, [[NTOK, 128], [NTOK * 128, EC], [1, tcw]]),
                                waits=(list(prev) if ci == 0 else [lnc_free]), sem=s_lnld,
                                tag=f"fl{ci}_{half}")
                last_g = None
                for oi in range(OC):
                    wb = oi % 2
                    wldt = p.dma("sync", wgt[wb][:],
                                 APd(wg_scr, oi * EC * 128 * 128,
                                     [[128, 128], [128 * 128, EC], [1, 128]]),
                                 waits=[wg_free[wb]] + (prev if ci == 0 and oi < 2 else []),
                                 sem=s_wld[wb], tag=f"fg{ci}_{oi}")
                    wldt2 = p.dma("sync", wut[wb][:],
                                  APd(wu_scr, oi * EC * 128 * 128,
                                      [[128, 128], [128 * 128, EC], [1, 128]]),
                                  sem=s_wld[wb], tag=f"fu{ci}_{oi}")
                    toks = {}
                    for (ps, wt, nm, psfree) in [(psG, wgt[wb], "g", psG_free),
                                                 (psU, wut[wb], "u", psU_free)]:
                        lastmm = None
                        for hc in range(EC):
                            w = [wldt, wldt2, lnt, psfree] if hc == 0 else ()
                            p.op("tensor", lambda e, ps=ps, wt=wt, hc=hc, tcw=tcw:
                                 e.matmul(ps[:, :tcw], wt[:, hc * 128:(hc + 1) * 128],
                                          lnc[0][:, hc * tcw:(hc + 1) * tcw],
                                          start=(hc == 0), stop=False),
                                 waits=w, tag=f"f{nm}{ci}_{oi}_{hc}a")
                            lastmm = p.op("tensor", lambda e, ps=ps, wt=wt, hc=hc, tcw=tcw:
                                 e.matmul(ps[:, :tcw], wt[:, hc * 128:(hc + 1) * 128],
                                          lnc[1][:, hc * tcw:(hc + 1) * tcw],
                                          start=False, stop=(hc == EC - 1)),
                                 inc=(s_g, 1) if hc == EC - 1 else None,
                                 tag=f"f{nm}{ci}_{oi}_{hc}b")
                        toks[nm] = lastmm
                    wg_free[wb] = toks["u"]
                    last_g = toks["u"]
                    sil = p.op("scalar", lambda e, oi=oi, tcw=tcw:
                               e.activation(gsil[:, :tcw], psG[:, :tcw], ACTF.Silu,
                                            scale=self.sg_sb[:, oi:oi + 1]),
                               waits=[toks["g"], gu_free], inc=(s_sv, 1),
                               tag=f"fsl{ci}_{oi}")
                    psG_free = sil
                    up = p.op("scalar", lambda e, oi=oi, tcw=tcw:
                              e.activation(ups[:, :tcw], psU[:, :tcw], ACTF.Copy,
                                           scale=self.su_sb[:, oi:oi + 1]),
                              waits=[toks["u"]], inc=(s_sv, 1), tag=f"fue{ci}_{oi}")
                    psU_free = up
                    fw = [sil, up]
                    if oi == 0:
                        fw.append(fuse_free)
                    gu_free = p.op("vector", lambda e, tcw=tcw: e.tensor_mul(
                            fuse32[:, :tcw], gsil[:, :tcw], ups[:, :tcw]),
                         waits=fw, inc=(s_fv, 1), tag=f"fm{ci}_{oi}")
                    p.op("vector", lambda e, oi=oi, tcw=tcw:
                         e.tensor_copy(fuse_hi[:, oi * TC:oi * TC + tcw], fuse32[:, :tcw]),
                         tag=f"fh{ci}_{oi}")
                    p.op("vector", lambda e, oi=oi, tcw=tcw:
                         e.tensor_sub(fuse_lo[:, oi * TC:oi * TC + tcw], fuse32[:, :tcw],
                                      fuse_hi[:, oi * TC:oi * TC + tcw]),
                         inc=(s_fv, 1), tag=f"fl2{ci}_{oi}")
                fv = (s_fv, s_fv.n)
                lnc_free = last_g
                dmm_last = None
                for hc in range(EC):
                    wb = hc % 2
                    wdldt = p.dma("sync", wdt[wb][:],
                                  APd(wd_scr, hc * OC * 128 * 128,
                                      [[128, 128], [128 * 128, OC], [1, 128]]),
                                  waits=[wd_free[wb]] + (prev if ci == 0 and hc < 2 else []),
                                  sem=s_wdld[wb], tag=f"fw{ci}_{hc}")
                    lastd = None
                    for oi in range(OC):
                        w = [fv, wdldt, psD_free] if oi == 0 else ()
                        p.op("tensor", lambda e, oi=oi, tcw=tcw, wb=wb:
                             e.matmul(psD[:, :tcw], wdt[wb][:, oi * 128:(oi + 1) * 128],
                                      fuse_hi[:, oi * TC:oi * TC + tcw],
                                      start=(oi == 0), stop=False),
                             waits=w, tag=f"fd{ci}_{hc}_{oi}a")
                        lastd = p.op("tensor", lambda e, oi=oi, tcw=tcw, wb=wb:
                             e.matmul(psD[:, :tcw], wdt[wb][:, oi * 128:(oi + 1) * 128],
                                      fuse_lo[:, oi * TC:oi * TC + tcw],
                                      start=False, stop=(oi == OC - 1)),
                             inc=(s_dmm, 1) if oi == OC - 1 else None,
                             tag=f"fd{ci}_{hc}_{oi}b")
                    wd_free[wb] = lastd
                    dmm_last = lastd
                    dev = p.op("scalar", lambda e, hc=hc, tcw=tcw:
                               e.activation(dn[:, :tcw], psD[:, :tcw], ACTF.Copy,
                                            scale=self.sd_sb[:, hc:hc + 1]),
                               waits=[lastd, dn_free, dna_free if hc == 0 else None],
                               inc=(s_dev, 1), tag=f"fde{ci}_{hc}")
                    psD_free = dev
                    tt = None
                    for j in range(ntile_c):
                        tt = p.op("tensor", lambda e, j=j:
                                  e.transpose(psTr[:, j * 128:(j + 1) * 128],
                                              dn[:, j * 128:(j + 1) * 128], self.identf[:]),
                                  waits=[dev, psTr_free] if j == 0 else (),
                                  inc=(s_dtr, 1) if j == ntile_c - 1 else None,
                                  tag=f"fdt{ci}_{hc}_{j}")
                    dn_free = tt
                    ev = p.op("vector", lambda e, hc=hc, ntile_c=ntile_c:
                              e.tensor_copy(APv(dn_all, hc * 128, [[E, ntile_c], [1, 128]]),
                                            psTr[:, :ntile_c * 128]),
                              waits=[tt], inc=(s_dte, 1), tag=f"fte{ci}_{hc}")
                    psTr_free = ev
                dna_ready = psTr_free
                fuse_free = dmm_last
                ep_last = None
                for j in range(ntile_c):
                    gt = (tt0 // 128) + j
                    xb = x1t[j % 2]
                    xldt = p.dma("sync", xb[:], x1_scr[gt * 128:(gt + 1) * 128, :],
                                 waits=[x1_free[j % 2]], sem=s_x1[j % 2], tag=f"fx{ci}_{j}")
                    stats = self.ln_stats(bns, bno,
                                          lambda c, j=j: dn_all[:, j * E + c * 512:
                                                                j * E + (c + 1) * 512],
                                          E, [dna_ready], s_v, f"fb{ci}_{j}")
                    sq = p.op("scalar", lambda e: e.activation(std[:], bno[:, 1:2], ACTF.Sqrt,
                                                               bias=self.epsb[:]),
                              waits=[stats], inc=(s_s, 1), tag=f"fsq{ci}_{j}")
                    rct = p.op("vector", lambda e: e.reciprocal(rstd[:], std[:]), waits=[sq],
                               inc=(s_v, 1), tag=f"frc{ci}_{j}")
                    p.op("vector", lambda e, j=j: e.tensor_scalar(
                            ln4[:], dn_all[:, j * E:(j + 1) * E], bno[:, 0:1], rstd[:],
                            ALU.subtract, ALU.mult), waits=[rct], tag=f"fln4{ci}_{j}")
                    add = p.op("vector", lambda e, b=xb: e.tensor_add(outt[:], ln4[:], b[:]),
                               waits=[xldt, outt_free], inc=(s_v, 1), tag=f"fadd{ci}_{j}")
                    x1_free[j % 2] = add
                    od = p.dma("gpsimd", d["out"][gt * 128:(gt + 1) * 128, :], outt[:],
                               waits=[add], sem=self.out_sem, tag=f"fo{ci}_{j}")
                    outt_free = od
                    ep_last = od
                dna_free = ep_last


def own_row_index(cfg, par):
    rows = []
    for tau in range(max(1, cfg.NT // 2)):
        base = 512 * tau + 256 * par
        rows.extend(range(base, base + 256))
    return np.array(rows[:cfg.NTOK])


def rope_tables(T, Dh=128, base=1e6):
    inv = 1.0 / base ** (np.arange(0, Dh, 2, dtype=np.float32) / Dh)
    f = np.arange(T, dtype=np.float32)[:, None] * inv[None, :]
    cos = np.concatenate([np.cos(f), np.cos(f)], -1).astype(np.float32)
    sin = np.concatenate([np.sin(f), np.sin(f)], -1).astype(np.float32)
    return cos, sin


def stage_inputs(cfg, inputs, core):
    import ml_dtypes
    b, par = core // 2, core % 2
    x = np.ascontiguousarray(np.asarray(inputs["L0_attn_x"][b], np.float32))
    own_rows = own_row_index(cfg, par)
    m = {}
    m["x_seq"] = np.ascontiguousarray(x[:cfg.MTOK])
    m["x_own"] = np.ascontiguousarray(x[own_rows])
    m["wq"] = np.ascontiguousarray(np.asarray(inputs["L0_attn_wq"], np.float32))
    m["wk"] = np.ascontiguousarray(np.asarray(inputs["L0_attn_wk"], np.float32))
    m["wv"] = np.ascontiguousarray(np.asarray(inputs["L0_attn_wv"], np.float32))
    m["wo"] = np.ascontiguousarray(np.asarray(inputs["L0_attn_wo"], np.float32))
    m["wg8"] = np.ascontiguousarray(np.asarray(inputs["L0_ffn_w_gate"], np.int8)[:, :cfg.INTER])
    m["wu8"] = np.ascontiguousarray(np.asarray(inputs["L0_ffn_w_up"], np.int8)[:, :cfg.INTER])
    m["wd8"] = np.ascontiguousarray(np.asarray(inputs["L0_ffn_w_down"], np.int8)[:cfg.INTER])
    m["sg"] = np.ascontiguousarray(
        np.asarray(inputs["L0_ffn_s_gate"], np.float32)[:cfg.INTER].reshape(cfg.OC, 128).T)
    m["su"] = np.ascontiguousarray(
        np.asarray(inputs["L0_ffn_s_up"], np.float32)[:cfg.INTER].reshape(cfg.OC, 128).T)
    m["sd"] = np.ascontiguousarray(
        np.asarray(inputs["L0_ffn_s_down"], np.float32).reshape(cfg.EC, 128).T)
    m["ident"] = np.eye(128, dtype=np.float32).astype(ml_dtypes.bfloat16)
    m["identf"] = np.eye(128, dtype=np.float32)
    m["epsb"] = np.full((128, 1), 1e-5, np.float32)
    masks = np.zeros((3, 128, 128), np.float32)
    triln = np.triu(np.full((128, 128), -1e9, np.float32), 1)
    if par == 0:
        masks[0] = triln
        masks[1] = -1e9
        masks[2] = -1e9
    else:
        masks[2] = triln
    m["masks"] = masks.astype(ml_dtypes.bfloat16)
    cos, sin = rope_tables(2048)
    sin_signed = sin.copy()
    sin_signed[:, :64] = -sin[:, :64]
    cos_rep = np.tile(cos, (1, 4))
    sin_rep = np.tile(sin_signed, (1, 4))
    m["cos_all"] = np.ascontiguousarray(cos_rep[:cfg.MTOK])
    m["sin_all"] = np.ascontiguousarray(sin_rep[:cfg.MTOK])
    m["cos_own"] = np.ascontiguousarray(cos_rep[own_rows])
    m["sin_own"] = np.ascontiguousarray(sin_rep[own_rows])
    return m


def kernel(**inputs):
    from concourse.bass_utils import run_bass_kernel_spmd
    cfg = Cfg()
    in_maps = [stage_inputs(cfg, inputs, c) for c in range(8)]
    b = Builder(cfg)
    nc = b.build()
    res = run_bass_kernel_spmd(nc, in_maps, core_ids=list(range(8)))
    x = np.asarray(inputs["L0_attn_x"], np.float32)
    out = np.empty_like(x)
    for c in range(8):
        bq, par = c // 2, c % 2
        out[bq][own_row_index(cfg, par)] = np.asarray(res.results[c]["out"],
                                                     np.float32)
    return out
